# version 04
# baseline (speedup 1.0000x reference)
import sys
sys.path.insert(0, "/opt/trn_rl_repo")
import numpy as np
import concourse.bass as bass
import concourse.mybir as mybir
from concourse.bass_utils import run_bass_kernel_spmd

F32 = mybir.dt.float32
BF16 = mybir.dt.bfloat16
AF = mybir.ActivationFunctionType
ALU = mybir.AluOpType
AP = bass.AP

S = 2048
D = 1024
H = 16
DH = 64
DI = 4096
NL = 4
NB = 2
T = NB * S
G = 512
NG = S // G
NI = D // 128
NIH = DI // 128
KB = 1024
DBG_B = 0
DBG_G = 0
DBG_L = 0


class Sem:
    def __init__(self, nc, name):
        self.h = nc.alloc_semaphore(name=name)
        self.v = 0

    def reserve(self, k):
        self.v += k
        return self.v


class Buf:
    __slots__ = ("rdy", "cons", "rng")

    def __init__(self, rng=None):
        self.rdy = None
        self.cons = []
        self.rng = rng


class MK:
    ENGS = ("tensor", "scalar", "vector", "gpsimd", "sync")

    def __init__(self, nc):
        self.nc = nc
        self.q = {e: [] for e in self.ENGS}
        self.ST = Sem(nc, "sT")
        self.SS = Sem(nc, "sS")
        self.SV = Sem(nc, "sV")
        self.SW = Sem(nc, "sW")
        self.SH = Sem(nc, "sH")
        self.SO = Sem(nc, "sO")
        self.esem = {"tensor": self.ST, "scalar": self.SS, "vector": self.SV}
        self.overlaps = []
        self._waited = {e: {} for e in self.ENGS}

    def ev(self, eng, fn):
        self.q[eng].append(fn)

    def wait(self, eng, sem, v):
        assert v <= sem.v, f"wait beyond reserved: {v} > {sem.v}"
        if v <= 0:
            return
        w = self._waited[eng]
        if w.get(id(sem), -1) >= v:
            return
        w[id(sem)] = v
        self.ev(eng, lambda e, s=sem.h, vv=v: e.wait_ge(s, vv))

    def use(self, eng, *bufs):
        for b in bufs:
            if b.rdy is not None:
                self.wait(eng, *b.rdy)

    def take(self, eng, *bufs):
        for b in bufs:
            for sv in b.cons:
                self.wait(eng, *sv)
            b.cons = []
            if b.rng is not None:
                s0, e0 = b.rng
                for o in self.overlaps:
                    if o is b or o.rng is None:
                        continue
                    s1, e1 = o.rng
                    if s1 < e0 and s0 < e1:
                        if o.rdy is not None:
                            self.wait(eng, *o.rdy)
                            o.rdy = None
                        for sv in o.cons:
                            self.wait(eng, *sv)
                        o.cons = []

    def done_c(self, buf, sem, v):
        buf.cons.append((sem, v))

    def done_p(self, buf, sem, v):
        buf.rdy = (sem, v)
        buf.cons = []

    def sig(self, eng, k=1):
        sem = self.esem[eng]
        v = sem.reserve(k)
        self.ev(eng, lambda e, s=sem.h, kk=k: e.drain().then_inc(s, kk))
        return sem, v

    def mminc(self, k=1):
        return self.ST, self.ST.reserve(k)

    def dma(self, eng, sem, out_ap, in_ap):
        v = sem.reserve(16)
        self.ev(eng, lambda e, o=out_ap, i=in_ap, s=sem.h: e.dma_start(out=o, in_=i).then_inc(s, 16))
        return sem, v

    def flush(self, block):
        for e in self.ENGS:
            fns = self.q[e]
            if fns:
                getattr(block, e)(lambda eng, fns=fns: [f(eng) for f in fns])


def host_constants():
    w = 1.0 / 10000.0 ** (np.arange(0, DH, 2, dtype=np.float64) / DH)
    s = np.arange(S, dtype=np.float64)
    ang = s[None, :] * w[:, None]
    sin, cos = np.sin(ang), np.cos(ang)
    blk_sin = np.concatenate([sin, sin], 0)
    blk_cm = np.concatenate([cos, -cos], 0)
    blk_kt = np.concatenate([cos, sin], 0)
    SIN = np.concatenate([blk_sin, blk_sin], 0).astype(np.float32)
    CM = np.concatenate([blk_cm, blk_cm], 0).astype(np.float32)
    KT = np.concatenate([blk_kt, blk_kt], 0).astype(np.float32)
    TRI = (np.arange(128)[:, None] <= np.arange(128)[None, :]).astype(np.float32)
    EYE = np.eye(128, dtype=np.float32)
    ONEF = np.ones((128, 128), dtype=np.float32)
    return SIN, CM, KT, TRI, EYE, ONEF


def build_program(debug_taps=False, nl=NL, do_slot=True):
    nc = bass.Bass("TRN2", target_bir_lowering=False, debug=False, dynamic_dma_scratch_size=16384)

    xin = nc.dram_tensor("xin", [NB, S, D], F32, kind="ExternalInput")
    ins = {}
    for l in range(NL):
        for nm, shp in (("wq", [D, D]), ("wk", [D, D]), ("wv", [D, D]), ("wo", [D, D]),
                        ("w1", [D, DI]), ("w2", [DI, D]),
                        ("bqc", [128, NI]), ("bkc", [128, NI]), ("boc", [128, NI]),
                        ("b2c", [128, NI]), ("b1c", [128, NIH]), ("uc", [128, NI]),
                        ("l1wc", [128, NI]), ("l1bc", [128, NI]),
                        ("l2wc", [128, NI]), ("l2bc", [128, NI]),
                        ("bvr", [1, D])):
            ins[f"{nm}{l}"] = nc.dram_tensor(f"{nm}{l}", shp, F32, kind="ExternalInput")
    for nm, shp in (("w1s", [D, DI]), ("w2s", [D, DI]), ("h1s", [DI, DI]),
                    ("h2s", [DI, DI]), ("w3s", [DI, D]),
                    ("s1c", [128, NIH]), ("s2c", [128, NIH]), ("sh1c", [128, NIH]),
                    ("sh2c", [128, NIH]), ("s3c", [128, NI])):
        ins[nm] = nc.dram_tensor(nm, shp, F32, kind="ExternalInput")
    out = nc.dram_tensor("out", [NB, S, D], F32, kind="ExternalOutput")

    kind = "ExternalOutput" if debug_taps else "Internal"
    if debug_taps:
        dbgA = nc.dram_tensor("dbgA", [D, T], F32, kind="ExternalOutput")
        dbgKP = nc.dram_tensor("dbgKP", [128, NI * S], F32, kind="ExternalOutput")
        dbgVA = nc.dram_tensor("dbgVA", [128, 16 * H * 65], F32, kind="ExternalOutput")
        dbgQP = nc.dram_tensor("dbgQP", [128, NI * G], F32, kind="ExternalOutput")
        dbgQT = nc.dram_tensor("dbgQT", [128, NI * G], F32, kind="ExternalOutput")
        dbgCT = nc.dram_tensor("dbgCT", [128, NI * G], F32, kind="ExternalOutput")
        dbgXN = nc.dram_tensor("dbgXN", [128, NI * G], F32, kind="ExternalOutput")
        dbgQS1 = nc.dram_tensor("dbgQS1", [128, G], F32, kind="ExternalOutput")
        dbgQS2 = nc.dram_tensor("dbgQS2", [128, G], F32, kind="ExternalOutput")
        dbgSIN = nc.dram_tensor("dbgSIN", [128, G], F32, kind="ExternalOutput")
        dbgCM = nc.dram_tensor("dbgCM", [128, G], F32, kind="ExternalOutput")
        dbgBQ = nc.dram_tensor("dbgBQ", [128, NI], F32, kind="ExternalOutput")
        dbgBQU = nc.dram_tensor("dbgBQU", [128, NI], F32, kind="ExternalOutput")
        dbgUC = nc.dram_tensor("dbgUC", [128, NI], F32, kind="ExternalOutput")
        dbgWS2 = nc.dram_tensor("dbgWS2", [128, NI * D], F32, kind="ExternalOutput")
        dbgBQ2 = nc.dram_tensor("dbgBQ2", [128, NI], F32, kind="ExternalOutput")
        dbgBQU2 = nc.dram_tensor("dbgBQU2", [128, NI], F32, kind="ExternalOutput")
        dbgPSB = nc.dram_tensor("dbgPSB", [128, NI], F32, kind="ExternalOutput")
        dbgZ2 = nc.dram_tensor("dbgZ2", [128, NI * G], F32, kind="ExternalOutput")
        dbgZ3 = nc.dram_tensor("dbgZ3", [128, NI * G], F32, kind="ExternalOutput")
    hT = nc.dram_tensor("hT", [D, T], F32, kind=kind)
    xT = nc.dram_tensor("xT", [D, T], F32, kind=kind)
    x1T = nc.dram_tensor("x1T", [D, T], F32, kind=kind)
    g1d = nc.dram_tensor("g1d", [DI, T], BF16, kind=kind)
    g2d = nc.dram_tensor("g2d", [DI, T], BF16, kind=kind)
    g3d = nc.dram_tensor("g3d", [DI, T], BF16, kind=kind)

    SINc, CMc, KTc, TRIc, EYEc, ONEFc = host_constants()
    d_sin = nc.inline_tensor(SINc, name="c_sin")
    d_cm = nc.inline_tensor(CMc, name="c_cm")
    d_kt = nc.inline_tensor(KTc, name="c_kt")
    d_tri = nc.inline_tensor(TRIc, name="c_tri")
    d_eye = nc.inline_tensor(EYEc, name="c_eye")
    d_onef = nc.inline_tensor(ONEFc, name="c_onef")
    d_onecol = nc.inline_tensor(np.ones((128, 1), np.float32), name="c_onecol")
    d_eps5 = nc.inline_tensor(np.full((128, 1), 1e-5, np.float32), name="c_eps5")
    d_eps6 = nc.inline_tensor(np.full((128, 1), 1e-6, np.float32), name="c_eps6")

    A = nc.alloc_sbuf_tensor
    C_SIN = A("C_SIN", [128, S], BF16)
    C_CM = A("C_CM", [128, S], BF16)
    C_KT = A("C_KT", [128, S], BF16)
    C_TRI = A("C_TRI", [128, 128], BF16)
    C_EYE = A("C_EYE", [128, 128], F32)
    C_ONEF = A("C_ONEF", [128, 128], F32)
    C_ONEB = A("C_ONEB", [128, 128], BF16)
    C_ONECOL = A("C_ONECOL", [128, 1], BF16)
    C_EPS5 = A("C_EPS5", [128, 1], F32)
    C_EPS6 = A("C_EPS6", [128, 1], F32)
    BQ = A("BQ", [128, NI], F32)
    BK = A("BK", [128, NI], F32)
    BQU = A("BQU", [128, NI], F32)
    B1E = A("B1E", [128, NIH], F32)
    BQR = A("BQR", [128, NI], F32)
    BKR = A("BKR", [128, NI], F32)
    BOC = A("BOC", [128, NI], F32)
    B2C = A("B2C", [128, NI], F32)
    B1R = A("B1R", [128, NIH], F32)
    UC = A("UC", [128, NI], F32)
    L1W = A("L1W", [128, NI], F32)
    L1B = A("L1B", [128, NI], F32)
    L2W = A("L2W", [128, NI], F32)
    L2B = A("L2B", [128, NI], F32)
    LBB = A("LBB", [128, NI], BF16)
    BVT = A("BVT", [128, 2 * G], BF16)
    SC1 = A("SC1", [128, NIH], F32)
    SC2 = A("SC2", [128, NIH], F32)
    SCH1 = A("SCH1", [128, NIH], F32)
    SCH2 = A("SCH2", [128, NIH], F32)
    SC3 = A("SC3", [128, NI], F32)
    ROWA = A("ROWA", [128, G], F32)
    ROWB = A("ROWB", [128, G], F32)

    base = nc.sbuf_base

    def at(name, off_kb, shape, dtype):
        return nc.alloc_sbuf_tensor_at(name, shape, dtype, offset=base + int(off_kb * KB))

    KP = at("KP", 0, [128, NI * S], BF16)
    VA = at("VA", 32, [128, (S // 128) * H * 65], BF16)
    WS1 = at("WS1", 65, [128, NI * D], BF16)
    WS2 = at("WS2", 81, [128, NI * D], BF16)
    QP = at("QP", 97, [128, NI * G], BF16)
    QT = at("QT", 105, [128, NI * G], BF16)
    PM = at("PM", 113, [128, 4 * G], BF16)
    PD = at("PD", 117, [128, 3 * G], BF16)
    QS1 = at("QS1", 121, [128, G], BF16)
    QS2 = at("QS2", 122, [128, G], BF16)
    BCS = at("BCS", 123, [64, G], F32)
    W1S = at("W1S", 0, [128, NI * DI], BF16)
    W2S = at("W2S", 64, [128, NIH * D], BF16)
    GPAN = at("GPAN", 0, [128, NIH * 2048], BF16)
    XFA = at("XFA", 0, [128, NI * T], BF16)
    W3S = at("W3S", 48, [128, NIH * D], BF16)
    HG = at("HG", 128, [128, NI * G], F32)
    XN = at("XN", 144, [128, NI * G], BF16)
    CTXN = at("CTXN", 152, [128, NI * G], BF16)
    GT = at("GT", 152, [128, NIH * G], BF16)
    AS = at("AS", 184, [128, G], F32)
    XSC = at("XSC", 164, [128, G], F32)
    OSTG = at("OSTG", 160, [128, D], F32)
    XF3 = at("XF3", 144, [128, G], F32)
    STRA = at("STRA", 128, [128, NIH * 128], BF16)
    STRB = at("STRB", 136, [128, NIH * 128], BF16)
    EVW1 = at("EVW1", 144, [128, 2048], BF16)
    EVW2 = at("EVW2", 148, [128, 2048], BF16)
    arena_end = base + 186 * KB
    assert arena_end <= nc.sbuf_top, f"SBUF over: {arena_end} > {nc.sbuf_top}"

    psW = nc.alloc_psum_tensor("psW", [128, 4 * G], F32)
    psP = nc.alloc_psum_tensor("psP", [128, G], F32)
    psC = nc.alloc_psum_tensor("psC", [65, G], F32)
    psB1 = nc.alloc_psum_tensor("psB1", [128, G], F32)
    psB2 = nc.alloc_psum_tensor("psB2", [128, G], F32)

    p = MK(nc)

    B = {nm: Buf() for nm in (
        "KP", "VA", "WS1", "WS2", "QP", "QT", "PM", "PD", "QS1", "QS2", "BCS",
        "W1S", "W2S", "GPAN", "XFA", "W3S", "HG", "XN", "CTXN", "GT", "AS",
        "XSC", "OSTG", "STRA", "STRB", "EVW1", "EVW2", "XF3",
        "psW0", "psW1", "psW2", "psW3", "psP", "psC", "psB1", "psB2",
        "rows", "cache", "stage", "BVT", "LBB", "hT", "xT", "x1T", "g1", "g2", "g3")}

    _ranges = {
        "KP": (0, 32), "VA": (32, 64.5), "WS1": (65, 81), "WS2": (81, 97),
        "QP": (97, 105), "QT": (105, 113), "PM": (113, 117), "PD": (117, 120),
        "QS1": (121, 122), "QS2": (122, 123), "BCS": (123, 125),
        "W1S": (0, 64), "W2S": (64, 128), "GPAN": (0, 128), "XFA": (0, 64),
        "STRA": (128, 136), "STRB": (136, 144), "W3S": (48, 112),
        "HG": (128, 144), "XN": (144, 152), "CTXN": (152, 160), "GT": (152, 184),
        "OSTG": (160, 164), "XSC": (164, 166), "AS": (184, 186), "XF3": (144, 146),
        "EVW1": (144, 148), "EVW2": (148, 152),
    }
    for _nm, _rng in _ranges.items():
        B[_nm].rng = _rng
    p.overlaps = [B[_nm] for _nm in _ranges]

    def str_in(dram, c0, n, nt=NI):
        return AP(dram, c0, [[T, 128], [128 * T, nt], [1, n]])

    def sb3(sb, n, nt=NI, off=0):
        return AP(sb, off, [[nt * n, 128], [n, nt], [1, n]])

    def dbg_dump(dram, sb_ap=None, cols=None, buf=None):
        p.wait("gpsimd", p.SS, p.SS.v)
        p.wait("gpsimd", p.SV, p.SV.v)
        p.wait("gpsimd", p.ST, p.ST.v)
        dv = p.dma("gpsimd", p.SW, AP(dram, 0, [[cols, 128], [1, cols]]), sb_ap)
        for e in ("tensor", "scalar", "vector", "sync"):
            p.wait(e, p.SW, dv[1])
        if buf is not None:
            buf.cons.append(dv)

    def startup():
        for dst, src in ((C_SIN, d_sin), (C_CM, d_cm), (C_KT, d_kt), (C_TRI, d_tri)):
            p.dma("gpsimd", p.SW, dst[:, :], src.ap())
        p.dma("gpsimd", p.SW, C_ONEB[:, :], d_onef.ap())
        p.dma("gpsimd", p.SW, C_ONECOL[:, :], d_onecol.ap())
        for dst, src in ((C_EYE, d_eye), (C_ONEF, d_onef), (C_EPS5, d_eps5), (C_EPS6, d_eps6)):
            p.dma("sync", p.SH, dst[:, :], src.ap())
        p.wait("vector", p.SW, p.SW.v)
        p.wait("vector", p.SH, p.SH.v)
        p.ev("vector", lambda v: v.memset(AP(VA, 64, [[16 * 1040, 128], [1040, 16], [65, 16]]), 1.0))
        def zpd(v):
            for j in range(1, 4):
                v.memset(AP(PD, (j - 1) * G, [[3 * G, 128], [1, 128 * j]]), 0.0)
        p.ev("vector", zpd)
        sv = p.sig("vector")
        B["VA"].rdy = sv
        B["PD"].rdy = sv

    def layer_stage_load(l):
        for dst, nm in ((BQR, "bqc"), (BKR, "bkc"), (BOC, "boc"), (B2C, "b2c"),
                        (B1R, "b1c"), (UC, "uc"), (L1W, "l1wc"), (L1B, "l1bc"),
                        (L2W, "l2wc"), (L2B, "l2bc")):
            p.take("sync", B["stage"])
            p.dma("sync", p.SH, dst[:, :], ins[f"{nm}{l}"].ap())
        p.take("sync", B["rows"])
        p.dma("sync", p.SH, ROWA[0:1, :], ins[f"bvr{l}"].ap()[0:1, 0:G])
        sv = p.dma("sync", p.SH, ROWB[0:1, :], ins[f"bvr{l}"].ap()[0:1, G:2 * G])
        B["rows"].rdy = sv
        B["stage"].rdy = sv

    def slot_stage_load():
        for dst, nm in ((SC1, "s1c"), (SC2, "s2c"), (SCH1, "sh1c"), (SCH2, "sh2c"),
                        (SC3, "s3c")):
            p.dma("sync", p.SH, dst[:, :], ins[nm].ap())
        B["stage"].rdy = (p.SH, p.SH.v)

    def ln_stats(src_f32_sb, sq_sb, sq_buf, eps_tile, rstd_dst, mr_dst, nt=NI):
        ndim = nt * 128
        p.use("scalar", B["HG"])
        p.take("scalar", B["XN"], sq_buf)
        def casts(sc):
            for it in range(nt):
                sc.copy(XN[:, it * G:(it + 1) * G], src_f32_sb[:, it * G:(it + 1) * G])
            for it in range(nt):
                sc.activation(sq_sb[:, it * G:(it + 1) * G], src_f32_sb[:, it * G:(it + 1) * G], AF.Square)
        p.ev("scalar", casts)
        ss = p.sig("scalar")
        B["XN"].rdy = ss
        sq_buf.rdy = ss
        B["HG"].cons.append(ss)
        p.use("tensor", B["XN"], sq_buf)
        p.take("tensor", B["psB1"], B["psB2"])
        def stats(t):
            last = None
            for it in range(nt):
                last = t.matmul(psB1[0:1, :], C_ONECOL[:, :], XN[:, it * G:(it + 1) * G],
                                start=(it == 0), stop=(it == nt - 1))
            for it in range(nt):
                last = t.matmul(psB2[0:1, :], C_ONECOL[:, :], sq_sb[:, it * G:(it + 1) * G],
                                start=(it == 0), stop=(it == nt - 1))
            last.then_inc(p.ST.h, 1)
        p.ST.reserve(1)
        p.ev("tensor", stats)
        stv = (p.ST, p.ST.v)
        B["XN"].cons.append(stv)
        sq_buf.cons.append(stv)
        p.wait("scalar", *stv)
        p.take("scalar", B["rows"])
        inv = 1.0 / ndim
        p.ev("scalar", lambda sc: sc.mul(ROWA[0:1, :], psB1[0:1, :], inv))
        p.ev("scalar", lambda sc: sc.mul(ROWA[32:33, :], psB2[0:1, :], inv))
        p.ev("scalar", lambda sc: sc.activation(ROWB[32:33, :], ROWA[0:1, :], AF.Square))
        ss = p.sig("scalar")
        p.wait("vector", *ss)
        p.ev("vector", lambda v: v.tensor_sub(ROWA[64:65, :], ROWA[32:33, :], ROWB[32:33, :]))
        sv = p.sig("vector")
        p.wait("scalar", *sv)
        p.ev("scalar", lambda sc: sc.activation(ROWB[64:65, :], ROWA[64:65, :], AF.Ln,
                                                bias=eps_tile[64:65, 0:1]))
        p.ev("scalar", lambda sc: sc.activation(ROWB[0:1, :], ROWB[64:65, :], AF.Exp, bias=0.0, scale=-0.5))
        ss = p.sig("scalar")
        p.wait("vector", *ss)
        p.ev("vector", lambda v: v.tensor_mul(ROWB[32:33, :], ROWA[0:1, :], ROWB[0:1, :]))
        sv = p.sig("vector")
        rt, rb = rstd_dst
        mt, mb = mr_dst
        p.wait("scalar", *sv)
        p.take("scalar", B["cache"])
        p.ev("scalar", lambda sc: sc.copy(rt[rb:rb + 1, :], ROWB[0:1, :]))
        p.ev("scalar", lambda sc: sc.copy(mt[mb:mb + 1, :], ROWB[32:33, :]))
        ss = p.sig("scalar")
        B["rows"].rdy = ss
        B["cache"].rdy = ss
        B["psB1"].cons.append(ss)
        B["psB2"].cons.append(ss)
        return ss

    def ln_normalize(src_f32_sb, dst_sb, dst_dtype_is_f32, rstd_src, mr_src, nt=NI,
                     dst_buf=None, src_buf=None):
        rt, rb = rstd_src
        mt, mb = mr_src
        p.use("tensor", B["cache"])
        p.take("tensor", B["psB1"], B["psB2"])
        def bcast(t):
            t.matmul(psB1[:, :], C_ONEF[rb:rb + 1, :], rt[rb:rb + 1, :], start=True, stop=True)
            t.matmul(psB2[:, :], C_ONEF[mb:mb + 1, :], mt[mb:mb + 1, :],
                     start=True, stop=True).then_inc(p.ST.h, 1)
        p.ST.reserve(1)
        p.ev("tensor", bcast)
        stv = (p.ST, p.ST.v)
        B["cache"].cons.append(stv)
        p.wait("vector", *stv)
        if src_buf is not None:
            p.use("vector", src_buf)
        p.take("vector", B["AS"])
        if dst_buf is not None:
            p.take("vector", dst_buf)
        def norm(v):
            for it in range(nt):
                v.tensor_mul(AS[:, :], src_f32_sb[:, it * G:(it + 1) * G], psB1[:, :])
                v.tensor_sub(dst_sb[:, it * G:(it + 1) * G], AS[:, :], psB2[:, :])
        p.ev("vector", norm)
        sv = p.sig("vector")
        if dst_buf is not None:
            dst_buf.rdy = sv
        if src_buf is not None:
            src_buf.cons.append(sv)
        B["psB1"].cons.append(sv)
        B["psB2"].cons.append(sv)
        B["cache"].cons.append(sv)
        B["AS"].rdy = sv
        B["AS"].cons.append(sv)
        return sv


    def cast_lnb(col_src):
        p.use("scalar", B["stage"])
        p.take("scalar", B["LBB"])
        p.ev("scalar", lambda sc, cs=col_src: sc.copy(LBB[:, :], cs[:, :]))
        B["LBB"].rdy = p.sig("scalar")

    def load_w_pair(l, names, fold):
        slots = {names[0]: WS1, names[1]: WS2}
        bufs = {names[0]: B["WS1"], names[1]: B["WS2"]}
        for nm in names:
            sl, bb = slots[nm], bufs[nm]
            p.take("gpsimd", bb)
            dv = p.dma("gpsimd", p.SW, sb3(sl, D), AP(ins[f"{nm}{l}"], 0, [[D, 128], [128 * D, NI], [1, D]]))
            bb.rdy = dv
        for nm in names:
            sl, bb = slots[nm], bufs[nm]
            if nm in ("wq", "wk"):
                dstc, rawc = (BQ, BQR) if nm == "wq" else (BK, BKR)
                p.use("tensor", bb, B["LBB"])
                p.take("tensor", B["psB2"])
                def bias_mm(t, sl=sl):
                    last = None
                    for ot in range(NI):
                        for it in range(NI):
                            last = t.matmul(psB2[:, ot:ot + 1],
                                            sl[:, it * D + ot * 128:it * D + (ot + 1) * 128],
                                            LBB[:, it:it + 1], start=(it == 0), stop=(it == NI - 1))
                    last.then_inc(p.ST.h, 1)
                p.ST.reserve(1)
                p.ev("tensor", bias_mm)
                stv = (p.ST, p.ST.v)
                p.wait("vector", *stv)
                p.use("vector", B["stage"])
                p.ev("vector", lambda v, dstc=dstc, rawc=rawc: v.tensor_add(dstc[:, :], psB2[:, 0:NI], rawc[:, :]))
                if nm == "wq":
                    p.ev("vector", lambda v: v.engine_nop())
                    p.ev("vector", lambda v: v.tensor_add(BQU[:, :], BQ[:, :], UC[:, :]))
                sv = p.sig("vector")
                B["psB2"].cons.append(sv)
                B["stage"].cons.append(sv)
            elif nm == "wv":
                p.take("sync", B["rows"])
                p.dma("sync", p.SH, ROWA[0:1, :], ins[f"bvr{l}"].ap()[0:1, 0:G])
                dvb = p.dma("sync", p.SH, ROWB[0:1, :], ins[f"bvr{l}"].ap()[0:1, G:2 * G])
                B["rows"].rdy = dvb
                p.use("tensor", bb, B["LBB"])
                p.take("tensor", B["psB1"], B["psB2"])
                def bias_v(t, sl=sl):
                    last = None
                    for half, ps in ((0, psB2), (1, psB1)):
                        for it in range(NI):
                            last = t.matmul(ps[0:1, 0:G], LBB[:, it:it + 1],
                                            sl[:, it * D + half * G:it * D + half * G + G],
                                            start=(it == 0), stop=(it == NI - 1))
                    last.then_inc(p.ST.h, 1)
                p.ST.reserve(1)
                p.ev("tensor", bias_v)
                stv = (p.ST, p.ST.v)
                p.wait("vector", *stv)
                p.use("vector", B["stage"], B["rows"])
                p.wait("vector", *dvb)
                p.ev("vector", lambda v: v.tensor_add(ROWA[32:33, :], psB2[0:1, 0:G], ROWA[0:1, :]))
                p.ev("vector", lambda v: v.tensor_add(ROWB[32:33, :], psB1[0:1, 0:G], ROWB[0:1, :]))
                sv = p.sig("vector")
                B["psB1"].cons.append(sv)
                B["psB2"].cons.append(sv)
                p.wait("scalar", *sv)
                p.take("scalar", B["BVT"])
                p.ev("scalar", lambda sc: sc.copy(BVT[0:1, 0:G], ROWA[32:33, :]))
                p.ev("scalar", lambda sc: sc.copy(BVT[0:1, G:2 * G], ROWB[32:33, :]))
                ssv = p.sig("scalar")
                B["BVT"].rdy = ssv
                B["rows"].rdy = ssv
                p.wait("tensor", *ssv)
                p.take("tensor", B["psB1"], B["psB2"])
                p.ST.reserve(1)
                def bvbc(t):
                    t.matmul(psB1[:, :], C_ONEB[0:1, :], BVT[0:1, 0:G], start=True, stop=True)
                    t.matmul(psB2[:, :], C_ONEB[0:1, :], BVT[0:1, G:2 * G],
                             start=True, stop=True).then_inc(p.ST.h, 1)
                p.ev("tensor", bvbc)
                stv2 = (p.ST, p.ST.v)
                p.wait("scalar", *stv2)
                p.take("scalar", B["BVT"])
                p.ev("scalar", lambda sc: sc.copy(BVT[:, 0:G], psB1[:, :]))
                p.ev("scalar", lambda sc: sc.copy(BVT[:, G:2 * G], psB2[:, :]))
                sbv = p.sig("scalar")
                B["BVT"].rdy = sbv
                B["psB1"].cons.append(sbv)
                B["psB2"].cons.append(sbv)
            if fold and nm in ("wq", "wk", "wv"):
                p.use("vector", bb, B["stage"])
                if nm in ("wq", "wk"):
                    pass
                p.wait("vector", p.ST, p.ST.v)
                def scw(v, sl=sl):
                    for it in range(NI):
                        v.tensor_scalar_mul(sl[:, it * D:(it + 1) * D],
                                            sl[:, it * D:(it + 1) * D], L1W[:, it:it + 1])
                p.ev("vector", scw)
                bb.rdy = p.sig("vector")
                B["stage"].cons.append(bb.rdy)

    def load_w_ffn(l):
        p.take("gpsimd", B["W1S"])
        d1 = p.dma("gpsimd", p.SW, sb3(W1S, DI), AP(ins[f"w1{l}"], 0, [[DI, 128], [128 * DI, NI], [1, DI]]))
        B["W1S"].rdy = d1
        p.take("gpsimd", B["W2S"])
        d2 = p.dma("gpsimd", p.SW, sb3(W2S, D, nt=NIH), AP(ins[f"w2{l}"], 0, [[D, 128], [128 * D, NIH], [1, D]]))
        B["W2S"].rdy = d2
        cast_lnb(L2B)
        p.use("tensor", B["W1S"], B["LBB"])
        p.take("tensor", B["psB2"])
        def bias_mm(t):
            last = None
            for ot in range(NIH):
                for it in range(NI):
                    last = t.matmul(psB2[:, ot:ot + 1],
                                    W1S[:, it * DI + ot * 128:it * DI + (ot + 1) * 128],
                                    LBB[:, it:it + 1], start=(it == 0), stop=(it == NI - 1))
            last.then_inc(p.ST.h, 1)
        p.ST.reserve(1)
        p.ev("tensor", bias_mm)
        stv = (p.ST, p.ST.v)
        p.wait("vector", *stv)
        p.use("vector", B["stage"])
        p.ev("vector", lambda v: v.tensor_add(B1E[:, :], psB2[:, 0:NIH], B1R[:, :]))
        def scw(v):
            for it in range(NI):
                v.tensor_scalar_mul(W1S[:, it * DI:(it + 1) * DI],
                                    W1S[:, it * DI:(it + 1) * DI], L2W[:, it:it + 1])
        p.ev("vector", scw)
        sv = p.sig("vector")
        B["psB2"].cons.append(sv)
        B["W1S"].rdy = sv

    def kv_pass(l, b):
        src = xT if l == 0 else hT
        p.take("vector", B["VA"])
        p.ev("vector", lambda v: v.memset(AP(VA, 64, [[16 * 1040, 128], [1040, 16], [65, 16]]), 1.0))
        B["VA"].rdy = p.sig("vector")
        if debug_taps and l == DBG_L and b == DBG_B:
            p.use("gpsimd", B["WS2"])
            dbg_dump(dbgWS2, buf=B["WS2"], sb_ap=AP(WS2, 0, [[NI * D, 128], [1, NI * D]]), cols=NI * D)
        for c in range(NG):
            col0 = b * S + c * G
            p.take("sync", B["HG"])
            if l == 0:
                for tt in range(4):
                    for half in range(2):
                        p.take("sync", B["XSC"])
                        dv = p.dma("sync", p.SH, XSC[:, :],
                                   AP(xin, (b * S + c * G + tt * 128) * D + half * G,
                                      [[D, 128], [1, G]]))
                        B["XSC"].rdy = dv
                        p.use("tensor", B["XSC"])
                        for sub in range(4):
                            it = half * 4 + sub
                            ps, pb = (psP, B["psP"]) if (it % 2 == 0) else (psB1, B["psB1"])
                            p.take("tensor", pb)
                            p.ST.reserve(1)
                            p.ev("tensor", lambda t, sub=sub, ps=ps: t.transpose(
                                ps[:, 0:128], XSC[:, sub * 128:(sub + 1) * 128],
                                C_EYE[:, :]).then_inc(p.ST.h, 1))
                            stv = (p.ST, p.ST.v)
                            p.wait("scalar", *stv)
                            p.ev("scalar", lambda sc, it=it, tt=tt, ps=ps: sc.copy(
                                HG[:, it * G + tt * 128:it * G + (tt + 1) * 128], ps[:, 0:128]))
                            ssv = p.sig("scalar")
                            pb.cons.append(ssv)
                            B["XSC"].cons.append(ssv)
                B["HG"].rdy = (p.SS, p.SS.v)
                p.use("sync", B["HG"])
                sv = p.dma("sync", p.SO, str_in(xT, col0, G), sb3(HG, G))
                B["HG"].cons.append(sv)
            else:
                dv = p.dma("sync", p.SH, sb3(HG, G), str_in(src, col0, G))
                B["HG"].rdy = dv
            ln_stats(HG, QP, B["QP"], C_EPS6, (ROWB, 0), (ROWB, 32))
            ln_normalize(HG, XN, False, (ROWB, 0), (ROWB, 32), dst_buf=B["XN"], src_buf=B["HG"])
            p.use("tensor", B["XN"], B["WS1"])
            for ot in range(NI):
                ps, pb = (psP, B["psP"]) if ot % 2 == 0 else (psB1, B["psB1"])
                p.take("tensor", pb)
                def kmm(t, ot=ot, ps=ps):
                    last = None
                    for it in range(NI):
                        last = t.matmul(ps[:, :], WS1[:, it * D + ot * 128:it * D + (ot + 1) * 128],
                                        XN[:, it * G:(it + 1) * G], start=(it == 0), stop=(it == NI - 1))
                    last.then_inc(p.ST.h, 1)
                p.ST.reserve(1)
                p.ev("tensor", kmm)
                stv = (p.ST, p.ST.v)
                p.wait("scalar", *stv)
                p.take("scalar", B["KP"]) if (c == 0 and ot == 0) else None
                p.ev("scalar", lambda sc, ot=ot, ps=ps, c=c: sc.add(
                    KP[:, ot * S + c * G:ot * S + (c + 1) * G], ps[:, :], BK[:, ot:ot + 1]))
                ssv = p.sig("scalar")
                pb.cons.append(ssv)
            B["KP"].rdy = (p.SS, p.SS.v)
            B["XN"].cons.append((p.ST, p.ST.v))
            p.use("tensor", B["XN"], B["WS2"])
            for tt in range(4):
                for half in range(2):
                    ps, pb = (psP, B["psP"]) if half == 0 else (psB1, B["psB1"])
                    p.take("tensor", pb)
                    def vmm(t, tt=tt, half=half, ps=ps):
                        last = None
                        for it in range(NI):
                            last = t.matmul(ps[:, :], XN[:, it * G + tt * 128:it * G + (tt + 1) * 128],
                                     WS2[:, it * D + half * G:it * D + half * G + G],
                                     start=(it == 0), stop=(it == NI - 1))
                        last.then_inc(p.ST.h, 1)
                    p.ST.reserve(1)
                    p.ev("tensor", vmm)
                    stv = (p.ST, p.ST.v)
                    p.wait("vector", *stv)
                    p.use("vector", B["BVT"])
                    ttk = c * 4 + tt
                    p.ev("vector", lambda v, ttk=ttk, half=half, ps=ps: v.tensor_add(
                        AP(VA, ttk * 1040 + half * 8 * 65, [[16 * 1040, 128], [65, 8], [1, 64]]),
                        ps[:, :], BVT[:, half * G:(half + 1) * G]))
                    ssv = p.sig("vector")
                    pb.cons.append(ssv)
            B["VA"].rdy = (p.SV, p.SV.v)
            B["XN"].cons.append((p.ST, p.ST.v))
        B["WS1"].cons.append((p.ST, p.ST.v))
        B["WS2"].cons.append((p.ST, p.ST.v))
        B["BVT"].cons.append((p.SV, p.SV.v))
        if debug_taps and l == DBG_L and b == DBG_B:
            dbg_dump(dbgKP, buf=B["KP"], sb_ap=AP(KP, 0, [[NI * S, 128], [1, NI * S]]), cols=NI * S)
            dbg_dump(dbgVA, buf=B["VA"], sb_ap=AP(VA, 0, [[16 * H * 65, 128], [1, 16 * H * 65]]), cols=16 * H * 65)

    def attn_pass(l, b):
        src = xT if l == 0 else hT
        p.take("vector", B["PD"])
        def zpd2(v):
            for j in range(1, 4):
                v.memset(AP(PD, (j - 1) * G, [[3 * G, 128], [1, 128 * j]]), 0.0)
        p.ev("vector", zpd2)
        B["PD"].rdy = p.sig("vector")
        if debug_taps and l == DBG_L and b == DBG_B:
            p.wait("scalar", p.SV, p.SV.v)
            p.take("scalar", B["AS"])
            p.ev("scalar", lambda sc: sc.copy(AS[:, 0:NI], psB2[:, 0:NI]))
            sps = p.sig("scalar")
            B["psB2"].cons.append(sps)
            B["AS"].rdy = sps
            dbg_dump(dbgBQ2, sb_ap=AP(BQ, 0, [[NI, 128], [1, NI]]), cols=NI)
            dbg_dump(dbgBQU2, sb_ap=AP(BQU, 0, [[NI, 128], [1, NI]]), cols=NI)
            dbg_dump(dbgPSB, buf=B["AS"], sb_ap=AP(AS, 0, [[G, 128], [1, NI]]), cols=NI)
        for g in range(NG):
            col0 = b * S + g * G
            p.take("sync", B["HG"])
            dv = p.dma("sync", p.SH, sb3(HG, G), str_in(src, col0, G))
            B["HG"].rdy = dv
            ln_stats(HG, QP, B["QP"], C_EPS6, (ROWB, 0), (ROWB, 32))
            ln_normalize(HG, XN, False, (ROWB, 0), (ROWB, 32), dst_buf=B["XN"], src_buf=B["HG"])
            p.use("tensor", B["XN"], B["WS1"])
            for ot in range(NI):
                ps, pb = (psP, B["psP"]) if ot % 2 == 0 else (psB1, B["psB1"])
                p.take("tensor", pb)
                def qmm(t, ot=ot, ps=ps):
                    last = None
                    for it in range(NI):
                        last = t.matmul(ps[:, :], WS1[:, it * D + ot * 128:it * D + (ot + 1) * 128],
                                        XN[:, it * G:(it + 1) * G], start=(it == 0), stop=(it == NI - 1))
                    last.then_inc(p.ST.h, 1)
                p.ST.reserve(1)
                p.ev("tensor", qmm)
                stv = (p.ST, p.ST.v)
                p.wait("scalar", *stv)
                p.take("scalar", B["QS1"])
                p.ev("scalar", lambda sc, ot=ot, ps=ps: sc.add(
                    QP[:, ot * G:(ot + 1) * G], ps[:, :], BQ[:, ot:ot + 1]))
                p.ev("scalar", lambda sc, ot=ot, ps=ps: sc.add(
                    QS1[:, :], ps[:, :], BQU[:, ot:ot + 1]))
                ssv = p.sig("scalar")
                pb.cons.append(ssv)
                p.wait("vector", *ssv)
                def qtl(v, ot=ot, g=g):
                    v.tensor_copy(QS2[0:32, :], QS1[32:64, :])
                    v.tensor_copy(QS2[32:64, :], QS1[0:32, :])
                    v.tensor_copy(QS2[64:96, :], QS1[96:128, :])
                    v.tensor_copy(QS2[96:128, :], QS1[64:96, :])
                    v.drain()
                    v.tensor_mul(AS[:, :], QS1[:, :], C_SIN[:, g * G:(g + 1) * G])
                    v.tensor_mul(XSC[:, :], QS2[:, :], C_CM[:, g * G:(g + 1) * G])
                    v.drain()
                    v.tensor_add(QT[:, ot * G:(ot + 1) * G], AS[:, :], XSC[:, :])
                p.take("vector", B["AS"], B["XSC"])
                if ot == 0:
                    p.take("vector", B["GT"])
                p.ev("vector", qtl)
                svv = p.sig("vector")
                B["QS1"].cons.append(svv)
                B["AS"].rdy = svv
                B["AS"].cons.append(svv)
                B["XSC"].rdy = svv
                B["XSC"].cons.append(svv)
                B["GT"].cons.append(svv)
            svq = (p.SV, p.SV.v)
            B["QP"].rdy = (p.SS, p.SS.v)
            B["QT"].rdy = svq
            B["XN"].cons.append((p.ST, p.ST.v))
            if debug_taps and l == DBG_L and b == DBG_B and g == DBG_G:
                dbg_dump(dbgQP, buf=B["QP"], sb_ap=AP(QP, 0, [[NI * G, 128], [1, NI * G]]), cols=NI * G)
                dbg_dump(dbgQT, buf=B["QT"], sb_ap=AP(QT, 0, [[NI * G, 128], [1, NI * G]]), cols=NI * G)
                dbg_dump(dbgXN, buf=B["XN"], sb_ap=AP(XN, 0, [[NI * G, 128], [1, NI * G]]), cols=NI * G)
                dbg_dump(dbgQS1, buf=B["QS1"], sb_ap=AP(QS1, 0, [[G, 128], [1, G]]), cols=G)
                dbg_dump(dbgQS2, buf=B["QS2"], sb_ap=AP(QS2, 0, [[G, 128], [1, G]]), cols=G)
                dbg_dump(dbgSIN, AP(C_SIN, 0, [[S, 128], [1, G]]), G)
                dbg_dump(dbgCM, AP(C_CM, 0, [[S, 128], [1, G]]), G)
                dbg_dump(dbgBQ, AP(BQ, 0, [[NI, 128], [1, NI]]), NI)
                dbg_dump(dbgBQU, AP(BQU, 0, [[NI, 128], [1, NI]]), NI)
                dbg_dump(dbgUC, AP(UC, 0, [[NI, 128], [1, NI]]), NI)
            nkt = 4 * g + 4
            for h in range(H):
                pt, hb = h // 2, (h % 2) * 64
                ctx_started = False
                for ktg in range((nkt + 3) // 4):
                    js = [j for j in range(4) if ktg * 4 + j < nkt]
                    diag = (ktg == g)
                    p.use("tensor", B["QP"], B["QT"], B["KP"])
                    for j in js:
                        p.take("tensor", B[f"psW{j}"])
                    def smm(t, js=js, ktg=ktg, pt=pt, hb=hb, g=g):
                        last = None
                        for j in js:
                            kt = ktg * 4 + j
                            last = t.matmul(psW[:, j * G:(j + 1) * G],
                                            KP[hb:hb + 64, pt * S + kt * 128:pt * S + (kt + 1) * 128],
                                            QP[hb:hb + 64, pt * G:(pt + 1) * G],
                                            start=True, stop=False)
                            last = t.matmul(psW[:, j * G:(j + 1) * G],
                                            C_KT[hb:hb + 64, kt * 128:(kt + 1) * 128],
                                            QT[hb:hb + 64, pt * G:(pt + 1) * G],
                                            start=False, stop=True)
                        last.then_inc(p.ST.h, 1)
                    p.ST.reserve(1)
                    p.ev("tensor", smm)
                    stv = (p.ST, p.ST.v)
                    p.wait("scalar", *stv)
                    if diag:
                        p.take("scalar", B["PD"], B["PM"])
                        def expd(sc, js=js):
                            for j in js:
                                if j == 0:
                                    sc.activation(PM[:, 0:G], psW[:, 0:G], AF.Exp, bias=0.0, scale=0.125)
                                else:
                                    sc.activation(PD[:, (j - 1) * G + 128 * j:j * G],
                                                  psW[:, j * G + 128 * j:(j + 1) * G],
                                                  AF.Exp, bias=0.0, scale=0.125)
                        p.ev("scalar", expd)
                        ssv = p.sig("scalar")
                        for j in js:
                            B[f"psW{j}"].cons.append(ssv)
                        p.wait("vector", *ssv)
                        def trim(v, js=js):
                            for j in js:
                                if j == 0:
                                    v.tensor_mul(PM[:, 0:128], PM[:, 0:128], C_TRI[:, :])
                                else:
                                    v.tensor_mul(PD[:, (j - 1) * G + 128 * j:(j - 1) * G + 128 * (j + 1)],
                                                 PD[:, (j - 1) * G + 128 * j:(j - 1) * G + 128 * (j + 1)],
                                                 C_TRI[:, :])
                        p.ev("vector", trim)
                        pready = p.sig("vector")
                        B["PD"].rdy = pready
                        B["PM"].rdy = pready
                    else:
                        p.take("scalar", B["PM"])
                        def expm(sc, js=js):
                            n = len(js)
                            sc.activation(PM[:, js[0] * G:(js[0] + n) * G],
                                          psW[:, js[0] * G:(js[0] + n) * G],
                                          AF.Exp, bias=0.0, scale=0.125)
                        p.ev("scalar", expm)
                        pready = p.sig("scalar")
                        for j in js:
                            B[f"psW{j}"].cons.append(pready)
                        B["PM"].rdy = pready
                    p.wait("tensor", *pready)
                    p.use("tensor", B["VA"])
                    if not ctx_started:
                        p.take("tensor", B["psC"])
                    def cmm(t, js=js, ktg=ktg, h=h, diag=diag, first=not ctx_started,
                            last_kt=nkt - 1):
                        lastmm = None
                        for j in js:
                            kt = ktg * 4 + j
                            if diag and j > 0:
                                Pap = PD[:, (j - 1) * G:j * G]
                            else:
                                Pap = PM[:, j * G:(j + 1) * G]
                            lastmm = t.matmul(psC[:, :],
                                              AP(VA, kt * 1040 + h * 65, [[16 * 1040, 128], [1, 65]]),
                                              Pap,
                                              start=(first and j == js[0]), stop=(kt == last_kt))
                        lastmm.then_inc(p.ST.h, 1)
                    p.ST.reserve(1)
                    p.ev("tensor", cmm)
                    ctv = (p.ST, p.ST.v)
                    for j in js:
                        (B["PD"] if diag else B["PM"]).cons.append(ctv)
                    ctx_started = True
                p.wait("vector", *ctv)
                p.take("vector", B["rows"])
                p.ev("vector", lambda v: v.reciprocal(ROWA[0:1, :], psC[64:65, :]))
                sv = p.sig("vector")
                B["rows"].rdy = sv
                p.wait("tensor", *sv)
                p.take("tensor", B["psB2"])
                p.ST.reserve(1)
                p.ev("tensor", lambda t: t.matmul(psB2[0:64, :], C_ONEF[0:1, 0:64],
                                                  ROWA[0:1, :], start=True,
                                                  stop=True).then_inc(p.ST.h, 1))
                stv = (p.ST, p.ST.v)
                B["rows"].cons.append(stv)
                p.wait("scalar", *stv)
                p.take("scalar", B["BCS"])
                p.ev("scalar", lambda sc: sc.copy(BCS[:, :], psB2[0:64, :]))
                ssv = p.sig("scalar")
                B["psB2"].cons.append(ssv)
                B["BCS"].rdy = ssv
                p.wait("vector", *ssv)
                p.take("vector", B["CTXN"]) if h == 0 else None
                p.ev("vector", lambda v, pt=pt, hb=hb: v.tensor_mul(
                    CTXN[hb:hb + 64, pt * G:(pt + 1) * G], psC[0:64, :], BCS[:, :]))
                sv = p.sig("vector")
                B["psC"].cons.append(sv)
                B["rows"].cons.append(sv)
                B["BCS"].cons.append(sv)
            B["CTXN"].rdy = (p.SV, p.SV.v)
            B["QP"].cons.append((p.ST, p.ST.v))
            B["QT"].cons.append((p.ST, p.ST.v))
            B["KP"].cons.append((p.ST, p.ST.v))
            B["VA"].cons.append((p.ST, p.ST.v))
            if debug_taps and l == DBG_L and b == DBG_B and g == DBG_G:
                dbg_dump(dbgCT, buf=B["CTXN"], sb_ap=AP(CTXN, 0, [[NI * G, 128], [1, NI * G]]), cols=NI * G)
            p.use("tensor", B["CTXN"], B["WS2"])
            for ot in range(NI):
                ps, pb = (psP, B["psP"]) if ot % 2 == 0 else (psB1, B["psB1"])
                p.take("tensor", pb)
                def omm(t, ot=ot, ps=ps):
                    last = None
                    for ptt in range(NI):
                        last = t.matmul(ps[:, :], WS2[:, ptt * D + ot * 128:ptt * D + (ot + 1) * 128],
                                        CTXN[:, ptt * G:(ptt + 1) * G],
                                        start=(ptt == 0), stop=(ptt == NI - 1))
                    last.then_inc(p.ST.h, 1)
                p.ST.reserve(1)
                p.ev("tensor", omm)
                stv = (p.ST, p.ST.v)
                p.wait("scalar", *stv)
                p.take("scalar", B["AS"])
                p.ev("scalar", lambda sc, ot=ot, ps=ps: sc.add(AS[:, :], ps[:, :], BOC[:, ot:ot + 1]))
                ssv = p.sig("scalar")
                pb.cons.append(ssv)
                B["AS"].rdy = ssv
                p.wait("vector", *ssv)
                p.ev("vector", lambda v, ot=ot: v.tensor_add(
                    HG[:, ot * G:(ot + 1) * G], HG[:, ot * G:(ot + 1) * G], AS[:, :]))
                svv = p.sig("vector")
                B["AS"].cons.append(svv)
            svv = (p.SV, p.SV.v)
            B["HG"].rdy = svv
            p.use("sync", B["HG"])
            so = p.dma("sync", p.SO, str_in(hT, col0, G), sb3(HG, G))
            if debug_taps and l == DBG_L:
                so = p.dma("sync", p.SO, str_in(dbgA, col0, G), sb3(HG, G))
            B["HG"].cons.append(so)
            B["CTXN"].cons.append((p.ST, p.ST.v))
            B["WS1"].cons.append((p.ST, p.ST.v))
            B["WS2"].cons.append((p.ST, p.ST.v))

    def ffn_pass(l):
        load_w_ffn(l)
        for gg in range(NB * NG):
            b, c = gg // NG, gg % NG
            col0 = b * S + c * G
            p.take("sync", B["HG"])
            dv = p.dma("sync", p.SH, sb3(HG, G), str_in(hT, col0, G))
            B["HG"].rdy = dv
            ln_stats(HG, GT, B["GT"], C_EPS6, (ROWB, 0), (ROWB, 32))
            ln_normalize(HG, XN, False, (ROWB, 0), (ROWB, 32), dst_buf=B["XN"], src_buf=B["HG"])
            p.use("tensor", B["XN"], B["W1S"])
            p.take("scalar", B["GT"])
            for ot in range(NIH):
                j = ot % 4
                p.take("tensor", B[f"psW{j}"])
                def hmm(t, ot=ot, j=j):
                    last = None
                    for it in range(NI):
                        last = t.matmul(psW[:, j * G:(j + 1) * G],
                                        W1S[:, it * DI + ot * 128:it * DI + (ot + 1) * 128],
                                        XN[:, it * G:(it + 1) * G], start=(it == 0), stop=(it == NI - 1))
                    last.then_inc(p.ST.h, 1)
                p.ST.reserve(1)
                p.ev("tensor", hmm)
                stv = (p.ST, p.ST.v)
                p.wait("scalar", *stv)
                p.ev("scalar", lambda sc, ot=ot, j=j: sc.activation(
                    GT[:, ot * G:(ot + 1) * G], psW[:, j * G:(j + 1) * G], AF.Gelu,
                    bias=B1E[:, ot:ot + 1], scale=1.0))
                ssv = p.sig("scalar")
                B[f"psW{j}"].cons.append(ssv)
            B["GT"].rdy = (p.SS, p.SS.v)
            B["XN"].cons.append((p.ST, p.ST.v))
            B["W1S"].cons.append((p.ST, p.ST.v))
            p.use("tensor", B["GT"], B["W2S"])
            for ot in range(NI):
                ps, pb = (psP, B["psP"]) if ot % 2 == 0 else (psB1, B["psB1"])
                p.take("tensor", pb)
                def wmm(t, ot=ot, ps=ps):
                    last = None
                    for it in range(NIH):
                        last = t.matmul(ps[:, :], W2S[:, it * D + ot * 128:it * D + (ot + 1) * 128],
                                        GT[:, it * G:(it + 1) * G], start=(it == 0), stop=(it == NIH - 1))
                    last.then_inc(p.ST.h, 1)
                p.ST.reserve(1)
                p.ev("tensor", wmm)
                stv = (p.ST, p.ST.v)
                p.wait("scalar", *stv)
                p.take("scalar", B["AS"])
                p.ev("scalar", lambda sc, ot=ot, ps=ps: sc.add(AS[:, :], ps[:, :], B2C[:, ot:ot + 1]))
                ssv = p.sig("scalar")
                pb.cons.append(ssv)
                B["AS"].rdy = ssv
                p.wait("vector", *ssv)
                p.ev("vector", lambda v, ot=ot: v.tensor_add(
                    HG[:, ot * G:(ot + 1) * G], HG[:, ot * G:(ot + 1) * G], AS[:, :]))
                B["AS"].cons.append(p.sig("vector"))
            B["GT"].cons.append((p.ST, p.ST.v))
            B["W2S"].cons.append((p.ST, p.ST.v))
            if l == nl - 1 and do_slot:
                for it in range(NI):
                    p.take("sync", B["XSC"])
                    if it == 0:
                        p.take("sync", B["GT"])
                    dv = p.dma("sync", p.SH, XSC[:, :],
                               AP(xT, it * 128 * T + col0, [[T, 128], [1, G]]))
                    B["XSC"].rdy = dv
                    p.wait("vector", *dv)
                    p.wait("vector", p.SV, p.SV.v)
                    p.ev("vector", lambda v, it=it: v.tensor_add(
                        HG[:, it * G:(it + 1) * G], HG[:, it * G:(it + 1) * G], XSC[:, :]))
                    svz = p.sig("vector")
                    B["XSC"].cons.append(svz)
                    B["GT"].cons.append(svz)
                B["HG"].rdy = (p.SV, p.SV.v)
                ln_stats(HG, GT, B["GT"], C_EPS5, (ROWB, 0), (ROWB, 32))
                ln_normalize(HG, HG, True, (ROWB, 0), (ROWB, 32), src_buf=B["HG"])
                B["HG"].rdy = (p.SV, p.SV.v)
                p.use("sync", B["HG"])
                so = p.dma("sync", p.SO, str_in(x1T, col0, G), sb3(HG, G))
                B["HG"].cons.append(so)
                B["x1T"].rdy = so
            else:
                B["HG"].rdy = (p.SV, p.SV.v)
                p.use("sync", B["HG"])
                so = p.dma("sync", p.SO, str_in(hT, col0, G), sb3(HG, G))
                B["HG"].cons.append(so)

    def slot_f0():
        slot_stage_load()
        p.take("gpsimd", B["XFA"])
        if B["x1T"].rdy is not None:
            p.wait("gpsimd", *B["x1T"].rdy)
        for qq in range(4):
            dv = p.dma("gpsimd", p.SW, AP(XFA, qq * 1024, [[NI * T, 128], [T, NI], [1, 1024]]),
                       AP(x1T, qq * 1024, [[T, 128], [128 * T, NI], [1, 1024]]))
        B["XFA"].rdy = dv
        for ot in range(NIH):
            sa, ba = (STRA, B["STRA"]) if ot % 2 == 0 else (STRB, B["STRB"])
            p.take("gpsimd", ba)
            dv1 = p.dma("gpsimd", p.SW, AP(sa, 0, [[NIH * 128, 128], [128, NI], [1, 128]]),
                        AP(ins["w1s"], ot * 128, [[DI, 128], [128 * DI, NI], [1, 128]]))
            dv2 = p.dma("gpsimd", p.SW, AP(sa, NI * 128, [[NIH * 128, 128], [128, NI], [1, 128]]),
                        AP(ins["w2s"], ot * 128, [[DI, 128], [128 * DI, NI], [1, 128]]))
            ba.rdy = dv2
            for gg in range(NB * NG):
                p.use("tensor", ba, B["XFA"], B["stage"])
                p.take("tensor", B["psW0"], B["psW1"])
                def f0mm(t, sa=sa, gg=gg):
                    last = None
                    for w in range(2):
                        for it in range(NI):
                            last = t.matmul(psW[:, w * G:(w + 1) * G],
                                            sa[:, (w * NI + it) * 128:(w * NI + it + 1) * 128],
                                            XFA[:, it * T + gg * G:it * T + (gg + 1) * G],
                                            start=(it == 0), stop=(it == NI - 1))
                    last.then_inc(p.ST.h, 1)
                p.ST.reserve(1)
                p.ev("tensor", f0mm)
                stv = (p.ST, p.ST.v)
                p.wait("scalar", *stv)
                p.take("scalar", B["EVW1"], B["EVW2"])
                p.ev("scalar", lambda sc, ot=ot: sc.activation(
                    EVW1[:, 0:G], psW[:, 0:G], AF.Silu, bias=0.0, scale=SC1[:, ot:ot + 1]))
                p.ev("scalar", lambda sc, ot=ot: sc.mul(EVW2[:, 0:G], psW[:, G:2 * G], SC2[:, ot:ot + 1]))
                ssv = p.sig("scalar")
                B["psW0"].cons.append(ssv)
                B["psW1"].cons.append(ssv)
                p.wait("vector", *ssv)
                p.take("vector", B["EVW1"])
                p.ev("vector", lambda v: v.tensor_mul(EVW1[:, G:2 * G], EVW1[:, 0:G], EVW2[:, 0:G]))
                sv = p.sig("vector")
                B["EVW1"].rdy = sv
                B["EVW2"].cons.append(sv)
                p.wait("sync", *sv)
                so = p.dma("sync", p.SO, AP(g1d, ot * 128 * T + gg * G, [[T, 128], [1, G]]),
                           EVW1[:, G:2 * G])
                B["EVW1"].cons.append(so)
                ba.cons.append((p.ST, p.ST.v))
                B["XFA"].cons.append((p.ST, p.ST.v))

    def slot_f12(wname, src_d, dst_d, sc_col):
        for pan in range(2):
            p.take("sync", B["GPAN"])
            for qq in range(4):
                dv = p.dma("sync", p.SH,
                           AP(GPAN, qq * 8 * 2048, [[NIH * 2048, 128], [2048, 8], [1, 2048]]),
                           AP(src_d, qq * 8 * 128 * T + pan * 2048, [[T, 128], [128 * T, 8], [1, 2048]]))
            B["GPAN"].rdy = dv
            for ot in range(NIH):
                sa, ba = (STRA, B["STRA"]) if ot % 2 == 0 else (STRB, B["STRB"])
                p.take("gpsimd", ba)
                for qq in range(4):
                    dv = p.dma("gpsimd", p.SW,
                               AP(sa, qq * 8 * 128, [[NIH * 128, 128], [128, 8], [1, 128]]),
                               AP(ins[wname], qq * 8 * 128 * DI + ot * 128,
                                  [[DI, 128], [128 * DI, 8], [1, 128]]))
                ba.rdy = dv
                p.use("tensor", ba, B["GPAN"])
                for j in range(4):
                    p.take("tensor", B[f"psW{j}"])
                def fmm(t, sa=sa):
                    last = None
                    for it in range(NIH):
                        for j in range(4):
                            last = t.matmul(psW[:, j * G:(j + 1) * G],
                                            sa[:, it * 128:(it + 1) * 128],
                                            GPAN[:, it * 2048 + j * G:it * 2048 + (j + 1) * G],
                                            start=(it == 0), stop=(it == NIH - 1))
                    last.then_inc(p.ST.h, 1)
                p.ST.reserve(1)
                p.ev("tensor", fmm)
                stv = (p.ST, p.ST.v)
                ev, eb = (EVW1, B["EVW1"]) if ot % 2 == 0 else (EVW2, B["EVW2"])
                p.wait("scalar", *stv)
                p.take("scalar", eb)
                p.ev("scalar", lambda sc, ev=ev, ot=ot: sc.activation(
                    ev[:, :], psW[:, :], AF.Gelu, bias=0.0, scale=sc_col[:, ot:ot + 1]))
                ssv = p.sig("scalar")
                for j in range(4):
                    B[f"psW{j}"].cons.append(ssv)
                eb.rdy = ssv
                p.wait("sync", *ssv)
                so = p.dma("sync", p.SO, AP(dst_d, ot * 128 * T + pan * 2048, [[T, 128], [1, 2048]]),
                           ev[:, :])
                eb.cons.append(so)
                ba.cons.append((p.ST, p.ST.v))
                B["GPAN"].cons.append((p.ST, p.ST.v))

    def slot_f3():
        p.take("gpsimd", B["W3S"])
        for qq in range(4):
            dv = p.dma("gpsimd", p.SW, AP(W3S, qq * 8 * D, [[NIH * D, 128], [D, 8], [1, D]]),
                       AP(ins["w3s"], qq * 8 * 128 * D, [[D, 128], [128 * D, 8], [1, D]]))
        B["W3S"].rdy = dv
        for gg in range(NB * NG):
            b, c = gg // NG, gg % NG
            col0 = b * S + c * G
            p.take("sync", B["GT"])
            for qq in range(4):
                dv = p.dma("sync", p.SH,
                           AP(GT, qq * 8 * G, [[NIH * G, 128], [G, 8], [1, G]]),
                           AP(g3d, qq * 8 * 128 * T + col0, [[T, 128], [128 * T, 8], [1, G]]))
            B["GT"].rdy = dv
            for ot in range(NI):
                ps, pb = (psP, B["psP"]) if ot % 2 == 0 else (psB1, B["psB1"])
                p.use("tensor", B["GT"], B["W3S"])
                p.take("tensor", pb)
                def f3mm(t, ot=ot, ps=ps):
                    last = None
                    for it in range(NIH):
                        last = t.matmul(ps[:, :], W3S[:, it * D + ot * 128:it * D + (ot + 1) * 128],
                                        GT[:, it * G:(it + 1) * G], start=(it == 0), stop=(it == NIH - 1))
                    last.then_inc(p.ST.h, 1)
                p.ST.reserve(1)
                p.ev("tensor", f3mm)
                stv = (p.ST, p.ST.v)
                p.wait("scalar", *stv)
                p.take("scalar", B["AS"])
                p.ev("scalar", lambda sc, ot=ot, ps=ps: sc.mul(AS[:, :], ps[:, :], SC3[:, ot:ot + 1]))
                ssv = p.sig("scalar")
                pb.cons.append(ssv)
                B["AS"].rdy = ssv
                p.take("sync", B["XF3"])
                dv = p.dma("sync", p.SH, XF3[:, :], AP(x1T, ot * 128 * T + col0, [[T, 128], [1, G]]))
                B["XF3"].rdy = dv
                p.wait("vector", *ssv)
                p.wait("vector", *dv)
                p.take("vector", B["HG"]) if ot == 0 else None
                p.ev("vector", lambda v, ot=ot: v.tensor_add(
                    HG[:, ot * G:(ot + 1) * G], AS[:, :], XF3[:, :]))
                sv = p.sig("vector")
                B["AS"].cons.append(sv)
                B["XF3"].cons.append(sv)
            B["HG"].rdy = (p.SV, p.SV.v)
            B["GT"].cons.append((p.ST, p.ST.v))
            B["W3S"].cons.append((p.ST, p.ST.v))
            if debug_taps and gg == 0:
                dbg_dump(dbgZ2, buf=B["HG"], sb_ap=AP(HG, 0, [[NI * G, 128], [1, NI * G]]), cols=NI * G)
            ln_stats(HG, GT, B["GT"], C_EPS5, (ROWB, 0), (ROWB, 32))
            ln_normalize(HG, HG, True, (ROWB, 0), (ROWB, 32), src_buf=B["HG"])
            B["HG"].rdy = (p.SV, p.SV.v)
            if debug_taps and gg == 0:
                dbg_dump(dbgZ3, buf=B["HG"], sb_ap=AP(HG, 0, [[NI * G, 128], [1, NI * G]]), cols=NI * G)
            for tt in range(4):
                p.use("tensor", B["HG"])
                p.take("tensor", B["OSTG"], B["GT"])
                for it in range(NI):
                    ps, pb = (psB2, B["psB2"]) if it % 2 == 0 else (psP, B["psP"])
                    p.take("tensor", pb)
                    p.ST.reserve(1)
                    p.ev("tensor", lambda t, it=it, tt=tt, ps=ps: t.transpose(
                        ps[:, 0:128], HG[:, it * G + tt * 128:it * G + (tt + 1) * 128],
                        C_EYE[:, :]).then_inc(p.ST.h, 1))
                    stv = (p.ST, p.ST.v)
                    p.wait("scalar", *stv)
                    p.ev("scalar", lambda sc, it=it, ps=ps: sc.copy(
                        OSTG[:, it * 128:(it + 1) * 128], ps[:, 0:128]))
                    ssv = p.sig("scalar")
                    pb.cons.append(ssv)
                B["OSTG"].rdy = (p.SS, p.SS.v)
                p.use("sync", B["OSTG"])
                so = p.dma("sync", p.SO, AP(out, (b * S + c * G + tt * 128) * D, [[D, 128], [1, D]]),
                           OSTG[:, :])
                B["OSTG"].cons.append(so)
                B["HG"].cons.append((p.ST, p.ST.v))

    startup()
    for l in range(nl):
        layer_stage_load(l)
        cast_lnb(L1B)
        for b in range(NB):
            load_w_pair(l, ("wk", "wv"), fold=True)
            kv_pass(l, b)
            load_w_pair(l, ("wq", "wo"), fold=True)
            attn_pass(l, b)
        ffn_pass(l)
    if do_slot:
        slot_f0()
        slot_f12("h1s", g1d, g2d, SCH1)
        slot_f12("h2s", g2d, g3d, SCH2)
        slot_f3()
    p.wait("sync", p.SO, p.SO.v)
    p.wait("sync", p.SW, p.SW.v)

    with nc.Block() as block:
        p.flush(block)
    return nc


_cached = {}


def _get_program():
    if "nc" not in _cached:
        _cached["nc"] = build_program()
    return _cached["nc"]


def prep_common(inputs):
    def col(a, ncols):
        return np.ascontiguousarray(a.reshape(ncols, 128).T)

    common = {}
    for l in range(NL):
        common[f"wq{l}"] = np.ascontiguousarray(inputs["L0_attn_Wq"][l].T)
        common[f"wk{l}"] = np.ascontiguousarray(inputs["L0_attn_Wk"][l].T)
        common[f"wv{l}"] = np.ascontiguousarray(inputs["L0_attn_Wv"][l].T)
        common[f"wo{l}"] = np.ascontiguousarray(inputs["L0_attn_Wo"][l].T)
        common[f"w1{l}"] = np.ascontiguousarray(inputs["L0_attn_W1"][l].T)
        common[f"w2{l}"] = np.ascontiguousarray(inputs["L0_attn_W2"][l].T)
        common[f"bqc{l}"] = col(inputs["L0_attn_bq"][l], NI)
        common[f"bkc{l}"] = col(inputs["L0_attn_bk"][l], NI)
        common[f"boc{l}"] = col(inputs["L0_attn_bo"][l], NI)
        common[f"b2c{l}"] = col(inputs["L0_attn_b2"][l], NI)
        common[f"b1c{l}"] = col(inputs["L0_attn_b1"][l], NIH)
        common[f"uc{l}"] = col(inputs["L0_attn_u_bias"][l].reshape(-1), NI)
        common[f"l1wc{l}"] = col(inputs["L0_attn_ln1_w"][l], NI)
        common[f"l1bc{l}"] = col(inputs["L0_attn_ln1_b"][l], NI)
        common[f"l2wc{l}"] = col(inputs["L0_attn_ln2_w"][l], NI)
        common[f"l2bc{l}"] = col(inputs["L0_attn_ln2_b"][l], NI)
        common[f"bvr{l}"] = np.ascontiguousarray(inputs["L0_attn_bv"][l].reshape(1, D))
    common["w1s"] = np.ascontiguousarray(inputs["L0_ffn_w1_w"].T)
    common["w2s"] = np.ascontiguousarray(inputs["L0_ffn_w2_w"].T)
    common["h1s"] = np.ascontiguousarray(inputs["L0_ffn_h1_w"].T)
    common["h2s"] = np.ascontiguousarray(inputs["L0_ffn_h2_w"].T)
    common["w3s"] = np.ascontiguousarray(inputs["L0_ffn_w3_w"].T)
    common["s1c"] = col(inputs["L0_ffn_w1_s"], NIH)
    common["s2c"] = col(inputs["L0_ffn_w2_s"], NIH)
    common["sh1c"] = col(inputs["L0_ffn_h1_s"], NIH)
    common["sh2c"] = col(inputs["L0_ffn_h2_s"], NIH)
    common["s3c"] = col(inputs["L0_ffn_w3_s"], NI)
    return {k: np.ascontiguousarray(v.astype(np.float32)) for k, v in common.items()}


def kernel(**inputs):
    nc = _get_program()
    x = np.ascontiguousarray(inputs["L0_attn_x"])
    common = prep_common(inputs)
    in_maps = []
    for c in range(8):
        m = dict(common)
        m["xin"] = np.ascontiguousarray(x[2 * c:2 * c + 2]).astype(np.float32)
        in_maps.append(m)

    res = run_bass_kernel_spmd(nc, in_maps, core_ids=list(range(8)))
    return np.concatenate([r["out"] for r in res.results], axis=0)


if __name__ == "__main__":
    nc = build_program()
    print("instructions:", nc.n_instructions())


# version 06 (final)
# speedup vs baseline: 1.0052x; 1.0052x over previous
import sys
sys.path.insert(0, "/opt/trn_rl_repo")
import numpy as np
import concourse.bass as bass
import concourse.mybir as mybir
from concourse.bass_utils import run_bass_kernel_spmd

F32 = mybir.dt.float32
BF16 = mybir.dt.bfloat16
AF = mybir.ActivationFunctionType
ALU = mybir.AluOpType
AP = bass.AP

S = 2048
D = 1024
H = 16
DH = 64
DI = 4096
NL = 4
NB = 2
T = NB * S
G = 512
NG = S // G
NI = D // 128
NIH = DI // 128
KB = 1024
DBG_B = 0
DBG_G = 0
DBG_L = 0


class Sem:
    def __init__(self, nc, name):
        self.h = nc.alloc_semaphore(name=name)
        self.v = 0

    def reserve(self, k):
        self.v += k
        return self.v


class Buf:
    __slots__ = ("rdy", "cons", "rng")

    def __init__(self, rng=None):
        self.rdy = None
        self.cons = []
        self.rng = rng


class MK:
    ENGS = ("tensor", "scalar", "vector", "gpsimd", "sync")

    def __init__(self, nc):
        self.nc = nc
        self.q = {e: [] for e in self.ENGS}
        self.ST = Sem(nc, "sT")
        self.SS = Sem(nc, "sS")
        self.SV = Sem(nc, "sV")
        self.SW = Sem(nc, "sW")
        self.SH = Sem(nc, "sH")
        self.SO = Sem(nc, "sO")
        self.esem = {"tensor": self.ST, "scalar": self.SS, "vector": self.SV}
        self.overlaps = []
        self._waited = {e: {} for e in self.ENGS}
        self.n_drain_fallback = 0

    def ev(self, eng, fn, kind="op"):
        self.q[eng].append((fn, kind))

    def wait(self, eng, sem, v):
        assert v <= sem.v, f"wait beyond reserved: {v} > {sem.v}"
        if v <= 0:
            return
        w = self._waited[eng]
        if w.get(id(sem), -1) >= v:
            return
        w[id(sem)] = v
        self.ev(eng, lambda e, s=sem.h, vv=v: e.wait_ge(s, vv), kind="wait")

    def use(self, eng, *bufs):
        for b in bufs:
            if b.rdy is not None:
                self.wait(eng, *b.rdy)

    def take(self, eng, *bufs):
        for b in bufs:
            for sv in b.cons:
                self.wait(eng, *sv)
            b.cons = []
            if b.rng is not None:
                s0, e0 = b.rng
                for o in self.overlaps:
                    if o is b or o.rng is None:
                        continue
                    s1, e1 = o.rng
                    if s1 < e0 and s0 < e1:
                        if o.rdy is not None:
                            self.wait(eng, *o.rdy)
                            o.rdy = None
                        for sv in o.cons:
                            self.wait(eng, *sv)
                        o.cons = []

    def done_c(self, buf, sem, v):
        buf.cons.append((sem, v))

    def done_p(self, buf, sem, v):
        buf.rdy = (sem, v)
        buf.cons = []

    def sig(self, eng, k=1):
        assert eng in ("scalar", "vector"), eng
        sem = self.esem[eng]
        v = sem.reserve(k)
        q = self.q[eng]
        i = len(q) - 1
        while i >= 0 and q[i][1] == "wait":
            i -= 1
        if i >= 0 and q[i][1] == "op":
            f = q[i][0]
            def wrapped(e, f=f, s=sem.h, kk=k):
                op = f(e)
                op.then_inc(s, kk)
                return op
            q[i] = (wrapped, "sig")
        else:
            self.n_drain_fallback += 1
            self.ev(eng, lambda e, s=sem.h, kk=k: e.drain().then_inc(s, kk),
                    kind="sig")
        return sem, v

    def mminc(self, k=1):
        return self.ST, self.ST.reserve(k)

    def dma(self, eng, sem, out_ap, in_ap):
        v = sem.reserve(16)
        self.ev(eng, lambda e, o=out_ap, i=in_ap, s=sem.h: e.dma_start(out=o, in_=i).then_inc(s, 16),
                kind="sig")
        return sem, v

    def flush(self, block):
        for e in self.ENGS:
            fns = self.q[e]
            if fns:
                getattr(block, e)(lambda eng, fns=fns: [f(eng) for f, _ in fns])


def host_constants():
    w = 1.0 / 10000.0 ** (np.arange(0, DH, 2, dtype=np.float64) / DH)
    s = np.arange(S, dtype=np.float64)
    ang = s[None, :] * w[:, None]
    sin, cos = np.sin(ang), np.cos(ang)
    blk_sin = np.concatenate([sin, sin], 0)
    blk_cm = np.concatenate([cos, -cos], 0)
    blk_kt = np.concatenate([cos, sin], 0)
    SIN = np.concatenate([blk_sin, blk_sin], 0).astype(np.float32)
    CM = np.concatenate([blk_cm, blk_cm], 0).astype(np.float32)
    KT = np.concatenate([blk_kt, blk_kt], 0).astype(np.float32)
    TRI = (np.arange(128)[:, None] <= np.arange(128)[None, :]).astype(np.float32)
    EYE = np.eye(128, dtype=np.float32)
    ONEF = np.ones((128, 128), dtype=np.float32)
    return SIN, CM, KT, TRI, EYE, ONEF


def build_program(debug_taps=False, nl=NL, do_slot=True):
    nc = bass.Bass("TRN2", target_bir_lowering=False, debug=False, dynamic_dma_scratch_size=16384)

    xin = nc.dram_tensor("xin", [NB, S, D], F32, kind="ExternalInput")
    ins = {}
    for l in range(NL):
        for nm, shp in (("wq", [D, D]), ("wk", [D, D]), ("wv", [D, D]), ("wo", [D, D]),
                        ("w1", [D, DI]), ("w2", [DI, D]),
                        ("bqc", [128, NI]), ("bkc", [128, NI]), ("boc", [128, NI]),
                        ("b2c", [128, NI]), ("b1c", [128, NIH]), ("uc", [128, NI]),
                        ("l1wc", [128, NI]), ("l1bc", [128, NI]),
                        ("l2wc", [128, NI]), ("l2bc", [128, NI]),
                        ("bvr", [1, D])):
            ins[f"{nm}{l}"] = nc.dram_tensor(f"{nm}{l}", shp, F32, kind="ExternalInput")
    for nm, shp in (("w1s", [D, DI]), ("w2s", [D, DI]), ("h1s", [DI, DI]),
                    ("h2s", [DI, DI]), ("w3s", [DI, D]),
                    ("s1c", [128, NIH]), ("s2c", [128, NIH]), ("sh1c", [128, NIH]),
                    ("sh2c", [128, NIH]), ("s3c", [128, NI])):
        ins[nm] = nc.dram_tensor(nm, shp, F32, kind="ExternalInput")
    out = nc.dram_tensor("out", [NB, S, D], F32, kind="ExternalOutput")

    kind = "ExternalOutput" if debug_taps else "Internal"
    if debug_taps:
        dbgA = nc.dram_tensor("dbgA", [D, T], F32, kind="ExternalOutput")
        dbgKP = nc.dram_tensor("dbgKP", [128, NI * S], F32, kind="ExternalOutput")
        dbgVA = nc.dram_tensor("dbgVA", [128, 16 * H * 65], F32, kind="ExternalOutput")
        dbgQP = nc.dram_tensor("dbgQP", [128, NI * G], F32, kind="ExternalOutput")
        dbgQT = nc.dram_tensor("dbgQT", [128, NI * G], F32, kind="ExternalOutput")
        dbgCT = nc.dram_tensor("dbgCT", [128, NI * G], F32, kind="ExternalOutput")
        dbgXN = nc.dram_tensor("dbgXN", [128, NI * G], F32, kind="ExternalOutput")
        dbgQS1 = nc.dram_tensor("dbgQS1", [128, G], F32, kind="ExternalOutput")
        dbgQS2 = nc.dram_tensor("dbgQS2", [128, G], F32, kind="ExternalOutput")
        dbgSIN = nc.dram_tensor("dbgSIN", [128, G], F32, kind="ExternalOutput")
        dbgCM = nc.dram_tensor("dbgCM", [128, G], F32, kind="ExternalOutput")
        dbgBQ = nc.dram_tensor("dbgBQ", [128, NI], F32, kind="ExternalOutput")
        dbgBQU = nc.dram_tensor("dbgBQU", [128, NI], F32, kind="ExternalOutput")
        dbgUC = nc.dram_tensor("dbgUC", [128, NI], F32, kind="ExternalOutput")
        dbgWS2 = nc.dram_tensor("dbgWS2", [128, NI * D], F32, kind="ExternalOutput")
        dbgBQ2 = nc.dram_tensor("dbgBQ2", [128, NI], F32, kind="ExternalOutput")
        dbgBQU2 = nc.dram_tensor("dbgBQU2", [128, NI], F32, kind="ExternalOutput")
        dbgPSB = nc.dram_tensor("dbgPSB", [128, NI], F32, kind="ExternalOutput")
        dbgZ2 = nc.dram_tensor("dbgZ2", [128, NI * G], F32, kind="ExternalOutput")
        dbgZ3 = nc.dram_tensor("dbgZ3", [128, NI * G], F32, kind="ExternalOutput")
    hT = nc.dram_tensor("hT", [D, T], F32, kind=kind)
    xT = nc.dram_tensor("xT", [D, T], F32, kind=kind)
    x1T = nc.dram_tensor("x1T", [D, T], F32, kind=kind)
    g1d = nc.dram_tensor("g1d", [DI, T], BF16, kind=kind)
    g2d = nc.dram_tensor("g2d", [DI, T], BF16, kind=kind)
    g3d = nc.dram_tensor("g3d", [DI, T], BF16, kind=kind)
    h1b = nc.dram_tensor("h1b", [DI, DI], BF16, kind="Internal")
    h2b = nc.dram_tensor("h2b", [DI, DI], BF16, kind="Internal")

    SINc, CMc, KTc, TRIc, EYEc, ONEFc = host_constants()
    d_sin = nc.inline_tensor(SINc, name="c_sin")
    d_cm = nc.inline_tensor(CMc, name="c_cm")
    d_kt = nc.inline_tensor(KTc, name="c_kt")
    d_tri = nc.inline_tensor(TRIc, name="c_tri")
    d_eye = nc.inline_tensor(EYEc, name="c_eye")
    d_onef = nc.inline_tensor(ONEFc, name="c_onef")
    d_onecol = nc.inline_tensor(np.ones((128, 1), np.float32), name="c_onecol")
    d_eps5 = nc.inline_tensor(np.full((128, 1), 1e-5, np.float32), name="c_eps5")
    d_eps6 = nc.inline_tensor(np.full((128, 1), 1e-6, np.float32), name="c_eps6")

    A = nc.alloc_sbuf_tensor
    C_SIN = A("C_SIN", [128, S], BF16)
    C_CM = A("C_CM", [128, S], BF16)
    C_KT = A("C_KT", [128, S], BF16)
    C_TRI = A("C_TRI", [128, 128], BF16)
    C_EYE = A("C_EYE", [128, 128], F32)
    C_ONEF = A("C_ONEF", [128, 128], F32)
    C_ONEB = A("C_ONEB", [128, 128], BF16)
    C_ONECOL = A("C_ONECOL", [128, 1], BF16)
    C_EPS5 = A("C_EPS5", [128, 1], F32)
    C_EPS6 = A("C_EPS6", [128, 1], F32)
    BQ = A("BQ", [128, NI], F32)
    BK = A("BK", [128, NI], F32)
    BQU = A("BQU", [128, NI], F32)
    B1E = A("B1E", [128, NIH], F32)
    BQR = A("BQR", [128, NI], F32)
    BKR = A("BKR", [128, NI], F32)
    BOC = A("BOC", [128, NI], F32)
    B2C = A("B2C", [128, NI], F32)
    B1R = A("B1R", [128, NIH], F32)
    UC = A("UC", [128, NI], F32)
    L1W = A("L1W", [128, NI], F32)
    L1B = A("L1B", [128, NI], F32)
    L2W = A("L2W", [128, NI], F32)
    L2B = A("L2B", [128, NI], F32)
    LBB = A("LBB", [128, NI], BF16)
    BVT = A("BVT", [128, 2 * G], BF16)
    SC1 = A("SC1", [128, NIH], F32)
    SC2 = A("SC2", [128, NIH], F32)
    SCH1 = A("SCH1", [128, NIH], F32)
    SCH2 = A("SCH2", [128, NIH], F32)
    SC3 = A("SC3", [128, NI], F32)
    ROWA = A("ROWA", [128, G], F32)
    ROWB = A("ROWB", [128, G], F32)

    base = nc.sbuf_base

    def at(name, off_kb, shape, dtype):
        return nc.alloc_sbuf_tensor_at(name, shape, dtype, offset=base + int(off_kb * KB))

    KP = at("KP", 0, [128, NI * S], BF16)
    VA = at("VA", 32, [128, (S // 128) * H * 65], BF16)
    WS1 = at("WS1", 65, [128, NI * D], BF16)
    WS2 = at("WS2", 81, [128, NI * D], BF16)
    QP = at("QP", 97, [128, NI * G], BF16)
    QT = at("QT", 105, [128, NI * G], BF16)
    PM = at("PM", 113, [128, 4 * G], BF16)
    PD = at("PD", 117, [128, 3 * G], BF16)
    QS1 = at("QS1", 121, [128, G], BF16)
    QS2 = at("QS2", 122, [128, G], BF16)
    BCS = at("BCS", 123, [64, G], F32)
    BCS2 = at("BCS2", 121, [64, G], F32)
    RCB = at("RCB", 125, [33, G], F32)
    W1S = at("W1S", 0, [128, NI * DI], BF16)
    W2S = at("W2S", 64, [128, NIH * D], BF16)
    GPAN = at("GPAN", 0, [128, NIH * 2048], BF16)
    XFA = at("XFA", 0, [128, NI * T], BF16)
    W3S = at("W3S", 48, [128, NIH * D], BF16)
    HG = at("HG", 128, [128, NI * G], F32)
    XN = at("XN", 144, [128, NI * G], BF16)
    CTXN = at("CTXN", 152, [128, NI * G], BF16)
    GT = at("GT", 152, [128, NIH * G], BF16)
    AS = at("AS", 184, [128, G], F32)
    XSC = at("XSC", 164, [128, G], F32)
    OSTG = at("OSTG", 160, [128, D], F32)
    XF3 = at("XF3", 144, [128, G], F32)
    STRA = at("STRA", 128, [128, NIH * 128], BF16)
    STRB = at("STRB", 136, [128, NIH * 128], BF16)
    EVW1 = at("EVW1", 144, [128, 2048], BF16)
    EVW2 = at("EVW2", 148, [128, 2048], BF16)
    arena_end = base + 186 * KB
    assert arena_end <= nc.sbuf_top, f"SBUF over: {arena_end} > {nc.sbuf_top}"

    psW = nc.alloc_psum_tensor("psW", [128, 4 * G], F32)
    psP = nc.alloc_psum_tensor("psP", [128, G], F32)
    psC = nc.alloc_psum_tensor("psC", [65, G], F32)
    psB1 = nc.alloc_psum_tensor("psB1", [128, G], F32)
    psB2 = nc.alloc_psum_tensor("psB2", [128, G], F32)

    p = MK(nc)

    B = {nm: Buf() for nm in (
        "KP", "VA", "WS1", "WS2", "QP", "QT", "PM", "PD", "QS1", "QS2", "BCS",
        "BCS2", "RCB", "rc0", "rc1",
        "W1S", "W2S", "GPAN", "XFA", "W3S", "HG", "XN", "CTXN", "GT", "AS",
        "XSC", "OSTG", "STRA", "STRB", "EVW1", "EVW2", "XF3",
        "psW0", "psW1", "psW2", "psW3", "psP", "psC", "psB1", "psB2",
        "rows", "cache", "stage", "BVT", "LBB", "hT", "xT", "x1T", "g1", "g2", "g3")}

    _ranges = {
        "KP": (0, 32), "VA": (32, 64.5), "WS1": (65, 81), "WS2": (81, 97),
        "QP": (97, 105), "QT": (105, 113), "PM": (113, 117), "PD": (117, 120),
        "QS1": (121, 122), "QS2": (122, 123), "BCS": (123, 125), "BCS2": (121, 123),
        "RCB": (125, 127),
        "W1S": (0, 64), "W2S": (64, 128), "GPAN": (0, 128), "XFA": (0, 64),
        "STRA": (128, 136), "STRB": (136, 144), "W3S": (48, 112),
        "HG": (128, 144), "XN": (144, 152), "CTXN": (152, 160), "GT": (152, 184),
        "OSTG": (160, 164), "XSC": (164, 166), "AS": (184, 186), "XF3": (144, 146),
        "EVW1": (144, 148), "EVW2": (148, 152),
    }
    for _nm, _rng in _ranges.items():
        B[_nm].rng = _rng
    p.overlaps = [B[_nm] for _nm in _ranges]

    ROT5 = [(psP[:, :], B["psP"]), (psW[:, 0:G], B["psW0"]), (psW[:, G:2 * G], B["psW1"]),
            (psW[:, 2 * G:3 * G], B["psW2"]), (psW[:, 3 * G:4 * G], B["psW3"])]

    def str_in(dram, c0, n, nt=NI):
        return AP(dram, c0, [[T, 128], [128 * T, nt], [1, n]])

    def sb3(sb, n, nt=NI, off=0):
        return AP(sb, off, [[nt * n, 128], [n, nt], [1, n]])

    def dbg_dump(dram, sb_ap=None, cols=None, buf=None):
        p.wait("gpsimd", p.SS, p.SS.v)
        p.wait("gpsimd", p.SV, p.SV.v)
        p.wait("gpsimd", p.ST, p.ST.v)
        dv = p.dma("gpsimd", p.SW, AP(dram, 0, [[cols, 128], [1, cols]]), sb_ap)
        for e in ("tensor", "scalar", "vector", "sync"):
            p.wait(e, p.SW, dv[1])
        if buf is not None:
            buf.cons.append(dv)

    def startup():
        for dst, src in ((C_SIN, d_sin), (C_CM, d_cm), (C_KT, d_kt), (C_TRI, d_tri)):
            p.dma("gpsimd", p.SW, dst[:, :], src.ap())
        p.dma("gpsimd", p.SW, C_ONEB[:, :], d_onef.ap())
        p.dma("gpsimd", p.SW, C_ONECOL[:, :], d_onecol.ap())
        for dst, src in ((C_EYE, d_eye), (C_ONEF, d_onef), (C_EPS5, d_eps5), (C_EPS6, d_eps6)):
            p.dma("sync", p.SH, dst[:, :], src.ap())
        p.wait("vector", p.SW, p.SW.v)
        p.wait("vector", p.SH, p.SH.v)
        p.ev("vector", lambda v: v.memset(AP(VA, 64, [[16 * 1040, 128], [1040, 16], [65, 16]]), 1.0))
        def zpd(v):
            last = None
            for j in range(1, 4):
                last = v.memset(AP(PD, (j - 1) * G, [[3 * G, 128], [1, 128 * j]]), 0.0)
            return last
        p.ev("vector", zpd)
        sv = p.sig("vector")
        B["VA"].rdy = sv
        B["PD"].rdy = sv

    def layer_stage_load(l):
        for dst, nm in ((BQR, "bqc"), (BKR, "bkc"), (BOC, "boc"), (B2C, "b2c"),
                        (B1R, "b1c"), (UC, "uc"), (L1W, "l1wc"), (L1B, "l1bc"),
                        (L2W, "l2wc"), (L2B, "l2bc")):
            p.take("sync", B["stage"])
            p.dma("sync", p.SH, dst[:, :], ins[f"{nm}{l}"].ap())
        p.take("sync", B["rows"])
        p.dma("sync", p.SH, ROWA[0:1, :], ins[f"bvr{l}"].ap()[0:1, 0:G])
        sv = p.dma("sync", p.SH, ROWB[0:1, :], ins[f"bvr{l}"].ap()[0:1, G:2 * G])
        B["rows"].rdy = sv
        B["stage"].rdy = sv

    def slot_stage_load():
        for dst, nm in ((SC1, "s1c"), (SC2, "s2c"), (SCH1, "sh1c"), (SCH2, "sh2c"),
                        (SC3, "s3c")):
            p.dma("sync", p.SH, dst[:, :], ins[nm].ap())
        B["stage"].rdy = (p.SH, p.SH.v)

    def ln_stats(src_f32_sb, sq_sb, sq_buf, eps_tile, rstd_dst, mr_dst, nt=NI):
        ndim = nt * 128
        p.use("scalar", B["HG"])
        p.take("scalar", B["XN"], sq_buf)
        def casts(sc):
            last = None
            for it in range(nt):
                sc.copy(XN[:, it * G:(it + 1) * G], src_f32_sb[:, it * G:(it + 1) * G])
            for it in range(nt):
                last = sc.activation(sq_sb[:, it * G:(it + 1) * G], src_f32_sb[:, it * G:(it + 1) * G], AF.Square)
            return last
        p.ev("scalar", casts)
        ss = p.sig("scalar")
        B["XN"].rdy = ss
        sq_buf.rdy = ss
        B["HG"].cons.append(ss)
        p.use("tensor", B["XN"], sq_buf)
        p.take("tensor", B["psB1"], B["psB2"])
        def stats(t):
            last = None
            for it in range(nt):
                last = t.matmul(psB1[0:1, :], C_ONECOL[:, :], XN[:, it * G:(it + 1) * G],
                                start=(it == 0), stop=(it == nt - 1))
            for it in range(nt):
                last = t.matmul(psB2[0:1, :], C_ONECOL[:, :], sq_sb[:, it * G:(it + 1) * G],
                                start=(it == 0), stop=(it == nt - 1))
            last.then_inc(p.ST.h, 1)
        p.ST.reserve(1)
        p.ev("tensor", stats)
        stv = (p.ST, p.ST.v)
        B["XN"].cons.append(stv)
        sq_buf.cons.append(stv)
        p.wait("scalar", *stv)
        p.take("scalar", B["rows"])
        inv = 1.0 / ndim
        p.ev("scalar", lambda sc: sc.mul(ROWA[0:1, :], psB1[0:1, :], inv))
        p.ev("scalar", lambda sc: sc.mul(ROWA[32:33, :], psB2[0:1, :], inv))
        p.ev("scalar", lambda sc: sc.activation(ROWB[32:33, :], ROWA[0:1, :], AF.Square))
        ss = p.sig("scalar")
        p.wait("vector", *ss)
        p.ev("vector", lambda v: v.tensor_sub(ROWA[64:65, :], ROWA[32:33, :], ROWB[32:33, :]))
        sv = p.sig("vector")
        p.wait("scalar", *sv)
        p.ev("scalar", lambda sc: sc.activation(ROWB[64:65, :], ROWA[64:65, :], AF.Ln,
                                                bias=eps_tile[64:65, 0:1]))
        p.ev("scalar", lambda sc: sc.activation(ROWB[0:1, :], ROWB[64:65, :], AF.Exp, bias=0.0, scale=-0.5))
        ss = p.sig("scalar")
        p.wait("vector", *ss)
        p.ev("vector", lambda v: v.tensor_mul(ROWB[32:33, :], ROWA[0:1, :], ROWB[0:1, :]))
        sv = p.sig("vector")
        rt, rb = rstd_dst
        mt, mb = mr_dst
        p.wait("scalar", *sv)
        p.take("scalar", B["cache"])
        p.ev("scalar", lambda sc: sc.copy(rt[rb:rb + 1, :], ROWB[0:1, :]))
        p.ev("scalar", lambda sc: sc.copy(mt[mb:mb + 1, :], ROWB[32:33, :]))
        ss = p.sig("scalar")
        B["rows"].rdy = ss
        B["cache"].rdy = ss
        B["psB1"].cons.append(ss)
        B["psB2"].cons.append(ss)
        return ss

    def ln_normalize(src_f32_sb, dst_sb, dst_dtype_is_f32, rstd_src, mr_src, nt=NI,
                     dst_buf=None, src_buf=None):
        rt, rb = rstd_src
        mt, mb = mr_src
        p.use("tensor", B["cache"])
        p.take("tensor", B["psB1"], B["psB2"])
        def bcast(t):
            t.matmul(psB1[:, :], C_ONEF[rb:rb + 1, :], rt[rb:rb + 1, :], start=True, stop=True)
            t.matmul(psB2[:, :], C_ONEF[mb:mb + 1, :], mt[mb:mb + 1, :],
                     start=True, stop=True).then_inc(p.ST.h, 1)
        p.ST.reserve(1)
        p.ev("tensor", bcast)
        stv = (p.ST, p.ST.v)
        B["cache"].cons.append(stv)
        p.wait("vector", *stv)
        if src_buf is not None:
            p.use("vector", src_buf)
        p.take("vector", B["AS"])
        if dst_buf is not None:
            p.take("vector", dst_buf)
        def norm(v):
            last = None
            for it in range(nt):
                v.tensor_mul(AS[:, :], src_f32_sb[:, it * G:(it + 1) * G], psB1[:, :])
                last = v.tensor_sub(dst_sb[:, it * G:(it + 1) * G], AS[:, :], psB2[:, :])
            return last
        p.ev("vector", norm)
        sv = p.sig("vector")
        if dst_buf is not None:
            dst_buf.rdy = sv
        if src_buf is not None:
            src_buf.cons.append(sv)
        B["psB1"].cons.append(sv)
        B["psB2"].cons.append(sv)
        B["cache"].cons.append(sv)
        B["AS"].rdy = sv
        B["AS"].cons.append(sv)
        return sv


    def cast_lnb(col_src):
        p.use("scalar", B["stage"])
        p.take("scalar", B["LBB"])
        p.ev("scalar", lambda sc, cs=col_src: sc.copy(LBB[:, :], cs[:, :]))
        B["LBB"].rdy = p.sig("scalar")

    def load_w_pair(l, names, fold):
        slots = {names[0]: WS1, names[1]: WS2}
        bufs = {names[0]: B["WS1"], names[1]: B["WS2"]}
        for nm in names:
            sl, bb = slots[nm], bufs[nm]
            p.take("gpsimd", bb)
            dv = p.dma("gpsimd", p.SW, sb3(sl, D), AP(ins[f"{nm}{l}"], 0, [[D, 128], [128 * D, NI], [1, D]]))
            bb.rdy = dv
        for nm in names:
            sl, bb = slots[nm], bufs[nm]
            if nm in ("wq", "wk"):
                dstc, rawc = (BQ, BQR) if nm == "wq" else (BK, BKR)
                p.use("tensor", bb, B["LBB"])
                p.take("tensor", B["psB2"])
                def bias_mm(t, sl=sl):
                    last = None
                    for ot in range(NI):
                        for it in range(NI):
                            last = t.matmul(psB2[:, ot:ot + 1],
                                            sl[:, it * D + ot * 128:it * D + (ot + 1) * 128],
                                            LBB[:, it:it + 1], start=(it == 0), stop=(it == NI - 1))
                    last.then_inc(p.ST.h, 1)
                p.ST.reserve(1)
                p.ev("tensor", bias_mm)
                stv = (p.ST, p.ST.v)
                p.wait("vector", *stv)
                p.use("vector", B["stage"])
                p.ev("vector", lambda v, dstc=dstc, rawc=rawc: v.tensor_add(dstc[:, :], psB2[:, 0:NI], rawc[:, :]))
                if nm == "wq":
                    p.ev("vector", lambda v: v.engine_nop())
                    p.ev("vector", lambda v: v.tensor_add(BQU[:, :], BQ[:, :], UC[:, :]))
                sv = p.sig("vector")
                B["psB2"].cons.append(sv)
                B["stage"].cons.append(sv)
            elif nm == "wv":
                p.take("sync", B["rows"])
                p.dma("sync", p.SH, ROWA[0:1, :], ins[f"bvr{l}"].ap()[0:1, 0:G])
                dvb = p.dma("sync", p.SH, ROWB[0:1, :], ins[f"bvr{l}"].ap()[0:1, G:2 * G])
                B["rows"].rdy = dvb
                p.use("tensor", bb, B["LBB"])
                p.take("tensor", B["psB1"], B["psB2"])
                def bias_v(t, sl=sl):
                    last = None
                    for half, ps in ((0, psB2), (1, psB1)):
                        for it in range(NI):
                            last = t.matmul(ps[0:1, 0:G], LBB[:, it:it + 1],
                                            sl[:, it * D + half * G:it * D + half * G + G],
                                            start=(it == 0), stop=(it == NI - 1))
                    last.then_inc(p.ST.h, 1)
                p.ST.reserve(1)
                p.ev("tensor", bias_v)
                stv = (p.ST, p.ST.v)
                p.wait("vector", *stv)
                p.use("vector", B["stage"], B["rows"])
                p.wait("vector", *dvb)
                p.ev("vector", lambda v: v.tensor_add(ROWA[32:33, :], psB2[0:1, 0:G], ROWA[0:1, :]))
                p.ev("vector", lambda v: v.tensor_add(ROWB[32:33, :], psB1[0:1, 0:G], ROWB[0:1, :]))
                sv = p.sig("vector")
                B["psB1"].cons.append(sv)
                B["psB2"].cons.append(sv)
                p.wait("scalar", *sv)
                p.take("scalar", B["BVT"])
                p.ev("scalar", lambda sc: sc.copy(BVT[0:1, 0:G], ROWA[32:33, :]))
                p.ev("scalar", lambda sc: sc.copy(BVT[0:1, G:2 * G], ROWB[32:33, :]))
                ssv = p.sig("scalar")
                B["BVT"].rdy = ssv
                B["rows"].rdy = ssv
                p.wait("tensor", *ssv)
                p.take("tensor", B["psB1"], B["psB2"])
                p.ST.reserve(1)
                def bvbc(t):
                    t.matmul(psB1[:, :], C_ONEB[0:1, :], BVT[0:1, 0:G], start=True, stop=True)
                    t.matmul(psB2[:, :], C_ONEB[0:1, :], BVT[0:1, G:2 * G],
                             start=True, stop=True).then_inc(p.ST.h, 1)
                p.ev("tensor", bvbc)
                stv2 = (p.ST, p.ST.v)
                p.wait("scalar", *stv2)
                p.take("scalar", B["BVT"])
                p.ev("scalar", lambda sc: sc.copy(BVT[:, 0:G], psB1[:, :]))
                p.ev("scalar", lambda sc: sc.copy(BVT[:, G:2 * G], psB2[:, :]))
                sbv = p.sig("scalar")
                B["BVT"].rdy = sbv
                B["psB1"].cons.append(sbv)
                B["psB2"].cons.append(sbv)
            if fold and nm in ("wq", "wk", "wv"):
                p.use("vector", bb, B["stage"])
                if nm in ("wq", "wk"):
                    pass
                p.wait("vector", p.ST, p.ST.v)
                def scw(v, sl=sl):
                    last = None
                    for it in range(NI):
                        last = v.tensor_scalar_mul(sl[:, it * D:(it + 1) * D],
                                                   sl[:, it * D:(it + 1) * D], L1W[:, it:it + 1])
                    return last
                p.ev("vector", scw)
                bb.rdy = p.sig("vector")
                B["stage"].cons.append(bb.rdy)

    def load_w_ffn(l):
        p.take("gpsimd", B["W1S"])
        d1 = p.dma("gpsimd", p.SW, sb3(W1S, DI), AP(ins[f"w1{l}"], 0, [[DI, 128], [128 * DI, NI], [1, DI]]))
        B["W1S"].rdy = d1
        p.take("gpsimd", B["W2S"])
        d2 = p.dma("gpsimd", p.SW, sb3(W2S, D, nt=NIH), AP(ins[f"w2{l}"], 0, [[D, 128], [128 * D, NIH], [1, D]]))
        B["W2S"].rdy = d2
        cast_lnb(L2B)
        p.use("tensor", B["W1S"], B["LBB"])
        p.take("tensor", B["psB2"])
        def bias_mm(t):
            last = None
            for ot in range(NIH):
                for it in range(NI):
                    last = t.matmul(psB2[:, ot:ot + 1],
                                    W1S[:, it * DI + ot * 128:it * DI + (ot + 1) * 128],
                                    LBB[:, it:it + 1], start=(it == 0), stop=(it == NI - 1))
            last.then_inc(p.ST.h, 1)
        p.ST.reserve(1)
        p.ev("tensor", bias_mm)
        stv = (p.ST, p.ST.v)
        p.wait("vector", *stv)
        p.use("vector", B["stage"])
        p.ev("vector", lambda v: v.tensor_add(B1E[:, :], psB2[:, 0:NIH], B1R[:, :]))
        def scw(v):
            last = None
            for it in range(NI):
                last = v.tensor_scalar_mul(W1S[:, it * DI:(it + 1) * DI],
                                           W1S[:, it * DI:(it + 1) * DI], L2W[:, it:it + 1])
            return last
        p.ev("vector", scw)
        sv = p.sig("vector")
        B["psB2"].cons.append(sv)
        B["W1S"].rdy = sv

    def kv_pass(l, b):
        src = xT if l == 0 else hT
        p.take("vector", B["VA"])
        p.ev("vector", lambda v: v.memset(AP(VA, 64, [[16 * 1040, 128], [1040, 16], [65, 16]]), 1.0))
        B["VA"].rdy = p.sig("vector")
        if debug_taps and l == DBG_L and b == DBG_B:
            p.use("gpsimd", B["WS2"])
            dbg_dump(dbgWS2, buf=B["WS2"], sb_ap=AP(WS2, 0, [[NI * D, 128], [1, NI * D]]), cols=NI * D)
        for c in range(NG):
            col0 = b * S + c * G
            p.take("sync", B["HG"])
            if l == 0:
                for tt in range(4):
                    for half in range(2):
                        p.take("sync", B["XSC"])
                        dv = p.dma("sync", p.SH, XSC[:, :],
                                   AP(xin, (b * S + c * G + tt * 128) * D + half * G,
                                      [[D, 128], [1, G]]))
                        B["XSC"].rdy = dv
                        p.use("tensor", B["XSC"])
                        for sub in range(4):
                            it = half * 4 + sub
                            ps, pb = (psP, B["psP"]) if (it % 2 == 0) else (psB1, B["psB1"])
                            p.take("tensor", pb)
                            p.ST.reserve(1)
                            p.ev("tensor", lambda t, sub=sub, ps=ps: t.transpose(
                                ps[:, 0:128], XSC[:, sub * 128:(sub + 1) * 128],
                                C_EYE[:, :]).then_inc(p.ST.h, 1))
                            stv = (p.ST, p.ST.v)
                            p.wait("scalar", *stv)
                            p.ev("scalar", lambda sc, it=it, tt=tt, ps=ps: sc.copy(
                                HG[:, it * G + tt * 128:it * G + (tt + 1) * 128], ps[:, 0:128]))
                            ssv = p.sig("scalar")
                            pb.cons.append(ssv)
                            B["XSC"].cons.append(ssv)
                B["HG"].rdy = (p.SS, p.SS.v)
                p.use("sync", B["HG"])
                sv = p.dma("sync", p.SO, str_in(xT, col0, G), sb3(HG, G))
                B["HG"].cons.append(sv)
            else:
                dv = p.dma("sync", p.SH, sb3(HG, G), str_in(src, col0, G))
                B["HG"].rdy = dv
            ln_stats(HG, QP, B["QP"], C_EPS6, (ROWB, 0), (ROWB, 32))
            ln_normalize(HG, XN, False, (ROWB, 0), (ROWB, 32), dst_buf=B["XN"], src_buf=B["HG"])
            p.use("tensor", B["XN"], B["WS1"])
            rot = 0
            for ot in range(NI):
                pa, pb = ROT5[rot % 5]
                rot += 1
                p.take("tensor", pb)
                def kmm(t, ot=ot, pa=pa):
                    last = None
                    for it in range(NI):
                        last = t.matmul(pa, WS1[:, it * D + ot * 128:it * D + (ot + 1) * 128],
                                        XN[:, it * G:(it + 1) * G], start=(it == 0), stop=(it == NI - 1))
                    last.then_inc(p.ST.h, 1)
                p.ST.reserve(1)
                p.ev("tensor", kmm)
                stv = (p.ST, p.ST.v)
                p.wait("scalar", *stv)
                p.take("scalar", B["KP"]) if (c == 0 and ot == 0) else None
                p.ev("scalar", lambda sc, ot=ot, pa=pa, c=c: sc.add(
                    KP[:, ot * S + c * G:ot * S + (c + 1) * G], pa, BK[:, ot:ot + 1]))
                ssv = p.sig("scalar")
                pb.cons.append(ssv)
            B["KP"].rdy = (p.SS, p.SS.v)
            B["XN"].cons.append((p.ST, p.ST.v))
            p.use("tensor", B["XN"], B["WS2"])
            for tt in range(4):
                for half in range(2):
                    pa, pb = ROT5[rot % 5]
                    rot += 1
                    p.take("tensor", pb)
                    def vmm(t, tt=tt, half=half, pa=pa):
                        last = None
                        for it in range(NI):
                            last = t.matmul(pa, XN[:, it * G + tt * 128:it * G + (tt + 1) * 128],
                                     WS2[:, it * D + half * G:it * D + half * G + G],
                                     start=(it == 0), stop=(it == NI - 1))
                        last.then_inc(p.ST.h, 1)
                    p.ST.reserve(1)
                    p.ev("tensor", vmm)
                    stv = (p.ST, p.ST.v)
                    p.wait("vector", *stv)
                    p.use("vector", B["BVT"])
                    ttk = c * 4 + tt
                    p.ev("vector", lambda v, ttk=ttk, half=half, pa=pa: v.tensor_add(
                        AP(VA, ttk * 1040 + half * 8 * 65, [[16 * 1040, 128], [65, 8], [1, 64]]),
                        pa, BVT[:, half * G:(half + 1) * G]))
                    ssv = p.sig("vector")
                    pb.cons.append(ssv)
            B["VA"].rdy = (p.SV, p.SV.v)
            B["XN"].cons.append((p.ST, p.ST.v))
        B["WS1"].cons.append((p.ST, p.ST.v))
        B["WS2"].cons.append((p.ST, p.ST.v))
        B["BVT"].cons.append((p.SV, p.SV.v))
        if debug_taps and l == DBG_L and b == DBG_B:
            dbg_dump(dbgKP, buf=B["KP"], sb_ap=AP(KP, 0, [[NI * S, 128], [1, NI * S]]), cols=NI * S)
            dbg_dump(dbgVA, buf=B["VA"], sb_ap=AP(VA, 0, [[16 * H * 65, 128], [1, 16 * H * 65]]), cols=16 * H * 65)

    def attn_pass(l, b):
        src = xT if l == 0 else hT
        p.take("vector", B["PD"])
        def zpd2(v):
            last = None
            for j in range(1, 4):
                last = v.memset(AP(PD, (j - 1) * G, [[3 * G, 128], [1, 128 * j]]), 0.0)
            return last
        p.ev("vector", zpd2)
        B["PD"].rdy = p.sig("vector")
        if debug_taps and l == DBG_L and b == DBG_B:
            p.wait("scalar", p.SV, p.SV.v)
            p.take("scalar", B["AS"])
            p.ev("scalar", lambda sc: sc.copy(AS[:, 0:NI], psB2[:, 0:NI]))
            sps = p.sig("scalar")
            B["psB2"].cons.append(sps)
            B["AS"].rdy = sps
            dbg_dump(dbgBQ2, sb_ap=AP(BQ, 0, [[NI, 128], [1, NI]]), cols=NI)
            dbg_dump(dbgBQU2, sb_ap=AP(BQU, 0, [[NI, 128], [1, NI]]), cols=NI)
            dbg_dump(dbgPSB, buf=B["AS"], sb_ap=AP(AS, 0, [[G, 128], [1, NI]]), cols=NI)
        for g in range(NG):
            col0 = b * S + g * G
            p.take("sync", B["HG"])
            dv = p.dma("sync", p.SH, sb3(HG, G), str_in(src, col0, G))
            B["HG"].rdy = dv
            ln_stats(HG, QP, B["QP"], C_EPS6, (ROWB, 0), (ROWB, 32))
            ln_normalize(HG, XN, False, (ROWB, 0), (ROWB, 32), dst_buf=B["XN"], src_buf=B["HG"])
            p.use("tensor", B["XN"], B["WS1"])
            rot = 0
            for ot in range(NI):
                pa, pb = ROT5[rot % 5]
                rot += 1
                p.take("tensor", pb)
                def qmm(t, ot=ot, pa=pa):
                    last = None
                    for it in range(NI):
                        last = t.matmul(pa, WS1[:, it * D + ot * 128:it * D + (ot + 1) * 128],
                                        XN[:, it * G:(it + 1) * G], start=(it == 0), stop=(it == NI - 1))
                    last.then_inc(p.ST.h, 1)
                p.ST.reserve(1)
                p.ev("tensor", qmm)
                stv = (p.ST, p.ST.v)
                p.wait("scalar", *stv)
                p.take("scalar", B["QS1"])
                p.ev("scalar", lambda sc, ot=ot, pa=pa: sc.add(
                    QP[:, ot * G:(ot + 1) * G], pa, BQ[:, ot:ot + 1]))
                p.ev("scalar", lambda sc, ot=ot, pa=pa: sc.add(
                    QS1[:, :], pa, BQU[:, ot:ot + 1]))
                ssv = p.sig("scalar")
                pb.cons.append(ssv)
                p.wait("vector", *ssv)
                def qtl(v, ot=ot, g=g):
                    v.tensor_copy(QS2[0:32, :], QS1[32:64, :])
                    v.tensor_copy(QS2[32:64, :], QS1[0:32, :])
                    v.tensor_copy(QS2[64:96, :], QS1[96:128, :])
                    v.tensor_copy(QS2[96:128, :], QS1[64:96, :])
                    v.drain()
                    v.tensor_mul(AS[:, :], QS1[:, :], C_SIN[:, g * G:(g + 1) * G])
                    v.tensor_mul(XSC[:, :], QS2[:, :], C_CM[:, g * G:(g + 1) * G])
                    v.drain()
                    return v.tensor_add(QT[:, ot * G:(ot + 1) * G], AS[:, :], XSC[:, :])
                p.take("vector", B["AS"], B["XSC"])
                if ot == 0:
                    p.take("vector", B["GT"])
                p.ev("vector", qtl)
                svv = p.sig("vector")
                B["QS1"].cons.append(svv)
                B["AS"].rdy = svv
                B["AS"].cons.append(svv)
                B["XSC"].rdy = svv
                B["XSC"].cons.append(svv)
                B["GT"].cons.append(svv)
            svq = (p.SV, p.SV.v)
            B["QP"].rdy = (p.SS, p.SS.v)
            B["QT"].rdy = svq
            B["XN"].cons.append((p.ST, p.ST.v))
            if debug_taps and l == DBG_L and b == DBG_B and g == DBG_G:
                dbg_dump(dbgQP, buf=B["QP"], sb_ap=AP(QP, 0, [[NI * G, 128], [1, NI * G]]), cols=NI * G)
                dbg_dump(dbgQT, buf=B["QT"], sb_ap=AP(QT, 0, [[NI * G, 128], [1, NI * G]]), cols=NI * G)
                dbg_dump(dbgXN, buf=B["XN"], sb_ap=AP(XN, 0, [[NI * G, 128], [1, NI * G]]), cols=NI * G)
                dbg_dump(dbgQS1, buf=B["QS1"], sb_ap=AP(QS1, 0, [[G, 128], [1, G]]), cols=G)
                dbg_dump(dbgQS2, buf=B["QS2"], sb_ap=AP(QS2, 0, [[G, 128], [1, G]]), cols=G)
                dbg_dump(dbgSIN, AP(C_SIN, 0, [[S, 128], [1, G]]), G)
                dbg_dump(dbgCM, AP(C_CM, 0, [[S, 128], [1, G]]), G)
                dbg_dump(dbgBQ, AP(BQ, 0, [[NI, 128], [1, NI]]), NI)
                dbg_dump(dbgBQU, AP(BQU, 0, [[NI, 128], [1, NI]]), NI)
                dbg_dump(dbgUC, AP(UC, 0, [[NI, 128], [1, NI]]), NI)
            nkt = 4 * g + 4
            pend_chain = None
            for h in range(H):
                pt, hb = h // 2, (h % 2) * 64
                par = h % 2
                cbt = psC if par == 0 else psP
                cbuf = B["psC"] if par == 0 else B["psP"]
                bc, bcbuf = (psB2, B["psB2"]) if par == 0 else (psB1, B["psB1"])
                bcs, bcsbuf = (BCS, B["BCS"]) if par == 0 else (BCS2, B["BCS2"])
                rrow = RCB[32 * par:32 * par + 1, :]
                rbuf = B["rc0"] if par == 0 else B["rc1"]
                pairs = [(ktg, [j0, j0 + 1]) for ktg in range(nkt // 4) for j0 in (0, 2)]
                pend_cmm = None
                for pi, (ktg, js) in enumerate(pairs):
                    diag = (ktg == g)
                    p.use("tensor", B["QP"], B["QT"], B["KP"])
                    for j in js:
                        p.take("tensor", B[f"psW{j}"])
                    def smm(t, js=js, ktg=ktg, pt=pt, hb=hb, g=g):
                        last = None
                        for j in js:
                            kt = ktg * 4 + j
                            last = t.matmul(psW[:, j * G:(j + 1) * G],
                                            KP[hb:hb + 64, pt * S + kt * 128:pt * S + (kt + 1) * 128],
                                            QP[hb:hb + 64, pt * G:(pt + 1) * G],
                                            start=True, stop=False)
                            last = t.matmul(psW[:, j * G:(j + 1) * G],
                                            C_KT[hb:hb + 64, kt * 128:(kt + 1) * 128],
                                            QT[hb:hb + 64, pt * G:(pt + 1) * G],
                                            start=False, stop=True)
                        last.then_inc(p.ST.h, 1)
                    p.ST.reserve(1)
                    p.ev("tensor", smm)
                    stv = (p.ST, p.ST.v)
                    p.wait("scalar", *stv)
                    if diag:
                        p.take("scalar", B["PD"], B["PM"])
                        def expd(sc, js=js):
                            last = None
                            for j in js:
                                if j == 0:
                                    last = sc.activation(PM[:, 0:G], psW[:, 0:G], AF.Exp, bias=0.0, scale=0.125)
                                else:
                                    last = sc.activation(PD[:, (j - 1) * G + 128 * j:j * G],
                                                         psW[:, j * G + 128 * j:(j + 1) * G],
                                                         AF.Exp, bias=0.0, scale=0.125)
                            return last
                        p.ev("scalar", expd)
                        ssv = p.sig("scalar")
                        for j in js:
                            B[f"psW{j}"].cons.append(ssv)
                        p.wait("vector", *ssv)
                        def trim(v, js=js):
                            last = None
                            for j in js:
                                if j == 0:
                                    last = v.tensor_mul(PM[:, 0:128], PM[:, 0:128], C_TRI[:, :])
                                else:
                                    last = v.tensor_mul(PD[:, (j - 1) * G + 128 * j:(j - 1) * G + 128 * (j + 1)],
                                                        PD[:, (j - 1) * G + 128 * j:(j - 1) * G + 128 * (j + 1)],
                                                        C_TRI[:, :])
                            return last
                        p.ev("vector", trim)
                        pready = p.sig("vector")
                        B["PD"].rdy = pready
                        B["PM"].rdy = pready
                    else:
                        p.take("scalar", B["PM"])
                        def expm(sc, js=js):
                            n = len(js)
                            return sc.activation(PM[:, js[0] * G:(js[0] + n) * G],
                                                 psW[:, js[0] * G:(js[0] + n) * G],
                                                 AF.Exp, bias=0.0, scale=0.125)
                        p.ev("scalar", expm)
                        pready = p.sig("scalar")
                        for j in js:
                            B[f"psW{j}"].cons.append(pready)
                        B["PM"].rdy = pready
                    def make_cmm(js=js, ktg=ktg, h=h, diag=diag, first=(pi == 0),
                                 pready=pready, cbt=cbt, cbuf=cbuf):
                        def emit():
                            p.wait("tensor", *pready)
                            p.use("tensor", B["VA"])
                            if first:
                                p.take("tensor", cbuf)
                            def cmm(t):
                                lastmm = None
                                for j in js:
                                    kt = ktg * 4 + j
                                    if diag and j > 0:
                                        Pap = PD[:, (j - 1) * G:j * G]
                                    else:
                                        Pap = PM[:, j * G:(j + 1) * G]
                                    lastmm = t.matmul(cbt[0:65, :],
                                                      AP(VA, kt * 1040 + h * 65, [[16 * 1040, 128], [1, 65]]),
                                                      Pap,
                                                      start=(first and j == js[0]),
                                                      stop=(kt == nkt - 1))
                                lastmm.then_inc(p.ST.h, 1)
                            p.ST.reserve(1)
                            p.ev("tensor", cmm)
                            ctv = (p.ST, p.ST.v)
                            for j in js:
                                (B["PD"] if diag else B["PM"]).cons.append(ctv)
                            return ctv
                        return emit
                    if pend_cmm is not None:
                        pend_cmm()
                    pend_cmm = make_cmm()
                ctv = pend_cmm()
                p.wait("vector", *ctv)
                p.take("vector", rbuf)
                p.ev("vector", lambda v, cbt=cbt, rrow=rrow: v.reciprocal(
                    rrow, cbt[64:65, :]))
                ssr = p.sig("vector")
                rbuf.rdy = ssr

                def make_chain(h=h, par=par, cbt=cbt, cbuf=cbuf, bc=bc, bcbuf=bcbuf,
                               bcs=bcs, bcsbuf=bcsbuf, rrow=rrow, rbuf=rbuf, ssr=ssr,
                               pt=pt, hb=hb):
                    def emit():
                        p.wait("tensor", *ssr)
                        p.take("tensor", bcbuf)
                        p.ST.reserve(1)
                        p.ev("tensor", lambda t: t.matmul(
                            bc[0:64, :], C_ONEF[32 * par:32 * par + 1, 0:64], rrow,
                            start=True, stop=True).then_inc(p.ST.h, 1))
                        stv = (p.ST, p.ST.v)
                        rbuf.cons.append(stv)
                        p.wait("scalar", *stv)
                        p.take("scalar", bcsbuf)
                        p.ev("scalar", lambda sc: sc.copy(bcs[:, :], bc[0:64, :]))
                        ssv = p.sig("scalar")
                        bcbuf.cons.append(ssv)
                        bcsbuf.rdy = ssv
                        p.wait("vector", *ssv)
                        p.take("vector", B["CTXN"]) if h == 0 else None
                        p.ev("vector", lambda v: v.tensor_mul(
                            CTXN[hb:hb + 64, pt * G:(pt + 1) * G], cbt[0:64, :], bcs[:, :]))
                        sv = p.sig("vector")
                        cbuf.cons.append(sv)
                        bcsbuf.cons.append(sv)
                    return emit
                if pend_chain is not None:
                    pend_chain()
                pend_chain = make_chain()
            pend_chain()
            B["CTXN"].rdy = (p.SV, p.SV.v)
            B["QP"].cons.append((p.ST, p.ST.v))
            B["QT"].cons.append((p.ST, p.ST.v))
            B["KP"].cons.append((p.ST, p.ST.v))
            B["VA"].cons.append((p.ST, p.ST.v))
            B["RCB"].rdy = (p.SV, p.SV.v)
            B["RCB"].cons.append((p.ST, p.ST.v))
            if debug_taps and l == DBG_L and b == DBG_B and g == DBG_G:
                dbg_dump(dbgCT, buf=B["CTXN"], sb_ap=AP(CTXN, 0, [[NI * G, 128], [1, NI * G]]), cols=NI * G)
            p.use("tensor", B["CTXN"], B["WS2"])
            rot = 0
            for ot in range(NI):
                pa, pb = ROT5[rot % 5]
                rot += 1
                p.take("tensor", pb)
                def omm(t, ot=ot, pa=pa):
                    last = None
                    for ptt in range(NI):
                        last = t.matmul(pa, WS2[:, ptt * D + ot * 128:ptt * D + (ot + 1) * 128],
                                        CTXN[:, ptt * G:(ptt + 1) * G],
                                        start=(ptt == 0), stop=(ptt == NI - 1))
                    last.then_inc(p.ST.h, 1)
                p.ST.reserve(1)
                p.ev("tensor", omm)
                stv = (p.ST, p.ST.v)
                p.wait("scalar", *stv)
                p.take("scalar", B["AS"])
                p.ev("scalar", lambda sc, ot=ot, pa=pa: sc.add(AS[:, :], pa, BOC[:, ot:ot + 1]))
                ssv = p.sig("scalar")
                pb.cons.append(ssv)
                B["AS"].rdy = ssv
                p.wait("vector", *ssv)
                p.ev("vector", lambda v, ot=ot: v.tensor_add(
                    HG[:, ot * G:(ot + 1) * G], HG[:, ot * G:(ot + 1) * G], AS[:, :]))
                svv = p.sig("vector")
                B["AS"].cons.append(svv)
            svv = (p.SV, p.SV.v)
            B["HG"].rdy = svv
            p.use("sync", B["HG"])
            so = p.dma("sync", p.SO, str_in(hT, col0, G), sb3(HG, G))
            if debug_taps and l == DBG_L:
                so = p.dma("sync", p.SO, str_in(dbgA, col0, G), sb3(HG, G))
            B["HG"].cons.append(so)
            B["CTXN"].cons.append((p.ST, p.ST.v))
            B["WS1"].cons.append((p.ST, p.ST.v))
            B["WS2"].cons.append((p.ST, p.ST.v))

    def ffn_pass(l):
        load_w_ffn(l)
        for gg in range(NB * NG):
            b, c = gg // NG, gg % NG
            col0 = b * S + c * G
            p.take("sync", B["HG"])
            dv = p.dma("sync", p.SH, sb3(HG, G), str_in(hT, col0, G))
            B["HG"].rdy = dv
            ln_stats(HG, GT, B["GT"], C_EPS6, (ROWB, 0), (ROWB, 32))
            ln_normalize(HG, XN, False, (ROWB, 0), (ROWB, 32), dst_buf=B["XN"], src_buf=B["HG"])
            p.use("tensor", B["XN"], B["W1S"])
            p.take("scalar", B["GT"])
            for ot in range(NIH):
                j = ot % 4
                p.take("tensor", B[f"psW{j}"])
                def hmm(t, ot=ot, j=j):
                    last = None
                    for it in range(NI):
                        last = t.matmul(psW[:, j * G:(j + 1) * G],
                                        W1S[:, it * DI + ot * 128:it * DI + (ot + 1) * 128],
                                        XN[:, it * G:(it + 1) * G], start=(it == 0), stop=(it == NI - 1))
                    last.then_inc(p.ST.h, 1)
                p.ST.reserve(1)
                p.ev("tensor", hmm)
                stv = (p.ST, p.ST.v)
                p.wait("scalar", *stv)
                p.ev("scalar", lambda sc, ot=ot, j=j: sc.activation(
                    GT[:, ot * G:(ot + 1) * G], psW[:, j * G:(j + 1) * G], AF.Gelu,
                    bias=B1E[:, ot:ot + 1], scale=1.0))
                ssv = p.sig("scalar")
                B[f"psW{j}"].cons.append(ssv)
            B["GT"].rdy = (p.SS, p.SS.v)
            B["XN"].cons.append((p.ST, p.ST.v))
            B["W1S"].cons.append((p.ST, p.ST.v))
            p.use("tensor", B["GT"], B["W2S"])
            rot = 0
            for ot in range(NI):
                pa, pb = ROT5[rot % 5]
                rot += 1
                p.take("tensor", pb)
                def wmm(t, ot=ot, pa=pa):
                    last = None
                    for it in range(NIH):
                        last = t.matmul(pa, W2S[:, it * D + ot * 128:it * D + (ot + 1) * 128],
                                        GT[:, it * G:(it + 1) * G], start=(it == 0), stop=(it == NIH - 1))
                    last.then_inc(p.ST.h, 1)
                p.ST.reserve(1)
                p.ev("tensor", wmm)
                stv = (p.ST, p.ST.v)
                p.wait("scalar", *stv)
                p.take("scalar", B["AS"])
                p.ev("scalar", lambda sc, ot=ot, pa=pa: sc.add(AS[:, :], pa, B2C[:, ot:ot + 1]))
                ssv = p.sig("scalar")
                pb.cons.append(ssv)
                B["AS"].rdy = ssv
                p.wait("vector", *ssv)
                p.ev("vector", lambda v, ot=ot: v.tensor_add(
                    HG[:, ot * G:(ot + 1) * G], HG[:, ot * G:(ot + 1) * G], AS[:, :]))
                B["AS"].cons.append(p.sig("vector"))
            B["GT"].cons.append((p.ST, p.ST.v))
            B["W2S"].cons.append((p.ST, p.ST.v))
            if l == nl - 1 and do_slot:
                for it in range(NI):
                    p.take("sync", B["XSC"])
                    if it == 0:
                        p.take("sync", B["GT"])
                    dv = p.dma("sync", p.SH, XSC[:, :],
                               AP(xT, it * 128 * T + col0, [[T, 128], [1, G]]))
                    B["XSC"].rdy = dv
                    p.wait("vector", *dv)
                    p.wait("vector", p.SV, p.SV.v)
                    p.ev("vector", lambda v, it=it: v.tensor_add(
                        HG[:, it * G:(it + 1) * G], HG[:, it * G:(it + 1) * G], XSC[:, :]))
                    svz = p.sig("vector")
                    B["XSC"].cons.append(svz)
                    B["GT"].cons.append(svz)
                B["HG"].rdy = (p.SV, p.SV.v)
                ln_stats(HG, GT, B["GT"], C_EPS5, (ROWB, 0), (ROWB, 32))
                ln_normalize(HG, HG, True, (ROWB, 0), (ROWB, 32), src_buf=B["HG"])
                B["HG"].rdy = (p.SV, p.SV.v)
                p.use("sync", B["HG"])
                so = p.dma("sync", p.SO, str_in(x1T, col0, G), sb3(HG, G))
                B["HG"].cons.append(so)
                B["x1T"].rdy = so
            else:
                B["HG"].rdy = (p.SV, p.SV.v)
                p.use("sync", B["HG"])
                so = p.dma("sync", p.SO, str_in(hT, col0, G), sb3(HG, G))
                B["HG"].cons.append(so)

    def slot_f0():
        slot_stage_load()
        p.take("gpsimd", B["XFA"])
        if B["x1T"].rdy is not None:
            p.wait("gpsimd", *B["x1T"].rdy)
        for qq in range(4):
            dv = p.dma("gpsimd", p.SW, AP(XFA, qq * 1024, [[NI * T, 128], [T, NI], [1, 1024]]),
                       AP(x1T, qq * 1024, [[T, 128], [128 * T, NI], [1, 1024]]))
        B["XFA"].rdy = dv
        for ot in range(NIH):
            sa, ba = (STRA, B["STRA"]) if ot % 2 == 0 else (STRB, B["STRB"])
            p.take("gpsimd", ba)
            dv1 = p.dma("gpsimd", p.SW, AP(sa, 0, [[NIH * 128, 128], [128, NI], [1, 128]]),
                        AP(ins["w1s"], ot * 128, [[DI, 128], [128 * DI, NI], [1, 128]]))
            dv2 = p.dma("gpsimd", p.SW, AP(sa, NI * 128, [[NIH * 128, 128], [128, NI], [1, 128]]),
                        AP(ins["w2s"], ot * 128, [[DI, 128], [128 * DI, NI], [1, 128]]))
            ba.rdy = dv2
            for gg in range(NB * NG):
                base = (gg % 2) * 2
                p.use("tensor", ba, B["XFA"], B["stage"])
                p.take("tensor", B[f"psW{base}"], B[f"psW{base + 1}"])
                def f0mm(t, sa=sa, gg=gg, base=base):
                    last = None
                    for w in range(2):
                        for it in range(NI):
                            last = t.matmul(psW[:, (base + w) * G:(base + w + 1) * G],
                                            sa[:, (w * NI + it) * 128:(w * NI + it + 1) * 128],
                                            XFA[:, it * T + gg * G:it * T + (gg + 1) * G],
                                            start=(it == 0), stop=(it == NI - 1))
                    last.then_inc(p.ST.h, 1)
                p.ST.reserve(1)
                p.ev("tensor", f0mm)
                stv = (p.ST, p.ST.v)
                p.wait("scalar", *stv)
                p.take("scalar", B["EVW1"], B["EVW2"])
                p.ev("scalar", lambda sc, ot=ot, base=base: sc.activation(
                    EVW1[:, 0:G], psW[:, base * G:(base + 1) * G], AF.Silu, bias=0.0, scale=SC1[:, ot:ot + 1]))
                p.ev("scalar", lambda sc, ot=ot, base=base: sc.mul(
                    EVW2[:, 0:G], psW[:, (base + 1) * G:(base + 2) * G], SC2[:, ot:ot + 1]))
                ssv = p.sig("scalar")
                B[f"psW{base}"].cons.append(ssv)
                B[f"psW{base + 1}"].cons.append(ssv)
                p.wait("vector", *ssv)
                p.take("vector", B["EVW1"])
                p.ev("vector", lambda v: v.tensor_mul(EVW1[:, G:2 * G], EVW1[:, 0:G], EVW2[:, 0:G]))
                sv = p.sig("vector")
                B["EVW1"].rdy = sv
                B["EVW2"].cons.append(sv)
                p.wait("sync", *sv)
                so = p.dma("sync", p.SO, AP(g1d, ot * 128 * T + gg * G, [[T, 128], [1, G]]),
                           EVW1[:, G:2 * G])
                B["EVW1"].cons.append(so)
                ba.cons.append((p.ST, p.ST.v))
                B["XFA"].cons.append((p.ST, p.ST.v))

    def slot_f12(wsrc, src_d, dst_d, sc_col):
        for pan in range(2):
            p.take("sync", B["GPAN"])
            for qq in range(4):
                dv = p.dma("sync", p.SH,
                           AP(GPAN, qq * 8 * 2048, [[NIH * 2048, 128], [2048, 8], [1, 2048]]),
                           AP(src_d, qq * 8 * 128 * T + pan * 2048, [[T, 128], [128 * T, 8], [1, 2048]]))
            B["GPAN"].rdy = dv
            for ot in range(NIH):
                sa, ba = (STRA, B["STRA"]) if ot % 2 == 0 else (STRB, B["STRB"])
                p.take("gpsimd", ba)
                for qq in range(4):
                    dv = p.dma("gpsimd", p.SW,
                               AP(sa, qq * 8 * 128, [[NIH * 128, 128], [128, 8], [1, 128]]),
                               AP(wsrc, qq * 8 * 128 * DI + ot * 128,
                                  [[DI, 128], [128 * DI, 8], [1, 128]]))
                ba.rdy = dv
                p.use("tensor", ba, B["GPAN"])
                for j in range(4):
                    p.take("tensor", B[f"psW{j}"])
                def fmm(t, sa=sa):
                    last = None
                    for it in range(NIH):
                        for j in range(4):
                            last = t.matmul(psW[:, j * G:(j + 1) * G],
                                            sa[:, it * 128:(it + 1) * 128],
                                            GPAN[:, it * 2048 + j * G:it * 2048 + (j + 1) * G],
                                            start=(it == 0), stop=(it == NIH - 1))
                    last.then_inc(p.ST.h, 1)
                p.ST.reserve(1)
                p.ev("tensor", fmm)
                stv = (p.ST, p.ST.v)
                ev, eb = (EVW1, B["EVW1"]) if ot % 2 == 0 else (EVW2, B["EVW2"])
                p.wait("scalar", *stv)
                p.take("scalar", eb)
                p.ev("scalar", lambda sc, ev=ev, ot=ot: sc.activation(
                    ev[:, :], psW[:, :], AF.Gelu, bias=0.0, scale=sc_col[:, ot:ot + 1]))
                ssv = p.sig("scalar")
                for j in range(4):
                    B[f"psW{j}"].cons.append(ssv)
                eb.rdy = ssv
                p.wait("sync", *ssv)
                so = p.dma("sync", p.SO, AP(dst_d, ot * 128 * T + pan * 2048, [[T, 128], [1, 2048]]),
                           ev[:, :])
                eb.cons.append(so)
                ba.cons.append((p.ST, p.ST.v))
                B["GPAN"].cons.append((p.ST, p.ST.v))

    def slot_f3():
        p.take("gpsimd", B["W3S"])
        for qq in range(4):
            dv = p.dma("gpsimd", p.SW, AP(W3S, qq * 8 * D, [[NIH * D, 128], [D, 8], [1, D]]),
                       AP(ins["w3s"], qq * 8 * 128 * D, [[D, 128], [128 * D, 8], [1, D]]))
        B["W3S"].rdy = dv
        for gg in range(NB * NG):
            b, c = gg // NG, gg % NG
            col0 = b * S + c * G
            p.take("sync", B["GT"])
            for qq in range(4):
                dv = p.dma("sync", p.SH,
                           AP(GT, qq * 8 * G, [[NIH * G, 128], [G, 8], [1, G]]),
                           AP(g3d, qq * 8 * 128 * T + col0, [[T, 128], [128 * T, 8], [1, G]]))
            B["GT"].rdy = dv
            for ot in range(NI):
                ps, pb = (psP, B["psP"]) if ot % 2 == 0 else (psB1, B["psB1"])
                p.use("tensor", B["GT"], B["W3S"])
                p.take("tensor", pb)
                def f3mm(t, ot=ot, ps=ps):
                    last = None
                    for it in range(NIH):
                        last = t.matmul(ps[:, :], W3S[:, it * D + ot * 128:it * D + (ot + 1) * 128],
                                        GT[:, it * G:(it + 1) * G], start=(it == 0), stop=(it == NIH - 1))
                    last.then_inc(p.ST.h, 1)
                p.ST.reserve(1)
                p.ev("tensor", f3mm)
                stv = (p.ST, p.ST.v)
                p.wait("scalar", *stv)
                p.take("scalar", B["AS"])
                p.ev("scalar", lambda sc, ot=ot, ps=ps: sc.mul(AS[:, :], ps[:, :], SC3[:, ot:ot + 1]))
                ssv = p.sig("scalar")
                pb.cons.append(ssv)
                B["AS"].rdy = ssv
                p.take("sync", B["XF3"])
                dv = p.dma("sync", p.SH, XF3[:, :], AP(x1T, ot * 128 * T + col0, [[T, 128], [1, G]]))
                B["XF3"].rdy = dv
                p.wait("vector", *ssv)
                p.wait("vector", *dv)
                p.take("vector", B["HG"]) if ot == 0 else None
                p.ev("vector", lambda v, ot=ot: v.tensor_add(
                    HG[:, ot * G:(ot + 1) * G], AS[:, :], XF3[:, :]))
                sv = p.sig("vector")
                B["AS"].cons.append(sv)
                B["XF3"].cons.append(sv)
            B["HG"].rdy = (p.SV, p.SV.v)
            B["GT"].cons.append((p.ST, p.ST.v))
            B["W3S"].cons.append((p.ST, p.ST.v))
            if debug_taps and gg == 0:
                dbg_dump(dbgZ2, buf=B["HG"], sb_ap=AP(HG, 0, [[NI * G, 128], [1, NI * G]]), cols=NI * G)
            ln_stats(HG, GT, B["GT"], C_EPS5, (ROWB, 0), (ROWB, 32))
            ln_normalize(HG, HG, True, (ROWB, 0), (ROWB, 32), src_buf=B["HG"])
            B["HG"].rdy = (p.SV, p.SV.v)
            if debug_taps and gg == 0:
                dbg_dump(dbgZ3, buf=B["HG"], sb_ap=AP(HG, 0, [[NI * G, 128], [1, NI * G]]), cols=NI * G)
            for tt in range(4):
                p.use("tensor", B["HG"])
                p.take("tensor", B["OSTG"], B["GT"])
                for it in range(NI):
                    ps, pb = (psB2, B["psB2"]) if it % 2 == 0 else (psP, B["psP"])
                    p.take("tensor", pb)
                    p.ST.reserve(1)
                    p.ev("tensor", lambda t, it=it, tt=tt, ps=ps: t.transpose(
                        ps[:, 0:128], HG[:, it * G + tt * 128:it * G + (tt + 1) * 128],
                        C_EYE[:, :]).then_inc(p.ST.h, 1))
                    stv = (p.ST, p.ST.v)
                    p.wait("scalar", *stv)
                    p.ev("scalar", lambda sc, it=it, ps=ps: sc.copy(
                        OSTG[:, it * 128:(it + 1) * 128], ps[:, 0:128]))
                    ssv = p.sig("scalar")
                    pb.cons.append(ssv)
                B["OSTG"].rdy = (p.SS, p.SS.v)
                p.use("sync", B["OSTG"])
                so = p.dma("sync", p.SO, AP(out, (b * S + c * G + tt * 128) * D, [[D, 128], [1, D]]),
                           OSTG[:, :])
                B["OSTG"].cons.append(so)
                B["HG"].cons.append((p.ST, p.ST.v))

    precast = []
    for _src, _dst in ((ins["h1s"], h1b), (ins["h2s"], h2b)):
        for _ck in range(8):
            precast.append((_dst, _src, _ck * 512))

    def maybe_precast():
        if precast:
            _dst, _src, r0 = precast.pop(0)
            p.dma("gpsimd", p.SW, AP(_dst, r0 * DI, [[DI, 512], [1, DI]]),
                  AP(_src, r0 * DI, [[DI, 512], [1, DI]]))

    startup()
    for l in range(nl):
        layer_stage_load(l)
        cast_lnb(L1B)
        for b in range(NB):
            load_w_pair(l, ("wk", "wv"), fold=True)
            maybe_precast()
            kv_pass(l, b)
            load_w_pair(l, ("wq", "wo"), fold=True)
            maybe_precast()
            attn_pass(l, b)
        ffn_pass(l)
        maybe_precast()
    while precast:
        maybe_precast()
    p.wait("gpsimd", p.SW, p.SW.v)
    if do_slot:
        slot_f0()
        slot_f12(h1b, g1d, g2d, SCH1)
        slot_f12(h2b, g2d, g3d, SCH2)
        slot_f3()
    p.wait("sync", p.SO, p.SO.v)
    p.wait("sync", p.SW, p.SW.v)

    with nc.Block() as block:
        p.flush(block)
    return nc


_cached = {}


def _get_program():
    if "nc" not in _cached:
        _cached["nc"] = build_program()
    return _cached["nc"]


def prep_common(inputs):
    def col(a, ncols):
        return np.ascontiguousarray(a.reshape(ncols, 128).T)

    common = {}
    for l in range(NL):
        common[f"wq{l}"] = np.ascontiguousarray(inputs["L0_attn_Wq"][l].T)
        common[f"wk{l}"] = np.ascontiguousarray(inputs["L0_attn_Wk"][l].T)
        common[f"wv{l}"] = np.ascontiguousarray(inputs["L0_attn_Wv"][l].T)
        common[f"wo{l}"] = np.ascontiguousarray(inputs["L0_attn_Wo"][l].T)
        common[f"w1{l}"] = np.ascontiguousarray(inputs["L0_attn_W1"][l].T)
        common[f"w2{l}"] = np.ascontiguousarray(inputs["L0_attn_W2"][l].T)
        common[f"bqc{l}"] = col(inputs["L0_attn_bq"][l], NI)
        common[f"bkc{l}"] = col(inputs["L0_attn_bk"][l], NI)
        common[f"boc{l}"] = col(inputs["L0_attn_bo"][l], NI)
        common[f"b2c{l}"] = col(inputs["L0_attn_b2"][l], NI)
        common[f"b1c{l}"] = col(inputs["L0_attn_b1"][l], NIH)
        common[f"uc{l}"] = col(inputs["L0_attn_u_bias"][l].reshape(-1), NI)
        common[f"l1wc{l}"] = col(inputs["L0_attn_ln1_w"][l], NI)
        common[f"l1bc{l}"] = col(inputs["L0_attn_ln1_b"][l], NI)
        common[f"l2wc{l}"] = col(inputs["L0_attn_ln2_w"][l], NI)
        common[f"l2bc{l}"] = col(inputs["L0_attn_ln2_b"][l], NI)
        common[f"bvr{l}"] = np.ascontiguousarray(inputs["L0_attn_bv"][l].reshape(1, D))
    common["w1s"] = np.ascontiguousarray(inputs["L0_ffn_w1_w"].T)
    common["w2s"] = np.ascontiguousarray(inputs["L0_ffn_w2_w"].T)
    common["h1s"] = np.ascontiguousarray(inputs["L0_ffn_h1_w"].T)
    common["h2s"] = np.ascontiguousarray(inputs["L0_ffn_h2_w"].T)
    common["w3s"] = np.ascontiguousarray(inputs["L0_ffn_w3_w"].T)
    common["s1c"] = col(inputs["L0_ffn_w1_s"], NIH)
    common["s2c"] = col(inputs["L0_ffn_w2_s"], NIH)
    common["sh1c"] = col(inputs["L0_ffn_h1_s"], NIH)
    common["sh2c"] = col(inputs["L0_ffn_h2_s"], NIH)
    common["s3c"] = col(inputs["L0_ffn_w3_s"], NI)
    return {k: np.ascontiguousarray(v.astype(np.float32)) for k, v in common.items()}


def kernel(**inputs):
    nc = _get_program()
    x = np.ascontiguousarray(inputs["L0_attn_x"])
    common = prep_common(inputs)
    in_maps = []
    for c in range(8):
        m = dict(common)
        m["xin"] = np.ascontiguousarray(x[2 * c:2 * c + 2]).astype(np.float32)
        in_maps.append(m)

    res = run_bass_kernel_spmd(nc, in_maps, core_ids=list(range(8)))
    return np.concatenate([r["out"] for r in res.results], axis=0)


if __name__ == "__main__":
    nc = build_program()
    print("instructions:", nc.n_instructions())


# version 08
# speedup vs baseline: 1.1523x; 1.1463x over previous
import sys
sys.path.insert(0, "/opt/trn_rl_repo")
import numpy as np
import concourse.bass as bass
import concourse.mybir as mybir
from concourse.bass_utils import run_bass_kernel_spmd

F32 = mybir.dt.float32
BF16 = mybir.dt.bfloat16
AF = mybir.ActivationFunctionType
ALU = mybir.AluOpType
AP = bass.AP

S = 2048
D = 1024
H = 16
DH = 64
DI = 4096
NL = 4
NB = 2
T = NB * S
G = 512
NG = S // G
NI = D // 128
NIH = DI // 128
KB = 1024
DBG_B = 0
DBG_G = 0
DBG_L = 0


class Sem:
    def __init__(self, nc, name):
        self.h = nc.alloc_semaphore(name=name)
        self.v = 0

    def reserve(self, k):
        self.v += k
        return self.v


class Buf:
    __slots__ = ("rdy", "cons", "rng")

    def __init__(self, rng=None):
        self.rdy = None
        self.cons = []
        self.rng = rng


class MK:
    ENGS = ("tensor", "scalar", "vector", "gpsimd", "sync")

    def __init__(self, nc):
        self.nc = nc
        self.q = {e: [] for e in self.ENGS}
        self.ST = Sem(nc, "sT")
        self.SS = Sem(nc, "sS")
        self.SV = Sem(nc, "sV")
        self.SW = Sem(nc, "sW")
        self.SH = Sem(nc, "sH")
        self.SO = Sem(nc, "sO")
        self.esem = {"tensor": self.ST, "scalar": self.SS, "vector": self.SV}
        self.overlaps = []
        self._waited = {e: {} for e in self.ENGS}
        self.n_drain_fallback = 0

    def ev(self, eng, fn, kind="op"):
        self.q[eng].append((fn, kind))

    def wait(self, eng, sem, v):
        assert v <= sem.v, f"wait beyond reserved: {v} > {sem.v}"
        if v <= 0:
            return
        w = self._waited[eng]
        if w.get(id(sem), -1) >= v:
            return
        w[id(sem)] = v
        self.ev(eng, lambda e, s=sem.h, vv=v: e.wait_ge(s, vv), kind="wait")

    def use(self, eng, *bufs):
        for b in bufs:
            if b.rdy is not None:
                self.wait(eng, *b.rdy)

    def take(self, eng, *bufs):
        for b in bufs:
            for sv in b.cons:
                self.wait(eng, *sv)
            b.cons = []
            if b.rng is not None:
                s0, e0 = b.rng
                for o in self.overlaps:
                    if o is b or o.rng is None:
                        continue
                    s1, e1 = o.rng
                    if s1 < e0 and s0 < e1:
                        if o.rdy is not None:
                            self.wait(eng, *o.rdy)
                            o.rdy = None
                        for sv in o.cons:
                            self.wait(eng, *sv)
                        o.cons = []

    def done_c(self, buf, sem, v):
        buf.cons.append((sem, v))

    def done_p(self, buf, sem, v):
        buf.rdy = (sem, v)
        buf.cons = []

    def sig(self, eng, k=1):
        assert eng in ("scalar", "vector"), eng
        sem = self.esem[eng]
        v = sem.reserve(k)
        q = self.q[eng]
        i = len(q) - 1
        while i >= 0 and q[i][1] == "wait":
            i -= 1
        if i >= 0 and q[i][1] == "op":
            f = q[i][0]
            def wrapped(e, f=f, s=sem.h, kk=k):
                op = f(e)
                op.then_inc(s, kk)
                return op
            q[i] = (wrapped, "sig")
        else:
            self.n_drain_fallback += 1
            self.ev(eng, lambda e, s=sem.h, kk=k: e.drain().then_inc(s, kk),
                    kind="sig")
        return sem, v

    def mminc(self, k=1):
        return self.ST, self.ST.reserve(k)

    def dma(self, eng, sem, out_ap, in_ap):
        v = sem.reserve(16)
        self.ev(eng, lambda e, o=out_ap, i=in_ap, s=sem.h: e.dma_start(out=o, in_=i).then_inc(s, 16),
                kind="sig")
        return sem, v

    def flush(self, block):
        for e in self.ENGS:
            fns = self.q[e]
            if fns:
                getattr(block, e)(lambda eng, fns=fns: [f(eng) for f, _ in fns])


def host_constants():
    w = 1.0 / 10000.0 ** (np.arange(0, DH, 2, dtype=np.float64) / DH)
    s = np.arange(S, dtype=np.float64)
    ang = s[None, :] * w[:, None]
    sin, cos = np.sin(ang), np.cos(ang)
    blk_sin = np.concatenate([sin, sin], 0)
    blk_cm = np.concatenate([cos, -cos], 0)
    blk_kt = np.concatenate([cos, sin], 0)
    SIN = np.concatenate([blk_sin, blk_sin], 0).astype(np.float32)
    CM = np.concatenate([blk_cm, blk_cm], 0).astype(np.float32)
    KT = np.concatenate([blk_kt, blk_kt], 0).astype(np.float32)
    TRI = (np.arange(128)[:, None] <= np.arange(128)[None, :]).astype(np.float32)
    EYE = np.eye(128, dtype=np.float32)
    ONEF = np.ones((128, 128), dtype=np.float32)
    return SIN, CM, KT, TRI, EYE, ONEF


def build_program(debug_taps=False, nl=NL, do_slot=True):
    nc = bass.Bass("TRN2", target_bir_lowering=False, debug=False, dynamic_dma_scratch_size=16384)

    xin = nc.dram_tensor("xin", [NB, S, D], F32, kind="ExternalInput")
    ins = {}
    for l in range(NL):
        for nm, shp in (("wq", [D, D]), ("wk", [D, D]), ("wv", [D, D]), ("wo", [D, D]),
                        ("w1", [D, DI]), ("w2", [DI, D]),
                        ("bqc", [128, NI]), ("bkc", [128, NI]), ("boc", [128, NI]),
                        ("b2c", [128, NI]), ("b1c", [128, NIH]), ("uc", [128, NI]),
                        ("l1wc", [128, NI]), ("l1bc", [128, NI]),
                        ("l2wc", [128, NI]), ("l2bc", [128, NI]),
                        ("bvr", [1, D])):
            ins[f"{nm}{l}"] = nc.dram_tensor(f"{nm}{l}", shp, F32, kind="ExternalInput")
    for nm, shp in (("w1s", [D, DI]), ("w2s", [D, DI]), ("h1s", [DI, DI]),
                    ("h2s", [DI, DI]), ("w3s", [DI, D]),
                    ("s1c", [128, NIH]), ("s2c", [128, NIH]), ("sh1c", [128, NIH]),
                    ("sh2c", [128, NIH]), ("s3c", [128, NI])):
        ins[nm] = nc.dram_tensor(nm, shp, F32, kind="ExternalInput")
    out = nc.dram_tensor("out", [NB, S, D], F32, kind="ExternalOutput")

    kind = "ExternalOutput" if debug_taps else "Internal"
    if debug_taps:
        dbgA = nc.dram_tensor("dbgA", [D, T], F32, kind="ExternalOutput")
        dbgKP = nc.dram_tensor("dbgKP", [128, NI * S], F32, kind="ExternalOutput")
        dbgVA = nc.dram_tensor("dbgVA", [128, 16 * H * 65], F32, kind="ExternalOutput")
        dbgQP = nc.dram_tensor("dbgQP", [128, NI * G], F32, kind="ExternalOutput")
        dbgQT = nc.dram_tensor("dbgQT", [128, NI * G], F32, kind="ExternalOutput")
        dbgCT = nc.dram_tensor("dbgCT", [128, NI * G], F32, kind="ExternalOutput")
        dbgXN = nc.dram_tensor("dbgXN", [128, NI * G], F32, kind="ExternalOutput")
        dbgQS1 = nc.dram_tensor("dbgQS1", [128, G], F32, kind="ExternalOutput")
        dbgQS2 = nc.dram_tensor("dbgQS2", [128, G], F32, kind="ExternalOutput")
        dbgSIN = nc.dram_tensor("dbgSIN", [128, G], F32, kind="ExternalOutput")
        dbgCM = nc.dram_tensor("dbgCM", [128, G], F32, kind="ExternalOutput")
        dbgBQ = nc.dram_tensor("dbgBQ", [128, NI], F32, kind="ExternalOutput")
        dbgBQU = nc.dram_tensor("dbgBQU", [128, NI], F32, kind="ExternalOutput")
        dbgUC = nc.dram_tensor("dbgUC", [128, NI], F32, kind="ExternalOutput")
        dbgWS2 = nc.dram_tensor("dbgWS2", [128, NI * D], F32, kind="ExternalOutput")
        dbgBQ2 = nc.dram_tensor("dbgBQ2", [128, NI], F32, kind="ExternalOutput")
        dbgBQU2 = nc.dram_tensor("dbgBQU2", [128, NI], F32, kind="ExternalOutput")
        dbgPSB = nc.dram_tensor("dbgPSB", [128, NI], F32, kind="ExternalOutput")
        dbgZ2 = nc.dram_tensor("dbgZ2", [128, NI * G], F32, kind="ExternalOutput")
        dbgZ3 = nc.dram_tensor("dbgZ3", [128, NI * G], F32, kind="ExternalOutput")
    hT = nc.dram_tensor("hT", [D, T], F32, kind=kind)
    xT = nc.dram_tensor("xT", [D, T], F32, kind=kind)
    x1T = nc.dram_tensor("x1T", [D, T], F32, kind=kind)
    g1d = nc.dram_tensor("g1d", [DI, T], BF16, kind=kind)
    g2d = nc.dram_tensor("g2d", [DI, T], BF16, kind=kind)
    g3d = nc.dram_tensor("g3d", [DI, T], BF16, kind=kind)
    h1b = nc.dram_tensor("h1b", [DI, DI], BF16, kind="Internal")
    h2b = nc.dram_tensor("h2b", [DI, DI], BF16, kind="Internal")

    SINc, CMc, KTc, TRIc, EYEc, ONEFc = host_constants()
    d_sin = nc.inline_tensor(SINc, name="c_sin")
    d_cm = nc.inline_tensor(CMc, name="c_cm")
    d_kt = nc.inline_tensor(KTc, name="c_kt")
    d_tri = nc.inline_tensor(TRIc, name="c_tri")
    d_eye = nc.inline_tensor(EYEc, name="c_eye")
    d_onef = nc.inline_tensor(ONEFc, name="c_onef")
    d_onecol = nc.inline_tensor(np.ones((128, 1), np.float32), name="c_onecol")
    d_eps5 = nc.inline_tensor(np.full((128, 1), 1e-5, np.float32), name="c_eps5")
    d_eps6 = nc.inline_tensor(np.full((128, 1), 1e-6, np.float32), name="c_eps6")

    A = nc.alloc_sbuf_tensor
    C_SIN = A("C_SIN", [128, S], BF16)
    C_CM = A("C_CM", [128, S], BF16)
    C_KT = A("C_KT", [128, S], BF16)
    C_TRI = A("C_TRI", [128, 128], BF16)
    C_EYE = A("C_EYE", [128, 128], F32)
    C_ONEF = A("C_ONEF", [128, 128], F32)
    C_ONEB = A("C_ONEB", [128, 128], BF16)
    C_ONECOL = A("C_ONECOL", [128, 1], BF16)
    C_EPS5 = A("C_EPS5", [128, 1], F32)
    C_EPS6 = A("C_EPS6", [128, 1], F32)
    BQ = A("BQ", [128, NI], F32)
    BK = A("BK", [128, NI], F32)
    BQU = A("BQU", [128, NI], F32)
    B1E = A("B1E", [128, NIH], F32)
    BQR = A("BQR", [128, NI], F32)
    BKR = A("BKR", [128, NI], F32)
    BOC = A("BOC", [128, NI], F32)
    B2C = A("B2C", [128, NI], F32)
    B1R = A("B1R", [128, NIH], F32)
    UC = A("UC", [128, NI], F32)
    L1W = A("L1W", [128, NI], F32)
    L1B = A("L1B", [128, NI], F32)
    L2W = A("L2W", [128, NI], F32)
    L2B = A("L2B", [128, NI], F32)
    LBB = A("LBB", [128, NI], BF16)
    BVT = A("BVT", [128, 2 * G], BF16)
    SC1 = A("SC1", [128, NIH], F32)
    SC2 = A("SC2", [128, NIH], F32)
    SCH1 = A("SCH1", [128, NIH], F32)
    SCH2 = A("SCH2", [128, NIH], F32)
    SC3 = A("SC3", [128, NI], F32)
    ROWA = A("ROWA", [128, G], F32)
    ROWB = A("ROWB", [128, G], F32)

    base = nc.sbuf_base

    def at(name, off_kb, shape, dtype):
        return nc.alloc_sbuf_tensor_at(name, shape, dtype, offset=base + int(off_kb * KB))

    KP = at("KP", 0, [128, NI * S], BF16)
    VA = at("VA", 32, [128, (S // 128) * H * 65], BF16)
    WS1 = at("WS1", 65, [128, NI * D], BF16)
    WS2 = at("WS2", 81, [128, NI * D], BF16)
    QP = at("QP", 97, [128, NI * G], BF16)
    QT = at("QT", 105, [128, NI * G], BF16)
    PM = at("PM", 113, [128, 4 * G], BF16)
    PD = at("PD", 117, [128, 3 * G], BF16)
    QS1 = at("QS1", 121, [128, G], BF16)
    QS2 = at("QS2", 122, [128, G], BF16)
    BCS = at("BCS", 123, [64, G], F32)
    BCS2 = at("BCS2", 121, [64, G], F32)
    RCB = at("RCB", 125, [33, G], F32)
    W1S = at("W1S", 0, [128, NI * DI], BF16)
    W2S = at("W2S", 64, [128, NIH * D], BF16)
    GPAN = at("GPAN", 0, [128, NIH * 2048], BF16)
    XFA = at("XFA", 0, [128, NI * T], BF16)
    W3S = at("W3S", 48, [128, NIH * D], BF16)
    HG = at("HG", 128, [128, NI * G], F32)
    XN = at("XN", 144, [128, NI * G], BF16)
    CTXN = at("CTXN", 152, [128, NI * G], BF16)
    GT = at("GT", 152, [128, NIH * G], BF16)
    AS = at("AS", 184, [128, G], F32)
    XSC = at("XSC", 164, [128, G], F32)
    OSTG = at("OSTG", 160, [128, D], F32)
    XF3 = at("XF3", 144, [128, G], F32)
    STRA = at("STRA", 128, [128, NIH * 128], BF16)
    STRB = at("STRB", 136, [128, NIH * 128], BF16)
    EVW1 = at("EVW1", 144, [128, 2048], BF16)
    EVW2 = at("EVW2", 148, [128, 2048], BF16)
    arena_end = base + 186 * KB
    assert arena_end <= nc.sbuf_top, f"SBUF over: {arena_end} > {nc.sbuf_top}"

    psW = nc.alloc_psum_tensor("psW", [128, 4 * G], F32)
    psP = nc.alloc_psum_tensor("psP", [128, G], F32)
    psC = nc.alloc_psum_tensor("psC", [65, G], F32)
    psB1 = nc.alloc_psum_tensor("psB1", [128, G], F32)
    psB2 = nc.alloc_psum_tensor("psB2", [128, G], F32)

    p = MK(nc)

    B = {nm: Buf() for nm in (
        "KP", "VA", "WS1", "WS2", "QP", "QT", "PM", "PD", "QS1", "QS2", "BCS",
        "BCS2", "RCB", "rc0", "rc1",
        "W1S", "W2S", "GPAN", "XFA", "W3S", "HG", "XN", "CTXN", "GT", "AS",
        "XSC", "OSTG", "STRA", "STRB", "EVW1", "EVW2", "XF3",
        "psW0", "psW1", "psW2", "psW3", "psP", "psC", "psB1", "psB2",
        "rows", "cache", "stage", "BVT", "LBB", "hT", "xT", "x1T", "g1", "g2", "g3")}

    _ranges = {
        "KP": (0, 32), "VA": (32, 64.5), "WS1": (65, 81), "WS2": (81, 97),
        "QP": (97, 105), "QT": (105, 113), "PM": (113, 117), "PD": (117, 120),
        "QS1": (121, 122), "QS2": (122, 123), "BCS": (123, 125), "BCS2": (121, 123),
        "RCB": (125, 127),
        "W1S": (0, 64), "W2S": (64, 128), "GPAN": (0, 128), "XFA": (0, 64),
        "STRA": (128, 136), "STRB": (136, 144), "W3S": (48, 112),
        "HG": (128, 144), "XN": (144, 152), "CTXN": (152, 160), "GT": (152, 184),
        "OSTG": (160, 164), "XSC": (164, 166), "AS": (184, 186), "XF3": (144, 146),
        "EVW1": (144, 148), "EVW2": (148, 152),
    }
    for _nm, _rng in _ranges.items():
        B[_nm].rng = _rng
    p.overlaps = [B[_nm] for _nm in _ranges]

    ROT5 = [(psP[:, :], B["psP"]), (psW[:, 0:G], B["psW0"]), (psW[:, G:2 * G], B["psW1"]),
            (psW[:, 2 * G:3 * G], B["psW2"]), (psW[:, 3 * G:4 * G], B["psW3"])]

    def str_in(dram, c0, n, nt=NI):
        return AP(dram, c0, [[T, 128], [128 * T, nt], [1, n]])

    def sb3(sb, n, nt=NI, off=0):
        return AP(sb, off, [[nt * n, 128], [n, nt], [1, n]])

    def dbg_dump(dram, sb_ap=None, cols=None, buf=None):
        p.wait("gpsimd", p.SS, p.SS.v)
        p.wait("gpsimd", p.SV, p.SV.v)
        p.wait("gpsimd", p.ST, p.ST.v)
        dv = p.dma("gpsimd", p.SW, AP(dram, 0, [[cols, 128], [1, cols]]), sb_ap)
        for e in ("tensor", "scalar", "vector", "sync"):
            p.wait(e, p.SW, dv[1])
        if buf is not None:
            buf.cons.append(dv)

    def startup():
        for dst, src in ((C_SIN, d_sin), (C_CM, d_cm), (C_KT, d_kt), (C_TRI, d_tri)):
            p.dma("gpsimd", p.SW, dst[:, :], src.ap())
        p.dma("gpsimd", p.SW, C_ONEB[:, :], d_onef.ap())
        p.dma("gpsimd", p.SW, C_ONECOL[:, :], d_onecol.ap())
        for dst, src in ((C_EYE, d_eye), (C_ONEF, d_onef), (C_EPS5, d_eps5), (C_EPS6, d_eps6)):
            p.dma("sync", p.SH, dst[:, :], src.ap())
        p.wait("vector", p.SW, p.SW.v)
        p.wait("vector", p.SH, p.SH.v)
        p.ev("vector", lambda v: v.memset(AP(VA, 64, [[16 * 1040, 128], [1040, 16], [65, 16]]), 1.0))
        def zpd(v):
            last = None
            for j in range(1, 4):
                last = v.memset(AP(PD, (j - 1) * G, [[3 * G, 128], [1, 128 * j]]), 0.0)
            return last
        p.ev("vector", zpd)
        sv = p.sig("vector")
        B["VA"].rdy = sv
        B["PD"].rdy = sv

    def layer_stage_load(l):
        for dst, nm in ((BQR, "bqc"), (BKR, "bkc"), (BOC, "boc"), (B2C, "b2c"),
                        (B1R, "b1c"), (UC, "uc"), (L1W, "l1wc"), (L1B, "l1bc"),
                        (L2W, "l2wc"), (L2B, "l2bc")):
            p.take("sync", B["stage"])
            p.dma("sync", p.SH, dst[:, :], ins[f"{nm}{l}"].ap())
        p.take("sync", B["rows"])
        p.dma("sync", p.SH, ROWA[0:1, :], ins[f"bvr{l}"].ap()[0:1, 0:G])
        sv = p.dma("sync", p.SH, ROWB[0:1, :], ins[f"bvr{l}"].ap()[0:1, G:2 * G])
        B["rows"].rdy = sv
        B["stage"].rdy = sv

    def slot_stage_load():
        for dst, nm in ((SC1, "s1c"), (SC2, "s2c"), (SCH1, "sh1c"), (SCH2, "sh2c"),
                        (SC3, "s3c")):
            p.dma("sync", p.SH, dst[:, :], ins[nm].ap())
        B["stage"].rdy = (p.SH, p.SH.v)

    def load_hg(src, col0):
        for sem_v in B["HG"].cons:
            p.wait("sync", *sem_v)
            p.wait("gpsimd", *sem_v)
        B["HG"].cons = []
        p.wait("gpsimd", p.SO, p.SO.v)
        dv1 = p.dma("sync", p.SH, AP(HG, 0, [[NI * G, 128], [G, 4], [1, G]]),
                    AP(src, col0, [[T, 128], [128 * T, 4], [1, G]]))
        dv2 = p.dma("gpsimd", p.SW, AP(HG, 4 * G, [[NI * G, 128], [G, 4], [1, G]]),
                    AP(src, col0 + 4 * 128 * T, [[T, 128], [128 * T, 4], [1, G]]))
        B["HG"].rdy = dv1
        p.wait("scalar", *dv2)
        p.wait("vector", *dv2)

    def ln_stats(src_f32_sb, sq_sb, sq_buf, eps_tile, rstd_dst, mr_dst, nt=NI):
        ndim = nt * 128
        p.use("scalar", B["HG"])
        p.take("scalar", B["XN"], sq_buf)
        def casts(sc):
            last = None
            for it in range(nt):
                sc.copy(XN[:, it * G:(it + 1) * G], src_f32_sb[:, it * G:(it + 1) * G])
            for it in range(nt):
                last = sc.activation(sq_sb[:, it * G:(it + 1) * G], src_f32_sb[:, it * G:(it + 1) * G], AF.Square)
            return last
        p.ev("scalar", casts)
        ss = p.sig("scalar")
        B["XN"].rdy = ss
        sq_buf.rdy = ss
        B["HG"].cons.append(ss)
        p.use("tensor", B["XN"], sq_buf)
        p.take("tensor", B["psB1"], B["psB2"])
        def stats(t):
            last = None
            for it in range(nt):
                last = t.matmul(psB1[0:1, :], C_ONECOL[:, :], XN[:, it * G:(it + 1) * G],
                                start=(it == 0), stop=(it == nt - 1))
            for it in range(nt):
                last = t.matmul(psB2[0:1, :], C_ONECOL[:, :], sq_sb[:, it * G:(it + 1) * G],
                                start=(it == 0), stop=(it == nt - 1))
            last.then_inc(p.ST.h, 1)
        p.ST.reserve(1)
        p.ev("tensor", stats)
        stv = (p.ST, p.ST.v)
        B["XN"].cons.append(stv)
        sq_buf.cons.append(stv)
        p.wait("scalar", *stv)
        p.take("scalar", B["rows"])
        inv = 1.0 / ndim
        p.ev("scalar", lambda sc: sc.mul(ROWA[0:1, :], psB1[0:1, :], inv))
        p.ev("scalar", lambda sc: sc.mul(ROWA[32:33, :], psB2[0:1, :], inv))
        p.ev("scalar", lambda sc: sc.activation(ROWB[32:33, :], ROWA[0:1, :], AF.Square))
        ss = p.sig("scalar")
        p.wait("vector", *ss)
        p.ev("vector", lambda v: v.tensor_sub(ROWA[64:65, :], ROWA[32:33, :], ROWB[32:33, :]))
        sv = p.sig("vector")
        p.wait("scalar", *sv)
        p.ev("scalar", lambda sc: sc.activation(ROWB[64:65, :], ROWA[64:65, :], AF.Ln,
                                                bias=eps_tile[64:65, 0:1]))
        p.ev("scalar", lambda sc: sc.activation(ROWB[0:1, :], ROWB[64:65, :], AF.Exp, bias=0.0, scale=-0.5))
        ss = p.sig("scalar")
        p.wait("vector", *ss)
        p.ev("vector", lambda v: v.tensor_mul(ROWB[32:33, :], ROWA[0:1, :], ROWB[0:1, :]))
        sv = p.sig("vector")
        rt, rb = rstd_dst
        mt, mb = mr_dst
        p.wait("scalar", *sv)
        p.take("scalar", B["cache"])
        p.ev("scalar", lambda sc: sc.copy(rt[rb:rb + 1, :], ROWB[0:1, :]))
        p.ev("scalar", lambda sc: sc.copy(mt[mb:mb + 1, :], ROWB[32:33, :]))
        ss = p.sig("scalar")
        B["rows"].rdy = ss
        B["cache"].rdy = ss
        B["psB1"].cons.append(ss)
        B["psB2"].cons.append(ss)
        return ss

    def ln_normalize(src_f32_sb, dst_sb, dst_dtype_is_f32, rstd_src, mr_src, nt=NI,
                     dst_buf=None, src_buf=None):
        rt, rb = rstd_src
        mt, mb = mr_src
        p.use("tensor", B["cache"])
        p.take("tensor", B["psB1"], B["psB2"])
        def bcast(t):
            t.matmul(psB1[:, :], C_ONEF[rb:rb + 1, :], rt[rb:rb + 1, :], start=True, stop=True)
            t.matmul(psB2[:, :], C_ONEF[mb:mb + 1, :], mt[mb:mb + 1, :],
                     start=True, stop=True).then_inc(p.ST.h, 1)
        p.ST.reserve(1)
        p.ev("tensor", bcast)
        stv = (p.ST, p.ST.v)
        B["cache"].cons.append(stv)
        p.wait("vector", *stv)
        if src_buf is not None:
            p.use("vector", src_buf)
        p.take("vector", B["AS"])
        if dst_buf is not None:
            p.take("vector", dst_buf)
        def norm(v):
            last = None
            for it in range(nt):
                v.tensor_mul(AS[:, :], src_f32_sb[:, it * G:(it + 1) * G], psB1[:, :])
                last = v.tensor_sub(dst_sb[:, it * G:(it + 1) * G], AS[:, :], psB2[:, :])
            return last
        p.ev("vector", norm)
        sv = p.sig("vector")
        if dst_buf is not None:
            dst_buf.rdy = sv
        if src_buf is not None:
            src_buf.cons.append(sv)
        B["psB1"].cons.append(sv)
        B["psB2"].cons.append(sv)
        B["cache"].cons.append(sv)
        B["AS"].rdy = sv
        B["AS"].cons.append(sv)
        return sv


    def cast_lnb(col_src):
        p.use("scalar", B["stage"])
        p.take("scalar", B["LBB"])
        p.ev("scalar", lambda sc, cs=col_src: sc.copy(LBB[:, :], cs[:, :]))
        B["LBB"].rdy = p.sig("scalar")

    def load_w_pair(l, names, fold):
        slots = {names[0]: WS1, names[1]: WS2}
        bufs = {names[0]: B["WS1"], names[1]: B["WS2"]}
        for nm in names:
            sl, bb = slots[nm], bufs[nm]
            p.take("gpsimd", bb)
            dv = p.dma("gpsimd", p.SW, sb3(sl, D), AP(ins[f"{nm}{l}"], 0, [[D, 128], [128 * D, NI], [1, D]]))
            bb.rdy = dv
        for nm in names:
            sl, bb = slots[nm], bufs[nm]
            if nm in ("wq", "wk"):
                dstc, rawc = (BQ, BQR) if nm == "wq" else (BK, BKR)
                p.use("tensor", bb, B["LBB"])
                p.take("tensor", B["psB2"])
                def bias_mm(t, sl=sl):
                    last = None
                    for ot in range(NI):
                        for it in range(NI):
                            last = t.matmul(psB2[:, ot:ot + 1],
                                            sl[:, it * D + ot * 128:it * D + (ot + 1) * 128],
                                            LBB[:, it:it + 1], start=(it == 0), stop=(it == NI - 1))
                    last.then_inc(p.ST.h, 1)
                p.ST.reserve(1)
                p.ev("tensor", bias_mm)
                stv = (p.ST, p.ST.v)
                p.wait("vector", *stv)
                p.use("vector", B["stage"])
                p.ev("vector", lambda v, dstc=dstc, rawc=rawc: v.tensor_add(dstc[:, :], psB2[:, 0:NI], rawc[:, :]))
                if nm == "wq":
                    p.ev("vector", lambda v: v.engine_nop())
                    p.ev("vector", lambda v: v.tensor_add(BQU[:, :], BQ[:, :], UC[:, :]))
                sv = p.sig("vector")
                B["psB2"].cons.append(sv)
                B["stage"].cons.append(sv)
            elif nm == "wv":
                p.take("sync", B["rows"])
                p.dma("sync", p.SH, ROWA[0:1, :], ins[f"bvr{l}"].ap()[0:1, 0:G])
                dvb = p.dma("sync", p.SH, ROWB[0:1, :], ins[f"bvr{l}"].ap()[0:1, G:2 * G])
                B["rows"].rdy = dvb
                p.use("tensor", bb, B["LBB"])
                p.take("tensor", B["psB1"], B["psB2"])
                def bias_v(t, sl=sl):
                    last = None
                    for half, ps in ((0, psB2), (1, psB1)):
                        for it in range(NI):
                            last = t.matmul(ps[0:1, 0:G], LBB[:, it:it + 1],
                                            sl[:, it * D + half * G:it * D + half * G + G],
                                            start=(it == 0), stop=(it == NI - 1))
                    last.then_inc(p.ST.h, 1)
                p.ST.reserve(1)
                p.ev("tensor", bias_v)
                stv = (p.ST, p.ST.v)
                p.wait("vector", *stv)
                p.use("vector", B["stage"], B["rows"])
                p.wait("vector", *dvb)
                p.ev("vector", lambda v: v.tensor_add(ROWA[32:33, :], psB2[0:1, 0:G], ROWA[0:1, :]))
                p.ev("vector", lambda v: v.tensor_add(ROWB[32:33, :], psB1[0:1, 0:G], ROWB[0:1, :]))
                sv = p.sig("vector")
                B["psB1"].cons.append(sv)
                B["psB2"].cons.append(sv)
                p.wait("scalar", *sv)
                p.take("scalar", B["BVT"])
                p.ev("scalar", lambda sc: sc.copy(BVT[0:1, 0:G], ROWA[32:33, :]))
                p.ev("scalar", lambda sc: sc.copy(BVT[0:1, G:2 * G], ROWB[32:33, :]))
                ssv = p.sig("scalar")
                B["BVT"].rdy = ssv
                B["rows"].rdy = ssv
                p.wait("tensor", *ssv)
                p.take("tensor", B["psB1"], B["psB2"])
                p.ST.reserve(1)
                def bvbc(t):
                    t.matmul(psB1[:, :], C_ONEB[0:1, :], BVT[0:1, 0:G], start=True, stop=True)
                    t.matmul(psB2[:, :], C_ONEB[0:1, :], BVT[0:1, G:2 * G],
                             start=True, stop=True).then_inc(p.ST.h, 1)
                p.ev("tensor", bvbc)
                stv2 = (p.ST, p.ST.v)
                p.wait("scalar", *stv2)
                p.take("scalar", B["BVT"])
                p.ev("scalar", lambda sc: sc.copy(BVT[:, 0:G], psB1[:, :]))
                p.ev("scalar", lambda sc: sc.copy(BVT[:, G:2 * G], psB2[:, :]))
                sbv = p.sig("scalar")
                B["BVT"].rdy = sbv
                B["psB1"].cons.append(sbv)
                B["psB2"].cons.append(sbv)
            if fold and nm in ("wq", "wk", "wv"):
                p.use("vector", bb, B["stage"])
                if nm in ("wq", "wk"):
                    pass
                p.wait("vector", p.ST, p.ST.v)
                def scw(v, sl=sl):
                    last = None
                    for it in range(NI):
                        last = v.tensor_scalar_mul(sl[:, it * D:(it + 1) * D],
                                                   sl[:, it * D:(it + 1) * D], L1W[:, it:it + 1])
                    return last
                p.ev("vector", scw)
                bb.rdy = p.sig("vector")
                B["stage"].cons.append(bb.rdy)

    def load_w_ffn(l):
        p.take("gpsimd", B["W1S"])
        d1 = p.dma("gpsimd", p.SW, sb3(W1S, DI), AP(ins[f"w1{l}"], 0, [[DI, 128], [128 * DI, NI], [1, DI]]))
        B["W1S"].rdy = d1
        p.take("gpsimd", B["W2S"])
        d2 = p.dma("gpsimd", p.SW, sb3(W2S, D, nt=NIH), AP(ins[f"w2{l}"], 0, [[D, 128], [128 * D, NIH], [1, D]]))
        B["W2S"].rdy = d2
        cast_lnb(L2B)
        p.use("tensor", B["W1S"], B["LBB"])
        p.take("tensor", B["psB2"])
        def bias_mm(t):
            last = None
            for ot in range(NIH):
                for it in range(NI):
                    last = t.matmul(psB2[:, ot:ot + 1],
                                    W1S[:, it * DI + ot * 128:it * DI + (ot + 1) * 128],
                                    LBB[:, it:it + 1], start=(it == 0), stop=(it == NI - 1))
            last.then_inc(p.ST.h, 1)
        p.ST.reserve(1)
        p.ev("tensor", bias_mm)
        stv = (p.ST, p.ST.v)
        p.wait("vector", *stv)
        p.use("vector", B["stage"])
        p.ev("vector", lambda v: v.tensor_add(B1E[:, :], psB2[:, 0:NIH], B1R[:, :]))
        def scw(v):
            last = None
            for it in range(NI):
                last = v.tensor_scalar_mul(W1S[:, it * DI:(it + 1) * DI],
                                           W1S[:, it * DI:(it + 1) * DI], L2W[:, it:it + 1])
            return last
        p.ev("vector", scw)
        sv = p.sig("vector")
        B["psB2"].cons.append(sv)
        B["W1S"].rdy = sv

    def kv_pass(l, b):
        src = xT if l == 0 else hT
        p.take("vector", B["VA"])
        p.ev("vector", lambda v: v.memset(AP(VA, 64, [[16 * 1040, 128], [1040, 16], [65, 16]]), 1.0))
        B["VA"].rdy = p.sig("vector")
        if debug_taps and l == DBG_L and b == DBG_B:
            p.use("gpsimd", B["WS2"])
            dbg_dump(dbgWS2, buf=B["WS2"], sb_ap=AP(WS2, 0, [[NI * D, 128], [1, NI * D]]), cols=NI * D)
        for c in range(NG):
            col0 = b * S + c * G
            if l == 0:
                p.take("sync", B["HG"])
                for tt in range(4):
                    for half in range(2):
                        p.take("sync", B["XSC"])
                        dv = p.dma("sync", p.SH, XSC[:, :],
                                   AP(xin, (b * S + c * G + tt * 128) * D + half * G,
                                      [[D, 128], [1, G]]))
                        B["XSC"].rdy = dv
                        p.use("tensor", B["XSC"])
                        for sub in range(4):
                            it = half * 4 + sub
                            ps, pb = (psP, B["psP"]) if (it % 2 == 0) else (psB1, B["psB1"])
                            p.take("tensor", pb)
                            p.ST.reserve(1)
                            p.ev("tensor", lambda t, sub=sub, ps=ps: t.transpose(
                                ps[:, 0:128], XSC[:, sub * 128:(sub + 1) * 128],
                                C_EYE[:, :]).then_inc(p.ST.h, 1))
                            stv = (p.ST, p.ST.v)
                            p.wait("scalar", *stv)
                            p.ev("scalar", lambda sc, it=it, tt=tt, ps=ps: sc.copy(
                                HG[:, it * G + tt * 128:it * G + (tt + 1) * 128], ps[:, 0:128]))
                            ssv = p.sig("scalar")
                            pb.cons.append(ssv)
                            B["XSC"].cons.append(ssv)
                B["HG"].rdy = (p.SS, p.SS.v)
                p.use("sync", B["HG"])
                sv = p.dma("sync", p.SO, str_in(xT, col0, G), sb3(HG, G))
                B["HG"].cons.append(sv)
            else:
                load_hg(src, col0)
            ln_stats(HG, QP, B["QP"], C_EPS6, (ROWB, 0), (ROWB, 32))
            ln_normalize(HG, XN, False, (ROWB, 0), (ROWB, 32), dst_buf=B["XN"], src_buf=B["HG"])
            p.use("tensor", B["XN"], B["WS1"])
            rot = 0
            for ot in range(NI):
                pa, pb = ROT5[rot % 5]
                rot += 1
                p.take("tensor", pb)
                def kmm(t, ot=ot, pa=pa):
                    last = None
                    for it in range(NI):
                        last = t.matmul(pa, WS1[:, it * D + ot * 128:it * D + (ot + 1) * 128],
                                        XN[:, it * G:(it + 1) * G], start=(it == 0), stop=(it == NI - 1))
                    last.then_inc(p.ST.h, 1)
                p.ST.reserve(1)
                p.ev("tensor", kmm)
                stv = (p.ST, p.ST.v)
                p.wait("scalar", *stv)
                p.take("scalar", B["KP"]) if (c == 0 and ot == 0) else None
                p.ev("scalar", lambda sc, ot=ot, pa=pa, c=c: sc.add(
                    KP[:, ot * S + c * G:ot * S + (c + 1) * G], pa, BK[:, ot:ot + 1]))
                ssv = p.sig("scalar")
                pb.cons.append(ssv)
            B["KP"].rdy = (p.SS, p.SS.v)
            B["XN"].cons.append((p.ST, p.ST.v))
            p.use("tensor", B["XN"], B["WS2"])
            for tt in range(4):
                for half in range(2):
                    pa, pb = ROT5[rot % 5]
                    rot += 1
                    p.take("tensor", pb)
                    def vmm(t, tt=tt, half=half, pa=pa):
                        last = None
                        for it in range(NI):
                            last = t.matmul(pa, XN[:, it * G + tt * 128:it * G + (tt + 1) * 128],
                                     WS2[:, it * D + half * G:it * D + half * G + G],
                                     start=(it == 0), stop=(it == NI - 1))
                        last.then_inc(p.ST.h, 1)
                    p.ST.reserve(1)
                    p.ev("tensor", vmm)
                    stv = (p.ST, p.ST.v)
                    p.wait("vector", *stv)
                    p.use("vector", B["BVT"])
                    ttk = c * 4 + tt
                    p.ev("vector", lambda v, ttk=ttk, half=half, pa=pa: v.tensor_add(
                        AP(VA, ttk * 1040 + half * 8 * 65, [[16 * 1040, 128], [65, 8], [1, 64]]),
                        pa, BVT[:, half * G:(half + 1) * G]))
                    ssv = p.sig("vector")
                    pb.cons.append(ssv)
            B["VA"].rdy = (p.SV, p.SV.v)
            B["XN"].cons.append((p.ST, p.ST.v))
        B["WS1"].cons.append((p.ST, p.ST.v))
        B["WS2"].cons.append((p.ST, p.ST.v))
        B["BVT"].cons.append((p.SV, p.SV.v))
        if debug_taps and l == DBG_L and b == DBG_B:
            dbg_dump(dbgKP, buf=B["KP"], sb_ap=AP(KP, 0, [[NI * S, 128], [1, NI * S]]), cols=NI * S)
            dbg_dump(dbgVA, buf=B["VA"], sb_ap=AP(VA, 0, [[16 * H * 65, 128], [1, 16 * H * 65]]), cols=16 * H * 65)

    def attn_pass(l, b):
        src = xT if l == 0 else hT
        p.take("vector", B["PD"])
        def zpd2(v):
            last = None
            for j in range(1, 4):
                last = v.memset(AP(PD, (j - 1) * G, [[3 * G, 128], [1, 128 * j]]), 0.0)
            return last
        p.ev("vector", zpd2)
        B["PD"].rdy = p.sig("vector")
        if debug_taps and l == DBG_L and b == DBG_B:
            p.wait("scalar", p.SV, p.SV.v)
            p.take("scalar", B["AS"])
            p.ev("scalar", lambda sc: sc.copy(AS[:, 0:NI], psB2[:, 0:NI]))
            sps = p.sig("scalar")
            B["psB2"].cons.append(sps)
            B["AS"].rdy = sps
            dbg_dump(dbgBQ2, sb_ap=AP(BQ, 0, [[NI, 128], [1, NI]]), cols=NI)
            dbg_dump(dbgBQU2, sb_ap=AP(BQU, 0, [[NI, 128], [1, NI]]), cols=NI)
            dbg_dump(dbgPSB, buf=B["AS"], sb_ap=AP(AS, 0, [[G, 128], [1, NI]]), cols=NI)
        for g in range(NG):
            col0 = b * S + g * G
            load_hg(src, col0)
            ln_stats(HG, QP, B["QP"], C_EPS6, (ROWB, 0), (ROWB, 32))
            ln_normalize(HG, XN, False, (ROWB, 0), (ROWB, 32), dst_buf=B["XN"], src_buf=B["HG"])
            p.use("tensor", B["XN"], B["WS1"])
            rot = 0
            for ot in range(NI):
                pa, pb = ROT5[rot % 5]
                rot += 1
                p.take("tensor", pb)
                def qmm(t, ot=ot, pa=pa):
                    last = None
                    for it in range(NI):
                        last = t.matmul(pa, WS1[:, it * D + ot * 128:it * D + (ot + 1) * 128],
                                        XN[:, it * G:(it + 1) * G], start=(it == 0), stop=(it == NI - 1))
                    last.then_inc(p.ST.h, 1)
                p.ST.reserve(1)
                p.ev("tensor", qmm)
                stv = (p.ST, p.ST.v)
                p.wait("scalar", *stv)
                p.take("scalar", B["QS1"])
                p.ev("scalar", lambda sc, ot=ot, pa=pa: sc.add(
                    QP[:, ot * G:(ot + 1) * G], pa, BQ[:, ot:ot + 1]))
                p.ev("scalar", lambda sc, ot=ot, pa=pa: sc.add(
                    QS1[:, :], pa, BQU[:, ot:ot + 1]))
                ssv = p.sig("scalar")
                pb.cons.append(ssv)
                p.wait("vector", *ssv)
                def qtl(v, ot=ot, g=g):
                    v.tensor_copy(QS2[0:32, :], QS1[32:64, :])
                    v.tensor_copy(QS2[32:64, :], QS1[0:32, :])
                    v.tensor_copy(QS2[64:96, :], QS1[96:128, :])
                    v.tensor_copy(QS2[96:128, :], QS1[64:96, :])
                    v.drain()
                    v.tensor_mul(AS[:, :], QS1[:, :], C_SIN[:, g * G:(g + 1) * G])
                    v.tensor_mul(XSC[:, :], QS2[:, :], C_CM[:, g * G:(g + 1) * G])
                    v.drain()
                    return v.tensor_add(QT[:, ot * G:(ot + 1) * G], AS[:, :], XSC[:, :])
                p.take("vector", B["AS"], B["XSC"])
                if ot == 0:
                    p.take("vector", B["GT"])
                p.ev("vector", qtl)
                svv = p.sig("vector")
                B["QS1"].cons.append(svv)
                B["AS"].rdy = svv
                B["AS"].cons.append(svv)
                B["XSC"].rdy = svv
                B["XSC"].cons.append(svv)
                B["GT"].cons.append(svv)
            svq = (p.SV, p.SV.v)
            B["QP"].rdy = (p.SS, p.SS.v)
            B["QT"].rdy = svq
            B["XN"].cons.append((p.ST, p.ST.v))
            if debug_taps and l == DBG_L and b == DBG_B and g == DBG_G:
                dbg_dump(dbgQP, buf=B["QP"], sb_ap=AP(QP, 0, [[NI * G, 128], [1, NI * G]]), cols=NI * G)
                dbg_dump(dbgQT, buf=B["QT"], sb_ap=AP(QT, 0, [[NI * G, 128], [1, NI * G]]), cols=NI * G)
                dbg_dump(dbgXN, buf=B["XN"], sb_ap=AP(XN, 0, [[NI * G, 128], [1, NI * G]]), cols=NI * G)
                dbg_dump(dbgQS1, buf=B["QS1"], sb_ap=AP(QS1, 0, [[G, 128], [1, G]]), cols=G)
                dbg_dump(dbgQS2, buf=B["QS2"], sb_ap=AP(QS2, 0, [[G, 128], [1, G]]), cols=G)
                dbg_dump(dbgSIN, AP(C_SIN, 0, [[S, 128], [1, G]]), G)
                dbg_dump(dbgCM, AP(C_CM, 0, [[S, 128], [1, G]]), G)
                dbg_dump(dbgBQ, AP(BQ, 0, [[NI, 128], [1, NI]]), NI)
                dbg_dump(dbgBQU, AP(BQU, 0, [[NI, 128], [1, NI]]), NI)
                dbg_dump(dbgUC, AP(UC, 0, [[NI, 128], [1, NI]]), NI)
            nkt = 4 * g + 4
            pend_chain = None
            for h in range(H):
                pt, hb = h // 2, (h % 2) * 64
                par = h % 2
                cbt = psC if par == 0 else psP
                cbuf = B["psC"] if par == 0 else B["psP"]
                bc, bcbuf = (psB2, B["psB2"]) if par == 0 else (psB1, B["psB1"])
                bcs, bcsbuf = (BCS, B["BCS"]) if par == 0 else (BCS2, B["BCS2"])
                rrow = RCB[32 * par:32 * par + 1, :]
                rbuf = B["rc0"] if par == 0 else B["rc1"]
                pairs = [(ktg, [j0, j0 + 1]) for ktg in range(nkt // 4) for j0 in (0, 2)]
                pend_cmm = None
                for pi, (ktg, js) in enumerate(pairs):
                    diag = (ktg == g)
                    p.use("tensor", B["QP"], B["QT"], B["KP"])
                    for j in js:
                        p.take("tensor", B[f"psW{j}"])
                    def smm(t, js=js, ktg=ktg, pt=pt, hb=hb, g=g):
                        last = None
                        for j in js:
                            kt = ktg * 4 + j
                            last = t.matmul(psW[:, j * G:(j + 1) * G],
                                            KP[hb:hb + 64, pt * S + kt * 128:pt * S + (kt + 1) * 128],
                                            QP[hb:hb + 64, pt * G:(pt + 1) * G],
                                            start=True, stop=False)
                            last = t.matmul(psW[:, j * G:(j + 1) * G],
                                            C_KT[hb:hb + 64, kt * 128:(kt + 1) * 128],
                                            QT[hb:hb + 64, pt * G:(pt + 1) * G],
                                            start=False, stop=True)
                        last.then_inc(p.ST.h, 1)
                    p.ST.reserve(1)
                    p.ev("tensor", smm)
                    stv = (p.ST, p.ST.v)
                    p.wait("scalar", *stv)
                    if diag:
                        p.take("scalar", B["PD"], B["PM"])
                        def expd(sc, js=js):
                            last = None
                            for j in js:
                                if j == 0:
                                    last = sc.activation(PM[:, 0:G], psW[:, 0:G], AF.Exp, bias=0.0, scale=0.125)
                                else:
                                    last = sc.activation(PD[:, (j - 1) * G + 128 * j:j * G],
                                                         psW[:, j * G + 128 * j:(j + 1) * G],
                                                         AF.Exp, bias=0.0, scale=0.125)
                            return last
                        p.ev("scalar", expd)
                        ssv = p.sig("scalar")
                        for j in js:
                            B[f"psW{j}"].cons.append(ssv)
                        p.wait("vector", *ssv)
                        def trim(v, js=js):
                            last = None
                            for j in js:
                                if j == 0:
                                    last = v.tensor_mul(PM[:, 0:128], PM[:, 0:128], C_TRI[:, :])
                                else:
                                    last = v.tensor_mul(PD[:, (j - 1) * G + 128 * j:(j - 1) * G + 128 * (j + 1)],
                                                        PD[:, (j - 1) * G + 128 * j:(j - 1) * G + 128 * (j + 1)],
                                                        C_TRI[:, :])
                            return last
                        p.ev("vector", trim)
                        pready = p.sig("vector")
                        B["PD"].rdy = pready
                        B["PM"].rdy = pready
                    else:
                        p.take("scalar", B["PM"])
                        def expm(sc, js=js):
                            n = len(js)
                            return sc.activation(PM[:, js[0] * G:(js[0] + n) * G],
                                                 psW[:, js[0] * G:(js[0] + n) * G],
                                                 AF.Exp, bias=0.0, scale=0.125)
                        p.ev("scalar", expm)
                        pready = p.sig("scalar")
                        for j in js:
                            B[f"psW{j}"].cons.append(pready)
                        B["PM"].rdy = pready
                    def make_cmm(js=js, ktg=ktg, h=h, diag=diag, first=(pi == 0),
                                 pready=pready, cbt=cbt, cbuf=cbuf):
                        def emit():
                            p.wait("tensor", *pready)
                            p.use("tensor", B["VA"])
                            if first:
                                p.take("tensor", cbuf)
                            def cmm(t):
                                lastmm = None
                                for j in js:
                                    kt = ktg * 4 + j
                                    if diag and j > 0:
                                        Pap = PD[:, (j - 1) * G:j * G]
                                    else:
                                        Pap = PM[:, j * G:(j + 1) * G]
                                    lastmm = t.matmul(cbt[0:65, :],
                                                      AP(VA, kt * 1040 + h * 65, [[16 * 1040, 128], [1, 65]]),
                                                      Pap,
                                                      start=(first and j == js[0]),
                                                      stop=(kt == nkt - 1))
                                lastmm.then_inc(p.ST.h, 1)
                            p.ST.reserve(1)
                            p.ev("tensor", cmm)
                            ctv = (p.ST, p.ST.v)
                            for j in js:
                                (B["PD"] if diag else B["PM"]).cons.append(ctv)
                            return ctv
                        return emit
                    if pend_cmm is not None:
                        pend_cmm()
                    pend_cmm = make_cmm()
                ctv = pend_cmm()
                p.wait("vector", *ctv)
                p.take("vector", rbuf)
                p.ev("vector", lambda v, cbt=cbt, rrow=rrow: v.reciprocal(
                    rrow, cbt[64:65, :]))
                ssr = p.sig("vector")
                rbuf.rdy = ssr

                def make_chain(h=h, par=par, cbt=cbt, cbuf=cbuf, bc=bc, bcbuf=bcbuf,
                               bcs=bcs, bcsbuf=bcsbuf, rrow=rrow, rbuf=rbuf, ssr=ssr,
                               pt=pt, hb=hb):
                    def emit():
                        p.wait("tensor", *ssr)
                        p.take("tensor", bcbuf)
                        p.ST.reserve(1)
                        p.ev("tensor", lambda t: t.matmul(
                            bc[0:64, :], C_ONEF[32 * par:32 * par + 1, 0:64], rrow,
                            start=True, stop=True).then_inc(p.ST.h, 1))
                        stv = (p.ST, p.ST.v)
                        rbuf.cons.append(stv)
                        p.wait("scalar", *stv)
                        p.take("scalar", bcsbuf)
                        p.ev("scalar", lambda sc: sc.copy(bcs[:, :], bc[0:64, :]))
                        ssv = p.sig("scalar")
                        bcbuf.cons.append(ssv)
                        bcsbuf.rdy = ssv
                        p.wait("vector", *ssv)
                        p.take("vector", B["CTXN"]) if h == 0 else None
                        p.ev("vector", lambda v: v.tensor_mul(
                            CTXN[hb:hb + 64, pt * G:(pt + 1) * G], cbt[0:64, :], bcs[:, :]))
                        sv = p.sig("vector")
                        cbuf.cons.append(sv)
                        bcsbuf.cons.append(sv)
                    return emit
                if pend_chain is not None:
                    pend_chain()
                pend_chain = make_chain()
            pend_chain()
            B["CTXN"].rdy = (p.SV, p.SV.v)
            B["QP"].cons.append((p.ST, p.ST.v))
            B["QT"].cons.append((p.ST, p.ST.v))
            B["KP"].cons.append((p.ST, p.ST.v))
            B["VA"].cons.append((p.ST, p.ST.v))
            B["RCB"].rdy = (p.SV, p.SV.v)
            B["RCB"].cons.append((p.ST, p.ST.v))
            if debug_taps and l == DBG_L and b == DBG_B and g == DBG_G:
                dbg_dump(dbgCT, buf=B["CTXN"], sb_ap=AP(CTXN, 0, [[NI * G, 128], [1, NI * G]]), cols=NI * G)
            p.use("tensor", B["CTXN"], B["WS2"])
            rot = 0
            for ot in range(NI):
                pa, pb = ROT5[rot % 5]
                rot += 1
                p.take("tensor", pb)
                def omm(t, ot=ot, pa=pa):
                    last = None
                    for ptt in range(NI):
                        last = t.matmul(pa, WS2[:, ptt * D + ot * 128:ptt * D + (ot + 1) * 128],
                                        CTXN[:, ptt * G:(ptt + 1) * G],
                                        start=(ptt == 0), stop=(ptt == NI - 1))
                    last.then_inc(p.ST.h, 1)
                p.ST.reserve(1)
                p.ev("tensor", omm)
                stv = (p.ST, p.ST.v)
                p.wait("scalar", *stv)
                p.take("scalar", B["AS"])
                p.ev("scalar", lambda sc, ot=ot, pa=pa: sc.add(AS[:, :], pa, BOC[:, ot:ot + 1]))
                ssv = p.sig("scalar")
                pb.cons.append(ssv)
                B["AS"].rdy = ssv
                p.wait("vector", *ssv)
                p.ev("vector", lambda v, ot=ot: v.tensor_add(
                    HG[:, ot * G:(ot + 1) * G], HG[:, ot * G:(ot + 1) * G], AS[:, :]))
                svv = p.sig("vector")
                B["AS"].cons.append(svv)
            svv = (p.SV, p.SV.v)
            B["HG"].rdy = svv
            p.use("sync", B["HG"])
            so = p.dma("sync", p.SO, str_in(hT, col0, G), sb3(HG, G))
            if debug_taps and l == DBG_L:
                so = p.dma("sync", p.SO, str_in(dbgA, col0, G), sb3(HG, G))
            B["HG"].cons.append(so)
            B["CTXN"].cons.append((p.ST, p.ST.v))
            B["WS1"].cons.append((p.ST, p.ST.v))
            B["WS2"].cons.append((p.ST, p.ST.v))

    def ffn_pass(l):
        load_w_ffn(l)
        for gg in range(NB * NG):
            b, c = gg // NG, gg % NG
            col0 = b * S + c * G
            load_hg(hT, col0)
            ln_stats(HG, GT, B["GT"], C_EPS6, (ROWB, 0), (ROWB, 32))
            ln_normalize(HG, XN, False, (ROWB, 0), (ROWB, 32), dst_buf=B["XN"], src_buf=B["HG"])
            p.use("tensor", B["XN"], B["W1S"])
            p.take("scalar", B["GT"])
            for ot in range(NIH):
                j = ot % 4
                p.take("tensor", B[f"psW{j}"])
                def hmm(t, ot=ot, j=j):
                    last = None
                    for it in range(NI):
                        last = t.matmul(psW[:, j * G:(j + 1) * G],
                                        W1S[:, it * DI + ot * 128:it * DI + (ot + 1) * 128],
                                        XN[:, it * G:(it + 1) * G], start=(it == 0), stop=(it == NI - 1))
                    last.then_inc(p.ST.h, 1)
                p.ST.reserve(1)
                p.ev("tensor", hmm)
                stv = (p.ST, p.ST.v)
                p.wait("scalar", *stv)
                p.ev("scalar", lambda sc, ot=ot, j=j: sc.activation(
                    GT[:, ot * G:(ot + 1) * G], psW[:, j * G:(j + 1) * G], AF.Gelu,
                    bias=B1E[:, ot:ot + 1], scale=1.0))
                ssv = p.sig("scalar")
                B[f"psW{j}"].cons.append(ssv)
            B["GT"].rdy = (p.SS, p.SS.v)
            B["XN"].cons.append((p.ST, p.ST.v))
            B["W1S"].cons.append((p.ST, p.ST.v))
            p.use("tensor", B["GT"], B["W2S"])
            rot = 0
            for ot in range(NI):
                pa, pb = ROT5[rot % 5]
                rot += 1
                p.take("tensor", pb)
                def wmm(t, ot=ot, pa=pa):
                    last = None
                    for it in range(NIH):
                        last = t.matmul(pa, W2S[:, it * D + ot * 128:it * D + (ot + 1) * 128],
                                        GT[:, it * G:(it + 1) * G], start=(it == 0), stop=(it == NIH - 1))
                    last.then_inc(p.ST.h, 1)
                p.ST.reserve(1)
                p.ev("tensor", wmm)
                stv = (p.ST, p.ST.v)
                p.wait("scalar", *stv)
                p.take("scalar", B["AS"])
                p.ev("scalar", lambda sc, ot=ot, pa=pa: sc.add(AS[:, :], pa, B2C[:, ot:ot + 1]))
                ssv = p.sig("scalar")
                pb.cons.append(ssv)
                B["AS"].rdy = ssv
                p.wait("vector", *ssv)
                p.ev("vector", lambda v, ot=ot: v.tensor_add(
                    HG[:, ot * G:(ot + 1) * G], HG[:, ot * G:(ot + 1) * G], AS[:, :]))
                B["AS"].cons.append(p.sig("vector"))
            B["GT"].cons.append((p.ST, p.ST.v))
            B["W2S"].cons.append((p.ST, p.ST.v))
            if l == nl - 1 and do_slot:
                for it in range(NI):
                    p.take("sync", B["XSC"])
                    if it == 0:
                        p.take("sync", B["GT"])
                    dv = p.dma("sync", p.SH, XSC[:, :],
                               AP(xT, it * 128 * T + col0, [[T, 128], [1, G]]))
                    B["XSC"].rdy = dv
                    p.wait("vector", *dv)
                    p.wait("vector", p.SV, p.SV.v)
                    p.ev("vector", lambda v, it=it: v.tensor_add(
                        HG[:, it * G:(it + 1) * G], HG[:, it * G:(it + 1) * G], XSC[:, :]))
                    svz = p.sig("vector")
                    B["XSC"].cons.append(svz)
                    B["GT"].cons.append(svz)
                B["HG"].rdy = (p.SV, p.SV.v)
                ln_stats(HG, GT, B["GT"], C_EPS5, (ROWB, 0), (ROWB, 32))
                ln_normalize(HG, HG, True, (ROWB, 0), (ROWB, 32), src_buf=B["HG"])
                B["HG"].rdy = (p.SV, p.SV.v)
                p.use("sync", B["HG"])
                so = p.dma("sync", p.SO, str_in(x1T, col0, G), sb3(HG, G))
                B["HG"].cons.append(so)
                B["x1T"].rdy = so
            else:
                B["HG"].rdy = (p.SV, p.SV.v)
                p.use("sync", B["HG"])
                so = p.dma("sync", p.SO, str_in(hT, col0, G), sb3(HG, G))
                B["HG"].cons.append(so)

    def slot_f0():
        slot_stage_load()
        p.take("gpsimd", B["XFA"])
        if B["x1T"].rdy is not None:
            p.wait("gpsimd", *B["x1T"].rdy)
        for qq in range(4):
            dv = p.dma("gpsimd", p.SW, AP(XFA, qq * 1024, [[NI * T, 128], [T, NI], [1, 1024]]),
                       AP(x1T, qq * 1024, [[T, 128], [128 * T, NI], [1, 1024]]))
        B["XFA"].rdy = dv
        for ot in range(NIH):
            sa, ba = (STRA, B["STRA"]) if ot % 2 == 0 else (STRB, B["STRB"])
            p.take("gpsimd", ba)
            dv1 = p.dma("gpsimd", p.SW, AP(sa, 0, [[NIH * 128, 128], [128, NI], [1, 128]]),
                        AP(ins["w1s"], ot * 128, [[DI, 128], [128 * DI, NI], [1, 128]]))
            dv2 = p.dma("gpsimd", p.SW, AP(sa, NI * 128, [[NIH * 128, 128], [128, NI], [1, 128]]),
                        AP(ins["w2s"], ot * 128, [[DI, 128], [128 * DI, NI], [1, 128]]))
            ba.rdy = dv2
            for gg in range(NB * NG):
                base = (gg % 2) * 2
                p.use("tensor", ba, B["XFA"], B["stage"])
                p.take("tensor", B[f"psW{base}"], B[f"psW{base + 1}"])
                def f0mm(t, sa=sa, gg=gg, base=base):
                    last = None
                    for w in range(2):
                        for it in range(NI):
                            last = t.matmul(psW[:, (base + w) * G:(base + w + 1) * G],
                                            sa[:, (w * NI + it) * 128:(w * NI + it + 1) * 128],
                                            XFA[:, it * T + gg * G:it * T + (gg + 1) * G],
                                            start=(it == 0), stop=(it == NI - 1))
                    last.then_inc(p.ST.h, 1)
                p.ST.reserve(1)
                p.ev("tensor", f0mm)
                stv = (p.ST, p.ST.v)
                p.wait("scalar", *stv)
                p.take("scalar", B["EVW1"], B["EVW2"])
                p.ev("scalar", lambda sc, ot=ot, base=base: sc.activation(
                    EVW1[:, 0:G], psW[:, base * G:(base + 1) * G], AF.Silu, bias=0.0, scale=SC1[:, ot:ot + 1]))
                p.ev("scalar", lambda sc, ot=ot, base=base: sc.mul(
                    EVW2[:, 0:G], psW[:, (base + 1) * G:(base + 2) * G], SC2[:, ot:ot + 1]))
                ssv = p.sig("scalar")
                B[f"psW{base}"].cons.append(ssv)
                B[f"psW{base + 1}"].cons.append(ssv)
                p.wait("vector", *ssv)
                p.take("vector", B["EVW1"])
                p.ev("vector", lambda v: v.tensor_mul(EVW1[:, G:2 * G], EVW1[:, 0:G], EVW2[:, 0:G]))
                sv = p.sig("vector")
                B["EVW1"].rdy = sv
                B["EVW2"].cons.append(sv)
                p.wait("sync", *sv)
                so = p.dma("sync", p.SO, AP(g1d, ot * 128 * T + gg * G, [[T, 128], [1, G]]),
                           EVW1[:, G:2 * G])
                B["EVW1"].cons.append(so)
                ba.cons.append((p.ST, p.ST.v))
                B["XFA"].cons.append((p.ST, p.ST.v))

    def slot_f12(wsrc, src_d, dst_d, sc_col):
        for pan in range(2):
            p.take("sync", B["GPAN"])
            for qq in range(4):
                dv = p.dma("sync", p.SH,
                           AP(GPAN, qq * 8 * 2048, [[NIH * 2048, 128], [2048, 8], [1, 2048]]),
                           AP(src_d, qq * 8 * 128 * T + pan * 2048, [[T, 128], [128 * T, 8], [1, 2048]]))
            B["GPAN"].rdy = dv
            for ot in range(NIH):
                sa, ba = (STRA, B["STRA"]) if ot % 2 == 0 else (STRB, B["STRB"])
                p.take("gpsimd", ba)
                for qq in range(4):
                    dv = p.dma("gpsimd", p.SW,
                               AP(sa, qq * 8 * 128, [[NIH * 128, 128], [128, 8], [1, 128]]),
                               AP(wsrc, qq * 8 * 128 * DI + ot * 128,
                                  [[DI, 128], [128 * DI, 8], [1, 128]]))
                ba.rdy = dv
                p.use("tensor", ba, B["GPAN"])
                for j in range(4):
                    p.take("tensor", B[f"psW{j}"])
                def fmm(t, sa=sa):
                    last = None
                    for it in range(NIH):
                        for j in range(4):
                            last = t.matmul(psW[:, j * G:(j + 1) * G],
                                            sa[:, it * 128:(it + 1) * 128],
                                            GPAN[:, it * 2048 + j * G:it * 2048 + (j + 1) * G],
                                            start=(it == 0), stop=(it == NIH - 1))
                    last.then_inc(p.ST.h, 1)
                p.ST.reserve(1)
                p.ev("tensor", fmm)
                stv = (p.ST, p.ST.v)
                ev, eb = (EVW1, B["EVW1"]) if ot % 2 == 0 else (EVW2, B["EVW2"])
                p.wait("scalar", *stv)
                p.take("scalar", eb)
                p.ev("scalar", lambda sc, ev=ev, ot=ot: sc.activation(
                    ev[:, :], psW[:, :], AF.Gelu, bias=0.0, scale=sc_col[:, ot:ot + 1]))
                ssv = p.sig("scalar")
                for j in range(4):
                    B[f"psW{j}"].cons.append(ssv)
                eb.rdy = ssv
                p.wait("sync", *ssv)
                so = p.dma("sync", p.SO, AP(dst_d, ot * 128 * T + pan * 2048, [[T, 128], [1, 2048]]),
                           ev[:, :])
                eb.cons.append(so)
                ba.cons.append((p.ST, p.ST.v))
                B["GPAN"].cons.append((p.ST, p.ST.v))

    def slot_f3():
        p.take("gpsimd", B["W3S"])
        for qq in range(4):
            dv = p.dma("gpsimd", p.SW, AP(W3S, qq * 8 * D, [[NIH * D, 128], [D, 8], [1, D]]),
                       AP(ins["w3s"], qq * 8 * 128 * D, [[D, 128], [128 * D, 8], [1, D]]))
        B["W3S"].rdy = dv
        for gg in range(NB * NG):
            b, c = gg // NG, gg % NG
            col0 = b * S + c * G
            p.take("sync", B["GT"])
            for qq in range(4):
                dv = p.dma("sync", p.SH,
                           AP(GT, qq * 8 * G, [[NIH * G, 128], [G, 8], [1, G]]),
                           AP(g3d, qq * 8 * 128 * T + col0, [[T, 128], [128 * T, 8], [1, G]]))
            B["GT"].rdy = dv
            for ot in range(NI):
                ps, pb = (psP, B["psP"]) if ot % 2 == 0 else (psB1, B["psB1"])
                p.use("tensor", B["GT"], B["W3S"])
                p.take("tensor", pb)
                def f3mm(t, ot=ot, ps=ps):
                    last = None
                    for it in range(NIH):
                        last = t.matmul(ps[:, :], W3S[:, it * D + ot * 128:it * D + (ot + 1) * 128],
                                        GT[:, it * G:(it + 1) * G], start=(it == 0), stop=(it == NIH - 1))
                    last.then_inc(p.ST.h, 1)
                p.ST.reserve(1)
                p.ev("tensor", f3mm)
                stv = (p.ST, p.ST.v)
                p.wait("scalar", *stv)
                p.take("scalar", B["AS"])
                p.ev("scalar", lambda sc, ot=ot, ps=ps: sc.mul(AS[:, :], ps[:, :], SC3[:, ot:ot + 1]))
                ssv = p.sig("scalar")
                pb.cons.append(ssv)
                B["AS"].rdy = ssv
                p.take("sync", B["XF3"])
                dv = p.dma("sync", p.SH, XF3[:, :], AP(x1T, ot * 128 * T + col0, [[T, 128], [1, G]]))
                B["XF3"].rdy = dv
                p.wait("vector", *ssv)
                p.wait("vector", *dv)
                p.take("vector", B["HG"]) if ot == 0 else None
                p.ev("vector", lambda v, ot=ot: v.tensor_add(
                    HG[:, ot * G:(ot + 1) * G], AS[:, :], XF3[:, :]))
                sv = p.sig("vector")
                B["AS"].cons.append(sv)
                B["XF3"].cons.append(sv)
            B["HG"].rdy = (p.SV, p.SV.v)
            B["GT"].cons.append((p.ST, p.ST.v))
            B["W3S"].cons.append((p.ST, p.ST.v))
            if debug_taps and gg == 0:
                dbg_dump(dbgZ2, buf=B["HG"], sb_ap=AP(HG, 0, [[NI * G, 128], [1, NI * G]]), cols=NI * G)
            ln_stats(HG, GT, B["GT"], C_EPS5, (ROWB, 0), (ROWB, 32))
            ln_normalize(HG, HG, True, (ROWB, 0), (ROWB, 32), src_buf=B["HG"])
            B["HG"].rdy = (p.SV, p.SV.v)
            if debug_taps and gg == 0:
                dbg_dump(dbgZ3, buf=B["HG"], sb_ap=AP(HG, 0, [[NI * G, 128], [1, NI * G]]), cols=NI * G)
            for tt in range(4):
                p.use("tensor", B["HG"])
                p.take("tensor", B["OSTG"], B["GT"])
                for it in range(NI):
                    ps, pb = (psB2, B["psB2"]) if it % 2 == 0 else (psP, B["psP"])
                    p.take("tensor", pb)
                    p.ST.reserve(1)
                    p.ev("tensor", lambda t, it=it, tt=tt, ps=ps: t.transpose(
                        ps[:, 0:128], HG[:, it * G + tt * 128:it * G + (tt + 1) * 128],
                        C_EYE[:, :]).then_inc(p.ST.h, 1))
                    stv = (p.ST, p.ST.v)
                    p.wait("scalar", *stv)
                    p.ev("scalar", lambda sc, it=it, ps=ps: sc.copy(
                        OSTG[:, it * 128:(it + 1) * 128], ps[:, 0:128]))
                    ssv = p.sig("scalar")
                    pb.cons.append(ssv)
                B["OSTG"].rdy = (p.SS, p.SS.v)
                p.use("sync", B["OSTG"])
                so = p.dma("sync", p.SO, AP(out, (b * S + c * G + tt * 128) * D, [[D, 128], [1, D]]),
                           OSTG[:, :])
                B["OSTG"].cons.append(so)
                B["HG"].cons.append((p.ST, p.ST.v))

    precast = []
    for _src, _dst in ((ins["h1s"], h1b), (ins["h2s"], h2b)):
        for _ck in range(8):
            precast.append((_dst, _src, _ck * 512))

    def maybe_precast():
        if precast:
            _dst, _src, r0 = precast.pop(0)
            p.dma("gpsimd", p.SW, AP(_dst, r0 * DI, [[DI, 512], [1, DI]]),
                  AP(_src, r0 * DI, [[DI, 512], [1, DI]]))

    startup()
    for l in range(nl):
        layer_stage_load(l)
        cast_lnb(L1B)
        for b in range(NB):
            load_w_pair(l, ("wk", "wv"), fold=True)
            maybe_precast()
            kv_pass(l, b)
            load_w_pair(l, ("wq", "wo"), fold=True)
            maybe_precast()
            attn_pass(l, b)
        ffn_pass(l)
        maybe_precast()
    while precast:
        maybe_precast()
    p.wait("gpsimd", p.SW, p.SW.v)
    if do_slot:
        slot_f0()
        slot_f12(h1b, g1d, g2d, SCH1)
        slot_f12(h2b, g2d, g3d, SCH2)
        slot_f3()
    p.wait("sync", p.SO, p.SO.v)
    p.wait("sync", p.SW, p.SW.v)

    with nc.Block() as block:
        p.flush(block)
    return nc


_cached = {}


def _get_program():
    if "nc" not in _cached:
        _cached["nc"] = build_program()
    return _cached["nc"]


def prep_common(inputs):
    def col(a, ncols):
        return np.ascontiguousarray(a.reshape(ncols, 128).T)

    common = {}
    for l in range(NL):
        common[f"wq{l}"] = np.ascontiguousarray(inputs["L0_attn_Wq"][l].T)
        common[f"wk{l}"] = np.ascontiguousarray(inputs["L0_attn_Wk"][l].T)
        common[f"wv{l}"] = np.ascontiguousarray(inputs["L0_attn_Wv"][l].T)
        common[f"wo{l}"] = np.ascontiguousarray(inputs["L0_attn_Wo"][l].T)
        common[f"w1{l}"] = np.ascontiguousarray(inputs["L0_attn_W1"][l].T)
        common[f"w2{l}"] = np.ascontiguousarray(inputs["L0_attn_W2"][l].T)
        common[f"bqc{l}"] = col(inputs["L0_attn_bq"][l], NI)
        common[f"bkc{l}"] = col(inputs["L0_attn_bk"][l], NI)
        common[f"boc{l}"] = col(inputs["L0_attn_bo"][l], NI)
        common[f"b2c{l}"] = col(inputs["L0_attn_b2"][l], NI)
        common[f"b1c{l}"] = col(inputs["L0_attn_b1"][l], NIH)
        common[f"uc{l}"] = col(inputs["L0_attn_u_bias"][l].reshape(-1), NI)
        common[f"l1wc{l}"] = col(inputs["L0_attn_ln1_w"][l], NI)
        common[f"l1bc{l}"] = col(inputs["L0_attn_ln1_b"][l], NI)
        common[f"l2wc{l}"] = col(inputs["L0_attn_ln2_w"][l], NI)
        common[f"l2bc{l}"] = col(inputs["L0_attn_ln2_b"][l], NI)
        common[f"bvr{l}"] = np.ascontiguousarray(inputs["L0_attn_bv"][l].reshape(1, D))
    common["w1s"] = np.ascontiguousarray(inputs["L0_ffn_w1_w"].T)
    common["w2s"] = np.ascontiguousarray(inputs["L0_ffn_w2_w"].T)
    common["h1s"] = np.ascontiguousarray(inputs["L0_ffn_h1_w"].T)
    common["h2s"] = np.ascontiguousarray(inputs["L0_ffn_h2_w"].T)
    common["w3s"] = np.ascontiguousarray(inputs["L0_ffn_w3_w"].T)
    common["s1c"] = col(inputs["L0_ffn_w1_s"], NIH)
    common["s2c"] = col(inputs["L0_ffn_w2_s"], NIH)
    common["sh1c"] = col(inputs["L0_ffn_h1_s"], NIH)
    common["sh2c"] = col(inputs["L0_ffn_h2_s"], NIH)
    common["s3c"] = col(inputs["L0_ffn_w3_s"], NI)
    return {k: np.ascontiguousarray(v.astype(np.float32)) for k, v in common.items()}


def kernel(**inputs):
    nc = _get_program()
    x = np.ascontiguousarray(inputs["L0_attn_x"])
    common = prep_common(inputs)
    in_maps = []
    for c in range(8):
        m = dict(common)
        m["xin"] = np.ascontiguousarray(x[2 * c:2 * c + 2]).astype(np.float32)
        in_maps.append(m)

    res = run_bass_kernel_spmd(nc, in_maps, core_ids=list(range(8)))
    return np.concatenate([r["out"] for r in res.results], axis=0)


if __name__ == "__main__":
    nc = build_program()
    print("instructions:", nc.n_instructions())
